# version 02
# baseline (speedup 1.0000x reference)
import contextlib
import numpy as np

import concourse.bass as bass
import concourse.mybir as mybir
from concourse.bass_utils import run_bass_kernel_spmd

F32 = mybir.dt.float32
BF16 = mybir.dt.bfloat16
F16 = mybir.dt.float16
ALU = mybir.AluOpType
ACT = mybir.ActivationFunctionType
AXX = mybir.AxisListType.X

D = 1024
S = 2048
NSEQ = 2
NCORE = 8
DC = D // 128
TBLK = 512
NTB = S // TBLK
NTILE = TBLK // 128
H = 16
HD = 64
DFF = 4096
EPS_LN = 1e-5
ALL_PHASES = ("P", "A", "B", "C", "D")
A_STOP = [4]
A1_NTB = [NTB]
A1_NFC = [16]
LN_PARTS = [3]
DBG_EPARK = [False]
A_SEQS = [2]
C_STOP = [99]


class Eng:
    def __init__(self, obj, sem, name):
        self.o = obj
        self.sem = sem
        self.name = name
        self.n = 0
        self._seen = {}

    def mark(self, instr, k=1):
        assert instr is not None, "bare sem_inc fires before datapath drain"
        instr.then_inc(self.sem, k)
        self.n += k
        return self.n

    def wait(self, other, val=None):
        if val is None:
            val = other.n
        if val <= 0:
            return
        key = other.name
        if self._seen.get(key, -1) >= val:
            return
        self.o.wait_ge(other.sem, val)
        self._seen[key] = val


class DmaStream:

    def __init__(self, eng, sem, name):
        self.eng = eng
        self.sem = sem
        self.name = name
        self.n = 0

    def put(self, out, in_):
        self.eng.o.dma_start(out=out, in_=in_).then_inc(self.sem, 16)
        self.n += 16
        return self.n


def wait_dma(eng, st, val=None):
    if val is None:
        val = st.n
    if val <= 0:
        return
    key = "dma_" + st.name
    if eng._seen.get(key, -1) >= val:
        return
    eng.o.wait_ge(st.sem, val)
    eng._seen[key] = val


def build_program(phases=ALL_PHASES):
    phases = set(phases)
    nc = bass.Bass("TRN2", target_bir_lowering=False,
                   detect_race_conditions=False)

    xin = nc.dram_tensor("xin", [NSEQ * S, D], F32, kind="ExternalInput").ap()
    w0 = nc.dram_tensor("w0", [3 * D, D], F32, kind="ExternalInput").ap()
    b0c = nc.dram_tensor("b0c", [128, 24], F32, kind="ExternalInput").ap()
    b0r = nc.dram_tensor("b0r", [1, 3 * D], F32, kind="ExternalInput").ap()
    fw0 = nc.dram_tensor("fw0", [D, D], F32, kind="ExternalInput").ap()
    g0c = nc.dram_tensor("g0c", [128, DC], F32, kind="ExternalInput").ap()
    be0c = nc.dram_tensor("be0c", [128, DC], F32, kind="ExternalInput").ap()
    w1 = nc.dram_tensor("w1", [3 * D, D], F32, kind="ExternalInput").ap()
    b1c = nc.dram_tensor("b1c", [128, 24], F32, kind="ExternalInput").ap()
    b1r = nc.dram_tensor("b1r", [1, 3 * D], F32, kind="ExternalInput").ap()
    pw1 = nc.dram_tensor("pw1", [D, D], F32, kind="ExternalInput").ap()
    pb1r = nc.dram_tensor("pb1r", [1, D], F32, kind="ExternalInput").ap()
    fcw1 = nc.dram_tensor("fcw1", [DFF, D], F32, kind="ExternalInput").ap()
    lnwb2 = nc.dram_tensor("lnwb2", [2, DFF], F32, kind="ExternalInput").ap()
    pw2 = nc.dram_tensor("pw2", [D, DFF], F32, kind="ExternalInput").ap()
    cident = nc.dram_tensor("cident", [128, 128], F32, kind="ExternalInput").ap()
    conerow = nc.dram_tensor("conerow", [1, 128], F32, kind="ExternalInput").ap()
    cvals = nc.dram_tensor("cvals", [128, 8], F32, kind="ExternalInput").ap()
    yout = nc.dram_tensor("y", [NSEQ * S, D], F32, kind="ExternalOutput").ap()

    xpark = nc.dram_tensor("xpark", [NSEQ * S, D], F32,
                           kind="ExternalOutput" if DBG_EPARK[0] else "Internal").ap()
    epark = nc.dram_tensor("epark", [S, S], F32,
                           kind="ExternalOutput" if DBG_EPARK[0] else "Internal").ap()
    w0T = nc.dram_tensor("w0T", [D, 3 * D], F32).ap()
    w1T = nc.dram_tensor("w1T", [D, 3 * D], F32).ap()
    wp1T = nc.dram_tensor("wp1T", [D, D], F32).ap()
    t0T = nc.dram_tensor("t0T", [D, D], F32).ap()
    tfcT = nc.dram_tensor("tfcT", [D, DFF], F32).ap()
    tp2T = nc.dram_tensor("tp2T", [DFF, D], F32).ap()
    ccin = nc.dram_tensor("ccin", [1, 128], F32)
    ccout = nc.dram_tensor("ccout", [1, 128], F32)

    es = contextlib.ExitStack()
    with es:
        semT = es.enter_context(nc.semaphore("semT"))
        semV = es.enter_context(nc.semaphore("semV"))
        semS = es.enter_context(nc.semaphore("semS"))
        semG = es.enter_context(nc.semaphore("semG"))
        semY = es.enter_context(nc.semaphore("semY"))
        semLY = es.enter_context(nc.semaphore("semLY"))
        semLW = es.enter_context(nc.semaphore("semLW"))
        semSG = es.enter_context(nc.semaphore("semSG"))
        semLG = es.enter_context(nc.semaphore("semLG"))
        ccsem = es.enter_context(nc.semaphore("ccsem"))

        es.enter_context(nc.Block())

        T = Eng(nc.tensor, semT, "T")
        V = Eng(nc.vector, semV, "V")
        SC = Eng(nc.scalar, semS, "S")
        G = Eng(nc.gpsimd, semG, "G")
        Y = Eng(nc.sync, semY, "Y")
        ALL_E = [T, V, SC, G, Y]

        LY = DmaStream(Y, semLY, "LY")
        LW = DmaStream(Y, semLW, "LW")
        SG = DmaStream(G, semSG, "SG")
        LG = DmaStream(G, semLG, "LG")

        def barrier():
            for e in ALL_E:
                for o in ALL_E:
                    if o is not e:
                        e.wait(o)
                for st_ in (LY, LW, SG, LG):
                    wait_dma(e, st_)

        ident = es.enter_context(nc.sbuf_tensor("ident", [128, 128], F32))
        onerow = es.enter_context(nc.sbuf_tensor("onerow", [1, 128], F32))
        dump = es.enter_context(nc.sbuf_tensor("dump", [128, 1024], F32))
        stats = es.enter_context(nc.sbuf_tensor("stats", [128, 64], F32))
        srow = es.enter_context(nc.sbuf_tensor("srow", [1, 64], F32))
        b0sb = es.enter_context(nc.sbuf_tensor("b0sb", [128, 24], F32))
        b1sb = es.enter_context(nc.sbuf_tensor("b1sb", [128, 24], F32))
        g0sb = es.enter_context(nc.sbuf_tensor("g0sb", [128, DC], F32))
        be0sb = es.enter_context(nc.sbuf_tensor("be0sb", [128, DC], F32))
        scv = es.enter_context(nc.sbuf_tensor("scv", [128, 8], F32))
        qscal = es.enter_context(nc.sbuf_tensor("qscal", [128, 8], F32))

        LY.put(ident[:, :], cident[:, :])
        LY.put(onerow[:1, :], conerow[:1, :])
        LY.put(b0sb[:, :], b0c[:, :])
        LY.put(b1sb[:, :], b1c[:, :])
        LY.put(g0sb[:, :], g0c[:, :])
        LY.put(be0sb[:, :], be0c[:, :])
        LY.put(scv[:, :], cvals[:, :])
        nc.const_aps.register(F32, EPS_LN, scv[:, 1:2])
        nc.const_aps.register(F32, -63.5, scv[:, 2:3])

        def col_to_scalar(col_ap, out_ap, op, ps):
            T.wait(V)
            tr = T.o.transpose(ps[:1, :128], col_ap, ident[:, :])
            T.mark(tr)
            V.wait(T)
            rd = V.o.tensor_reduce(out_ap, ps[:1, :128], axis=AXX, op=op)
            V.mark(rd)

        def bcast_scalar(src_ap, dst_ap, ps):
            w = src_ap.shape[-1]
            T.wait(V)
            mm = T.o.matmul(ps[:, :w], onerow[:1, :], src_ap,
                            start=True, stop=True)
            T.mark(mm)
            V.wait(T)
            cp = V.o.tensor_copy(dst_ap, ps[:, :w])
            V.mark(cp)

        if "P" in phases:
          with contextlib.ExitStack() as ps_:
            win = ps_.enter_context(nc.sbuf_tensor("p_win", [128, 4096], F32))
            wot = ps_.enter_context(nc.sbuf_tensor("p_wot", [128, 8192], F32))
            pst = ps_.enter_context(nc.psum_tensor("p_pst", [128, 512], F32))
            pst2 = ps_.enter_context(nc.psum_tensor("p_pst2", [128, 512], F32))

            def transpose_rows(src, src_rows, src_cols, dstT):
                nrc = src_rows // 128
                ncc = src_cols // 128
                for rc in range(nrc):
                    Y.wait(T)
                    m = LY.put(win[:, :src_cols], src[rc * 128:(rc + 1) * 128, :])
                    wait_dma(T, LY, m)
                    for cc in range(ncc):
                        pb = pst if cc % 2 == 0 else pst2
                        T.wait(V)
                        tr = T.o.transpose(pb[:, :128],
                                           win[:, cc * 128:(cc + 1) * 128],
                                           ident[:, :])
                        T.mark(tr)
                        V.wait(T)
                        wait_dma(V, SG)
                        cp = V.o.tensor_copy(wot[:, cc * 128:(cc + 1) * 128],
                                             pb[:, :128])
                        V.mark(cp)
                        G.wait(V)
                        SG.put(dstT[cc * 128:(cc + 1) * 128,
                                    rc * 128:(rc + 1) * 128],
                               wot[:, cc * 128:(cc + 1) * 128])

            transpose_rows(w0, 3 * D, D, w0T)
            transpose_rows(w1, 3 * D, D, w1T)
            transpose_rows(pw1, D, D, wp1T)

            def gsum_abs(src, rows, cols, out_col):
                nrc = rows // 128
                for rc in range(nrc):
                    Y.wait(V)
                    Y.wait(SC)
                    m = LY.put(win[:, :cols], src[rc * 128:(rc + 1) * 128, :])
                    wait_dma(V, LY, m)
                    V.o.tensor_reduce(stats[:, 62:63], win[:, :cols], axis=AXX,
                                      op=ALU.add, apply_absolute_value=True)
                    V.o.drain()
                    if rc == 0:
                        cp = V.o.tensor_copy(stats[:, out_col:out_col + 1],
                                             stats[:, 62:63])
                    else:
                        cp = V.o.tensor_tensor(stats[:, out_col:out_col + 1],
                                               stats[:, out_col:out_col + 1],
                                               stats[:, 62:63], ALU.add)
                    V.mark(cp)
                    V.o.drain()
                col_to_scalar(stats[:, out_col:out_col + 1],
                              srow[:1, out_col:out_col + 1], ALU.add, pst)

            gsum_abs(fw0, D, D, 0)
            V.o.drain()
            dl_ = V.o.tensor_scalar(srow[:1, 1:2], srow[:1, 0:1],
                                    0.7 / (2.0 ** 20), None, ALU.mult)
            V.mark(dl_)
            bcast_scalar(srow[:1, 1:2], qscal[:, 2:3], pst)
            for rc in range(D // 128):
                Y.wait(V)
                Y.wait(SC)
                m = LY.put(win[:, :D], fw0[rc * 128:(rc + 1) * 128, :])
                wait_dma(SC, LY, m)
                SC.wait(V)
                ab = SC.o.activation(wot[:, :D], win[:, :D], ACT.Abs)
                SC.mark(ab)
                wait_dma(V, LY, m)
                V.wait(SC)
                V.o.tensor_scalar(wot[:, 4096:4096 + D], wot[:, :D],
                                  qscal[:, 2:3], None, ALU.is_gt)
                V.o.drain()
                V.o.tensor_reduce(stats[:, 62:63], wot[:, 4096:4096 + D],
                                  axis=AXX, op=ALU.add)
                V.o.drain()
                if rc == 0:
                    V.o.tensor_copy(stats[:, 3:4], stats[:, 62:63])
                else:
                    V.o.tensor_tensor(stats[:, 3:4], stats[:, 3:4],
                                      stats[:, 62:63], ALU.add)
                V.o.tensor_tensor(wot[:, :D], wot[:, :D], wot[:, 4096:4096 + D],
                                  ALU.mult)
                V.o.drain()
                V.o.tensor_reduce(stats[:, 62:63], wot[:, :D], axis=AXX,
                                  op=ALU.add)
                V.o.drain()
                if rc == 0:
                    cp = V.o.tensor_copy(stats[:, 4:5], stats[:, 62:63])
                else:
                    cp = V.o.tensor_tensor(stats[:, 4:5], stats[:, 4:5],
                                           stats[:, 62:63], ALU.add)
                V.mark(cp)
                V.o.drain()
            col_to_scalar(stats[:, 3:4], srow[:1, 3:4], ALU.add, pst)
            col_to_scalar(stats[:, 4:5], srow[:1, 4:5], ALU.add, pst2)
            V.o.tensor_scalar(srow[:1, 5:6], srow[:1, 3:4], 1.0, None, ALU.max)
            V.o.drain()
            V.o.reciprocal(srow[:1, 5:6], srow[:1, 5:6])
            V.o.drain()
            al = V.o.tensor_tensor(srow[:1, 6:7], srow[:1, 4:5], srow[:1, 5:6],
                                   ALU.mult)
            V.mark(al)
            bcast_scalar(srow[:1, 6:7], qscal[:, 3:4], pst)

            def quantize_transposed(src, rows, cols, dstT, thr_col, scale_col):
                for rc in range(rows // 128):
                    Y.wait(T)
                    m = LY.put(win[:, :cols], src[rc * 128:(rc + 1) * 128, :])
                    wait_dma(T, LY, m)
                    for cc in range(cols // 128):
                        pb = pst if cc % 2 == 0 else pst2
                        T.wait(V)
                        T.wait(SC)
                        tr = T.o.transpose(pb[:, :128],
                                           win[:, cc * 128:(cc + 1) * 128],
                                           ident[:, :])
                        T.mark(tr)
                        SC.wait(T)
                        wait_dma(SC, SG)
                        sg = SC.o.activation(
                            wot[:, 4096 + cc * 128:4096 + (cc + 1) * 128],
                            pb[:, :128], ACT.Sign)
                        ab = SC.o.activation(wot[:, cc * 128:(cc + 1) * 128],
                                             pb[:, :128], ACT.Abs)
                        SC.mark(ab)
                        V.wait(SC)
                        if scale_col is not None:
                            V.o.tensor_scalar(wot[:, cc * 128:(cc + 1) * 128],
                                              wot[:, cc * 128:(cc + 1) * 128],
                                              qscal[:, thr_col:thr_col + 1],
                                              qscal[:, scale_col:scale_col + 1],
                                              ALU.is_gt, ALU.mult)
                        else:
                            V.o.tensor_scalar(wot[:, cc * 128:(cc + 1) * 128],
                                              wot[:, cc * 128:(cc + 1) * 128],
                                              qscal[:, thr_col:thr_col + 1],
                                              None, ALU.is_gt)
                        V.o.drain()
                        cp = V.o.tensor_tensor(
                            wot[:, cc * 128:(cc + 1) * 128],
                            wot[:, cc * 128:(cc + 1) * 128],
                            wot[:, 4096 + cc * 128:4096 + (cc + 1) * 128],
                            ALU.mult)
                        V.mark(cp)
                        G.wait(V)
                        SG.put(dstT[cc * 128:(cc + 1) * 128,
                                    rc * 128:(rc + 1) * 128],
                               wot[:, cc * 128:(cc + 1) * 128])

            quantize_transposed(fw0, D, D, t0T, 2, 3)

            def absmean_ternary(src, rows, cols, dstT, statcol):
                gsum_abs(src, rows, cols, statcol)
                cnt = float(rows * cols)
                V.o.drain()
                V.o.tensor_scalar(srow[:1, statcol + 1:statcol + 2],
                                  srow[:1, statcol:statcol + 1],
                                  1.0 / cnt, 1e-6, ALU.mult, ALU.add)
                V.o.drain()
                thr = V.o.tensor_scalar(srow[:1, statcol + 1:statcol + 2],
                                        srow[:1, statcol + 1:statcol + 2],
                                        0.5, None, ALU.mult)
                V.mark(thr)
                bcast_scalar(srow[:1, statcol + 1:statcol + 2],
                             qscal[:, 4:5], pst)
                quantize_transposed(src, rows, cols, dstT, 4, None)

            absmean_ternary(fcw1, DFF, D, tfcT, 8)
            absmean_ternary(pw2, D, DFF, tp2T, 10)
          barrier()

        def load_ln_transpose(src, s, tb, xblk, xnT, psA, psB, ldstream,
                              gamma_col=None, beta_col=None):
            t0 = s * S + tb * TBLK
            ldstream.eng.wait(V)
            ldstream.eng.wait(SC)
            ldstream.eng.wait(T)
            m = ldstream.put(xblk[:, :].rearrange("p (a d) -> p a d", d=D),
                             src[t0:t0 + TBLK, :]
                             .rearrange("(a p) d -> p a d", p=128))
            wait_dma(V, ldstream, m)
            wait_dma(SC, ldstream, m)
            SC.wait(T)
            SC.wait(V)
            for a in range(NTILE):
                c = xblk[:, a * D:(a + 1) * D]
                rd_ = V.o.tensor_reduce(stats[:, 16 + a:17 + a], c, axis=AXX,
                                        op=ALU.add)
                sq_ = SC.o.activation(dump[:, :1024], c, ACT.Square,
                                      accum_out=stats[:, 20 + a:21 + a])
            V.mark(rd_)
            SC.mark(sq_)
            V.wait(SC)
            V.o.drain()
            V.o.tensor_scalar(stats[:, 16:20], stats[:, 16:20], 1.0 / D,
                              None, ALU.mult)
            V.o.tensor_scalar(stats[:, 20:24], stats[:, 20:24], 1.0 / D,
                              None, ALU.mult)
            V.o.drain()
            V.o.tensor_tensor(stats[:, 24:28], stats[:, 16:20],
                              stats[:, 16:20], ALU.mult)
            V.o.drain()
            vv = V.o.tensor_tensor(stats[:, 20:24], stats[:, 20:24],
                                   stats[:, 24:28], ALU.subtract)
            V.mark(vv)
            SC.wait(V)
            sq = SC.o.activation(stats[:, 24:28], stats[:, 20:24], ACT.Sqrt,
                                 bias=EPS_LN)
            SC.mark(sq)
            V.wait(SC)
            rr = V.o.reciprocal(stats[:, 24:28], stats[:, 24:28])
            V.mark(rr)
            if LN_PARTS[0] < 2:
                return
            for a in range(NTILE):
                src_c = xblk[:, a * D:(a + 1) * D]
                V.wait(T)
                nrm = V.o.tensor_scalar(dump[:, :1024], src_c,
                                        stats[:, 16 + a:17 + a],
                                        stats[:, 24 + a:25 + a],
                                        ALU.subtract, ALU.mult)
                V.mark(nrm)
                if LN_PARTS[0] < 3:
                    continue
                for c in range(DC):
                    pb = psA if c % 2 == 0 else psB
                    T.wait(V)
                    tr = T.o.transpose(pb[:, :128], dump[:, c * 128:(c + 1) * 128],
                                       ident[:, :])
                    T.mark(tr)
                    V.wait(T)
                    cp = V.o.tensor_copy(
                        xnT[:, c * TBLK + a * 128:c * TBLK + (a + 1) * 128],
                        pb[:, :128])
                    V.mark(cp)
            if gamma_col is not None:
                for c in range(DC):
                    sc_ = V.o.tensor_scalar(xnT[:, c * TBLK:(c + 1) * TBLK],
                                            xnT[:, c * TBLK:(c + 1) * TBLK],
                                            gamma_col[:, c:c + 1],
                                            beta_col[:, c:c + 1],
                                            ALU.mult, ALU.add)
                    V.mark(sc_)

        def phase_A(s):
            pfx = f"a{s}_"
            with contextlib.ExitStack() as ph:
                KT = ph.enter_context(nc.sbuf_tensor(pfx + "KT", [128, DC * S], F32))
                QT = ph.enter_context(nc.sbuf_tensor(pfx + "QT", [128, DC * S], F32))
                xblk = ph.enter_context(nc.sbuf_tensor(pfx + "xblk", [128, NTILE * D], F32))
                xnT = ph.enter_context(nc.sbuf_tensor(pfx + "xnT", [128, DC * TBLK], F32))
                wrow = ph.enter_context(nc.sbuf_tensor(pfx + "wrow", [128, 2 * 1024], F32))
                brow = ph.enter_context(nc.sbuf_tensor(pfx + "brow", [1, 3 * D], F32))
                zall = ph.enter_context(nc.sbuf_tensor(pfx + "zall", [128, 16], F32))

                ps1 = contextlib.ExitStack()
                psA = ps1.enter_context(nc.psum_tensor(pfx + "psA", [128, 512], F32))
                psB = ps1.enter_context(nc.psum_tensor(pfx + "psB", [128, 512], F32))
                tmark_half = [0, 0]
                for tb in range(A1_NTB[0]):
                    load_ln_transpose(xin, s, tb, xblk, xnT, psA, psB, LY)
                    for fc in range(A1_NFC[0]):
                        dst = QT if fc < 8 else KT
                        freg = fc % 8
                        half = fc % 2
                        pb = psA if half == 0 else psB
                        Y.wait(T, tmark_half[half])
                        m = LW.put(wrow[:, half * 1024:half * 1024 + 128 * DC]
                                   .rearrange("p (c w) -> p c w", w=128),
                                   w0T[:, fc * 128:(fc + 1) * 128]
                                   .rearrange("(c p) w -> p c w", p=128))
                        wait_dma(T, LW, m)
                        T.wait(V)
                        for c in range(DC):
                            mm_ = T.o.matmul(pb[:, :512],
                                             wrow[:, half * 1024 + c * 128:
                                                  half * 1024 + (c + 1) * 128],
                                             xnT[:, c * TBLK:(c + 1) * TBLK],
                                             start=(c == 0), stop=(c == DC - 1))
                        tmark_half[half] = T.mark(mm_)
                        V.wait(T)
                        ev = V.o.tensor_scalar(
                            dst[:, freg * S + tb * TBLK:freg * S + (tb + 1) * TBLK],
                            pb[:, :512], b0sb[:, fc:fc + 1], None, ALU.add)
                        V.mark(ev)
                barrier()
                ps1.close()
                if A_STOP[0] < 2:
                    return

                with contextlib.ExitStack() as p2:
                    psS1 = p2.enter_context(nc.psum_tensor(pfx + "psS1", [128, S], F32))
                    psS2 = p2.enter_context(nc.psum_tensor(pfx + "psS2", [128, S], F32))
                    est = p2.enter_context(nc.sbuf_tensor(pfx + "est", [128, 2 * S], F32))
                    for jt in range(16):
                        pb = psS1 if jt % 2 == 0 else psS2
                        half = jt % 2
                        T.wait(SC)
                        T.wait(V)
                        for ig in range(4):
                            for c in range(DC):
                                mm_ = T.o.matmul(pb[:, ig * 512:(ig + 1) * 512],
                                                 KT[:, c * S + jt * 128:c * S + (jt + 1) * 128],
                                                 QT[:, c * S + ig * 512:c * S + (ig + 1) * 512],
                                                 start=(c == 0), stop=(c == DC - 1))
                        T.mark(mm_)
                        V.wait(T)
                        V.o.tensor_reduce(stats[:, 40:41], pb[:, :], axis=AXX,
                                          op=ALU.max)
                        V.o.drain()
                        nm = V.o.tensor_scalar(stats[:, 41:42], stats[:, 40:41],
                                               -1.0 / 1024.0, None, ALU.mult)
                        V.mark(nm)
                        SC.wait(V)
                        wait_dma(SC, SG)
                        ex = SC.o.activation(est[:, half * S:(half + 1) * S],
                                             pb[:, :], ACT.Exp,
                                             bias=stats[:, 41:42],
                                             scale=1.0 / 1024.0,
                                             accum_out=zall[:, jt:jt + 1])
                        SC.mark(ex)
                        G.wait(SC)
                        SG.put(epark[jt * 128:(jt + 1) * 128, :],
                               est[:, half * S:(half + 1) * S])
                    V.wait(SC)
                    rz = V.o.reciprocal(zall[:, :], zall[:, :])
                    V.mark(rz)
                barrier()
                if A_STOP[0] < 3:
                    return

                Vb = QT
                wvres = KT
                with contextlib.ExitStack() as p3:
                    psA = p3.enter_context(nc.psum_tensor(pfx + "psA2", [128, 512], F32))
                    psB = p3.enter_context(nc.psum_tensor(pfx + "psB2", [128, 512], F32))
                    m = LW.put(wvres[:, :DC * 1024]
                               .rearrange("p (c w) -> p c w", w=1024),
                               w0T[:, 2048:3072]
                               .rearrange("(c p) w -> p c w", p=128))
                    mrow = LY.put(brow[:1, :], b0r[:1, :])
                    for tb in range(NTB):
                        load_ln_transpose(xin, s, tb, xblk, xnT, psA, psB, LY)
                        wait_dma(T, LW, m)
                        wait_dma(T, LY, mrow)
                        for a in range(NTILE):
                            jt = tb * NTILE + a
                            for kb in range(2):
                                pb = psA if kb == 0 else psB
                                T.wait(V)
                                for c in range(DC):
                                    T.o.matmul(pb[:, :512],
                                               xnT[:, c * TBLK + a * 128:
                                                   c * TBLK + (a + 1) * 128],
                                               wvres[:, c * 1024 + kb * 512:
                                                     c * 1024 + (kb + 1) * 512],
                                               start=(c == 0), stop=False)
                                mm_ = T.o.matmul(pb[:, :512], onerow[:1, :],
                                                 brow[:1, 2048 + kb * 512:
                                                      2048 + (kb + 1) * 512],
                                                 start=False, stop=True)
                                T.mark(mm_)
                                V.wait(T)
                                ev = V.o.tensor_scalar(
                                    Vb[:, jt * 1024 + kb * 512:
                                       jt * 1024 + (kb + 1) * 512],
                                    pb[:, :512], zall[:, jt:jt + 1],
                                    None, ALU.mult)
                                V.mark(ev)
                    barrier()
                    if A_STOP[0] < 4:
                        return

                    EC = KT
                    OB = 16 * 512
                    for ib in range(4):
                        t0 = s * S + ib * TBLK
                        Y.wait(T)
                        for jt in range(16):
                            LY.put(EC[:, jt * 512:(jt + 1) * 512],
                                   epark[jt * 128:(jt + 1) * 128,
                                         ib * TBLK:(ib + 1) * TBLK])
                        G.wait(V)
                        mx_ = LG.put(xblk[:, :].rearrange("p (a d) -> p a d", d=D),
                                     xin[t0:t0 + TBLK, :]
                                     .rearrange("(a p) d -> p a d", p=128))
                        wait_dma(T, LY)
                        wait_dma(V, SG)
                        for u in range(4):
                            for kb in range(2):
                                pb = psA if (u * 2 + kb) % 2 == 0 else psB
                                T.wait(V)
                                for jt in range(16):
                                    mm_ = T.o.matmul(pb[:, :512],
                                                     EC[:, jt * 512 + u * 128:
                                                        jt * 512 + u * 128 + 128],
                                                     Vb[:, jt * 1024 + kb * 512:
                                                        jt * 1024 + (kb + 1) * 512],
                                                     start=(jt == 0), stop=(jt == 15))
                                T.mark(mm_)
                                V.wait(T)
                                wait_dma(V, LG, mx_)
                                ev = V.o.tensor_tensor(
                                    KT[:, OB + u * D + kb * 512:
                                       OB + u * D + (kb + 1) * 512],
                                    pb[:, :512],
                                    xblk[:, u * D + kb * 512:u * D + (kb + 1) * 512],
                                    ALU.add)
                                V.mark(ev)
                        G.wait(V)
                        SG.put(xpark[t0:t0 + TBLK, :]
                               .rearrange("(a p) d -> p a d", p=128),
                               KT[:, OB:OB + NTILE * D]
                               .rearrange("p (a d) -> p a d", d=D))
                    barrier()

        if "A" in phases:
            phase_A(0)
            if A_SEQS[0] > 1:
                phase_A(1)
        else:
            with nc.sbuf_tensor("cp_buf", [128, NTILE * D], F32) as sb:
                for s in range(NSEQ):
                    for tb in range(NTB):
                        t0 = s * S + tb * TBLK
                        wait_dma(Y, SG)
                        m = LY.put(sb[:, :].rearrange("p (a d) -> p a d", d=D),
                                   xin[t0:t0 + TBLK, :]
                                   .rearrange("(a p) d -> p a d", p=128))
                        wait_dma(G, LY, m)
                        SG.put(xpark[t0:t0 + TBLK, :]
                               .rearrange("(a p) d -> p a d", p=128),
                               sb[:, :].rearrange("p (a d) -> p a d", d=D))
            barrier()


        def phase_B():
            with contextlib.ExitStack() as ph:
                xblk = ph.enter_context(nc.sbuf_tensor("b_xblk", [128, NTILE * D], F32))
                xnT = ph.enter_context(nc.sbuf_tensor("b_xnT", [128, DC * TBLK], F32))
                zb = ph.enter_context(nc.sbuf_tensor("b_zb", [128, DC * TBLK], F32))
                abuf = ph.enter_context(nc.sbuf_tensor("b_abuf", [128, DC * TBLK], F32))
                sgn = ph.enter_context(nc.sbuf_tensor("b_sgn", [128, DC * TBLK], F32))
                ternsb = ph.enter_context(nc.sbuf_tensor("b_tern", [128, DC * 1024], F32))
                outb = ph.enter_context(nc.sbuf_tensor("b_outb", [128, NTILE * D], F32))
                ccsb = ph.enter_context(nc.sbuf_tensor("b_ccsb", [1, 128], F32))
                sgst = ph.enter_context(nc.sbuf_tensor("b_sgst", [128, DC * TBLK], F32))
                gpbc = ph.enter_context(nc.sbuf_tensor("b_gpbc", [128, 16], F32))
                psA = ph.enter_context(nc.psum_tensor("b_psA", [128, 512], F32))
                psB = ph.enter_context(nc.psum_tensor("b_psB", [128, 512], F32))

                first = [True]
                for s in range(NSEQ):
                    for tb in range(NTB):
                        load_ln_transpose(xpark, s, tb, xblk, xnT, psA, psB, LY,
                                          gamma_col=g0sb, beta_col=be0sb)
                        V.o.drain()
                        V.o.tensor_reduce(stats[:, 48:49], xnT[:, :], axis=AXX,
                                          op=ALU.max, apply_absolute_value=True)
                        V.o.drain()
                        if first[0]:
                            mxcp = V.o.tensor_copy(stats[:, 49:50], stats[:, 48:49])
                            first[0] = False
                        else:
                            mxcp = V.o.tensor_tensor(stats[:, 49:50], stats[:, 49:50],
                                                     stats[:, 48:49], ALU.max)
                        V.mark(mxcp)
                        V.o.drain()
                col_to_scalar(stats[:, 49:50], srow[:1, 12:13], ALU.max, psA)
                V.o.drain()

                V.o.memset(ccsb[:1, :], 0.0)
                V.o.drain()
                c0 = V.o.tensor_copy(ccsb[:1, 0:1], srow[:1, 12:13])
                V.mark(c0)
                G.wait(V)
                SG.put(ccin.ap()[:1, :], ccsb[:1, :])
                wait_dma(G, SG)
                cc = G.o.collective_compute(
                    "AllReduce", ALU.max, replica_groups=[list(range(NCORE))],
                    ins=[ccin.ap().opt()], outs=[ccout.ap().opt()])
                G.mark(cc)
                G.wait(G)
                LG.put(ccsb[:1, :], ccout.ap()[:1, :])
                wait_dma(V, LG)
                V.o.tensor_reduce(srow[:1, 13:14], ccsb[:1, :], axis=AXX, op=ALU.max)
                V.o.drain()
                gm = V.o.tensor_scalar(srow[:1, 14:15], srow[:1, 13:14], 1e-8,
                                       None, ALU.max)
                V.mark(gm)
                bcast_scalar(srow[:1, 14:15], qscal[:, 5:6], psA)
                V.o.reciprocal(qscal[:, 6:7], scv[:, 0:1])
                V.o.drain()
                V.o.tensor_tensor(qscal[:, 0:1], qscal[:, 5:6], qscal[:, 6:7],
                                  ALU.mult)
                V.o.drain()
                V.o.reciprocal(qscal[:, 1:2], qscal[:, 0:1])
                V.o.drain()
                V.o.tensor_scalar(gpbc[:, 0:8], g0sb[:, :], qscal[:, 1:2],
                                  None, ALU.mult)
                gb = V.o.tensor_scalar(gpbc[:, 8:16], be0sb[:, :], qscal[:, 1:2],
                                       None, ALU.mult)
                V.mark(gb)
                V.o.drain()

                mt = LW.put(ternsb[:, :].rearrange("p (c w) -> p c w", w=1024),
                            t0T[:, :].rearrange("(c p) w -> p c w", p=128))

                for s in range(NSEQ):
                    for tb in range(NTB):
                        t0 = s * S + tb * TBLK
                        wait_dma(V, SG)
                        load_ln_transpose(xpark, s, tb, xblk, xnT, psA, psB, LY,
                                          gamma_col=gpbc[:, 0:8],
                                          beta_col=gpbc[:, 8:16])
                        SC.wait(V)
                        ab_ = SC.o.activation(abuf[:, :], xnT[:, :], ACT.Abs)
                        sg_ = SC.o.activation(sgn[:, :], xnT[:, :], ACT.Sign)
                        SC.mark(sg_)
                        V.wait(SC)
                        cz = V.o.tensor_copy(zb[:, :], abuf[:, :])
                        V.mark(cz)
                        for k in (64, 32, 16, 8, 4, 2, 1):
                            SC.wait(V)
                            sk = SC.o.activation(sgst[:, :], zb[:, :],
                                                 ACT.Sign, bias=-63.5)
                            SC.mark(sk)
                            V.wait(SC)
                            st1 = V.o.scalar_tensor_tensor(
                                zb[:, :], sgst[:, :], -k / 2.0,
                                zb[:, :], ALU.mult, ALU.add)
                            V.mark(st1)
                        V.o.drain()
                        V.o.scalar_tensor_tensor(abuf[:, :], abuf[:, :], 63.5,
                                                 zb[:, :], ALU.add, ALU.subtract)
                        V.o.drain()
                        nf = V.o.tensor_tensor(xnT[:, :], abuf[:, :], sgn[:, :],
                                               ALU.mult)
                        V.mark(nf)
                        wait_dma(T, LW, mt)
                        for a in range(NTILE):
                            for kb in range(2):
                                pb = psA if kb == 0 else psB
                                T.wait(V)
                                for c in range(DC):
                                    mm_ = T.o.matmul(
                                        pb[:, :512],
                                        xnT[:, c * TBLK + a * 128:
                                            c * TBLK + (a + 1) * 128],
                                        ternsb[:, c * 1024 + kb * 512:
                                               c * 1024 + (kb + 1) * 512],
                                        start=(c == 0), stop=(c == DC - 1))
                                T.mark(mm_)
                                V.wait(T)
                                ev = V.o.scalar_tensor_tensor(
                                    outb[:, a * D + kb * 512:a * D + (kb + 1) * 512],
                                    pb[:, :512], qscal[:, 0:1],
                                    xblk[:, a * D + kb * 512:a * D + (kb + 1) * 512],
                                    ALU.mult, ALU.add)
                                V.mark(ev)
                        G.wait(V)
                        SG.put(xpark[t0:t0 + TBLK, :]
                               .rearrange("(a p) d -> p a d", p=128),
                               outb[:, :].rearrange("p (a d) -> p a d", d=D))
                barrier()

        if "B" in phases:
            phase_B()

        def phase_C(s):
            pfx = f"c{s}_"
            with contextlib.ExitStack() as ph:
                KT = ph.enter_context(nc.sbuf_tensor(pfx + "KT", [128, DC * S], F32))
                VA = ph.enter_context(nc.sbuf_tensor(pfx + "VA", [128, 16 * 1040], F32))
                QTb = ph.enter_context(nc.sbuf_tensor(pfx + "QTb", [128, DC * 512], F32))
                xblk = ph.enter_context(nc.sbuf_tensor(pfx + "xblk", [128, NTILE * D], F32))
                xnT = ph.enter_context(nc.sbuf_tensor(pfx + "xnT", [128, DC * TBLK], F32))
                attK = xnT
                wst = ph.enter_context(nc.sbuf_tensor(pfx + "wst", [128, 4096], F32))
                EB = ph.enter_context(nc.sbuf_tensor(pfx + "EB", [128, 1024], F32))
                knb = ph.enter_context(nc.sbuf_tensor(pfx + "knb", [128, 256], F32))
                brow1 = ph.enter_context(nc.sbuf_tensor(pfx + "brow1", [1, 1024], F32))
                pbr = brow1
                cb = stats[:, 0:16]
                rzb = stats[:, 52:60]
                attT = QTb

                mb1 = LY.put(brow1[:1, :], b1r[:1, 2048:3072])

                with contextlib.ExitStack() as p1:
                    psA = p1.enter_context(nc.psum_tensor(pfx + "psA", [128, 512], F32))
                    psB = p1.enter_context(nc.psum_tensor(pfx + "psB", [128, 512], F32))
                    for tb in range(NTB):
                        load_ln_transpose(xpark, s, tb, xblk, xnT, psA, psB, LY)
                        for fc in range(8, 16):
                            freg = fc - 8
                            pb = psA if fc % 2 == 0 else psB
                            Y.wait(T)
                            mw = LW.put(wst[:, :DC * 128]
                                        .rearrange("p (c w) -> p c w", w=128),
                                        w1T[:, fc * 128:(fc + 1) * 128]
                                        .rearrange("(c p) w -> p c w", p=128))
                            wait_dma(T, LW, mw)
                            T.wait(V)
                            for c in range(DC):
                                mm_ = T.o.matmul(pb[:, :512],
                                                 wst[:, c * 128:(c + 1) * 128],
                                                 xnT[:, c * TBLK:(c + 1) * TBLK],
                                                 start=(c == 0), stop=(c == DC - 1))
                            T.mark(mm_)
                            V.wait(T)
                            ev = V.o.tensor_scalar(
                                KT[:, freg * S + tb * TBLK:freg * S + (tb + 1) * TBLK],
                                pb[:, :512], b1sb[:, fc:fc + 1], None, ALU.add)
                            V.mark(ev)
                        wait_dma(T, LY, mb1)
                        for kb in range(2):
                            Y.wait(T)
                            mw = LW.put(wst[:, :].rearrange("p (c w) -> p c w", w=512),
                                        w1T[:, 2048 + kb * 512:2048 + (kb + 1) * 512]
                                        .rearrange("(c p) w -> p c w", p=128))
                            wait_dma(T, LW, mw)
                            for a in range(NTILE):
                                jt = tb * NTILE + a
                                pb = psA if a % 2 == 0 else psB
                                T.wait(V)
                                for c in range(DC):
                                    T.o.matmul(pb[:, :512],
                                               xnT[:, c * TBLK + a * 128:
                                                   c * TBLK + (a + 1) * 128],
                                               wst[:, c * 512:(c + 1) * 512],
                                               start=(c == 0), stop=False)
                                mm_ = T.o.matmul(pb[:, :512], onerow[:1, :],
                                                 brow1[:1, kb * 512:(kb + 1) * 512],
                                                 start=False, stop=True)
                                T.mark(mm_)
                                V.wait(T)
                                dst = bass.AP(VA, jt * 1040 + kb * 520,
                                              [[16 * 1040, 128], [65, 8], [1, 64]])
                                ev = V.o.tensor_copy(
                                    dst, pb[:, :512].rearrange("p (h w) -> p h w", w=64))
                                V.mark(ev)
                    for jt in range(16):
                        on = V.o.memset(
                            bass.AP(VA, jt * 1040 + 64, [[16 * 1040, 128], [65, 16]]),
                            1.0)
                        V.mark(on)
                    for c in range(DC):
                        for half in range(2):
                            SC.wait(T)
                            SC.wait(V)
                            sq_ = SC.o.activation(
                                dump[:, :1024],
                                KT[:, c * S + half * 1024:c * S + (half + 1) * 1024],
                                ACT.Square)
                            SC.mark(sq_)
                            T.wait(SC)
                            T.wait(V)
                            for hh in range(2):
                                for jj in range(8):
                                    jt = half * 8 + jj
                                    base = hh * 64
                                    mm_ = T.o.matmul(
                                        psA[:, jt:jt + 1] if hh == 0 else psB[:, jt:jt + 1],
                                        dump[base:base + 64, jj * 128:(jj + 1) * 128],
                                        scv[base:base + 64, 3:4],
                                        start=(half == 0 and jj == 0),
                                        stop=(half == 1 and jj == 7))
                                T.mark(mm_)
                        V.wait(T)
                        cv1 = V.o.tensor_copy(knb[:, (2 * c) * 16:(2 * c) * 16 + 16],
                                              psA[:, 0:16])
                        cv2 = V.o.tensor_copy(knb[:, (2 * c + 1) * 16:(2 * c + 1) * 16 + 16],
                                              psB[:, 0:16])
                        V.mark(cv2)
                    for h in range(16):
                        V.o.drain()
                        rk = V.o.tensor_reduce(stats[:, 50:51],
                                               knb[:, h * 16:(h + 1) * 16],
                                               axis=AXX, op=ALU.max)
                        V.mark(rk)
                        col_to_scalar(stats[:, 50:51], srow[:1, 16 + h:17 + h],
                                      ALU.max, psA)
                barrier()

                with contextlib.ExitStack() as p2:
                    psE1 = p2.enter_context(nc.psum_tensor(pfx + "psE1", [128, 512], F32))
                    psE2 = p2.enter_context(nc.psum_tensor(pfx + "psE2", [128, 512], F32))
                    pacA = p2.enter_context(nc.psum_tensor(pfx + "pacA", [128, 260], F32))
                    pacB = p2.enter_context(nc.psum_tensor(pfx + "pacB", [128, 260], F32))
                    psW = p2.enter_context(nc.psum_tensor(pfx + "psW", [128, 512], F32))
                    psX = p2.enter_context(nc.psum_tensor(pfx + "psX", [128, 512], F32))
                    for ib in range(4):
                        t0 = s * S + ib * TBLK
                        wait_dma(V, SG)
                        load_ln_transpose(xpark, s, ib, xblk, xnT, psW, psX, LY)
                        for fc in range(8):
                            pb = psW if fc % 2 == 0 else psX
                            Y.wait(T)
                            mw = LW.put(wst[:, :DC * 128]
                                        .rearrange("p (c w) -> p c w", w=128),
                                        w1T[:, fc * 128:(fc + 1) * 128]
                                        .rearrange("(c p) w -> p c w", p=128))
                            wait_dma(T, LW, mw)
                            T.wait(V)
                            for c in range(DC):
                                mm_ = T.o.matmul(pb[:, :512],
                                                 wst[:, c * 128:(c + 1) * 128],
                                                 xnT[:, c * TBLK:(c + 1) * TBLK],
                                                 start=(c == 0), stop=(c == DC - 1))
                            T.mark(mm_)
                            V.wait(T)
                            ev = V.o.tensor_scalar(
                                QTb[:, fc * 512:(fc + 1) * 512],
                                pb[:, :512], b1sb[:, fc:fc + 1], None, ALU.add)
                            V.mark(ev)
                        if C_STOP[0] == 1:
                            G.wait(V)
                            SG.put(epark[0:128, :], QTb[:, 0:2048])
                            SG.put(epark[128:256, :], xnT[:, 0:2048])
                            wait_dma(G, SG)
                            return
                        for c in range(DC):
                            SC.wait(T)
                            SC.wait(V)
                            sq_ = SC.o.activation(dump[:, :512],
                                                  QTb[:, c * 512:(c + 1) * 512],
                                                  ACT.Square)
                            SC.mark(sq_)
                            T.wait(SC)
                            for hh in range(2):
                                base = hh * 64
                                for u in range(4):
                                    mm_ = T.o.matmul(
                                        psW[:, (2 * c + hh) * 4 + u:
                                            (2 * c + hh) * 4 + u + 1],
                                        dump[base:base + 64, u * 128:(u + 1) * 128],
                                        scv[base:base + 64, 3:4],
                                        start=(c == 0 and hh == 0 and u == 0),
                                        stop=(c == DC - 1 and hh == 1 and u == 3))
                            T.mark(mm_)
                        for h in range(16):
                            V.wait(T)
                            rq = V.o.tensor_reduce(stats[:, 51:52],
                                                   psW[:, 4 * h:4 * h + 4],
                                                   axis=AXX, op=ALU.max)
                            V.mark(rq)
                            col_to_scalar(stats[:, 51:52], srow[:1, h:h + 1],
                                          ALU.max, psX)
                        V.o.drain()
                        tt_ = V.o.tensor_tensor(srow[:1, 40:56], srow[:1, 0:16],
                                                srow[:1, 16:32], ALU.mult)
                        V.mark(tt_)
                        SC.wait(V)
                        sq_ = SC.o.activation(srow[:1, 40:56], srow[:1, 40:56],
                                              ACT.Sqrt)
                        SC.mark(sq_)
                        V.wait(SC)
                        ng = V.o.tensor_scalar(srow[:1, 40:56], srow[:1, 40:56],
                                               -0.125, None, ALU.mult)
                        V.mark(ng)
                        V.o.drain()
                        bcast_scalar(srow[:1, 40:56], cb, psW)

                        if C_STOP[0] == 2:
                            G.wait(V)
                            SG.put(epark[0:1, 0:64], srow[:1, 0:64])
                            SG.put(epark[2:130, 0:16], cb)
                            SG.put(epark[256:384, :], KT[:, 0:2048])
                            wait_dma(G, SG)
                            return
                        for hp in range(8):
                            hA, hB = 2 * hp, 2 * hp + 1
                            for jt in range(16):
                                T.wait(SC)
                                if jt >= 1:
                                    pass
                                mmA = T.o.matmul(
                                    psE1[:, :512],
                                    KT[0:64, hp * S + jt * 128:hp * S + (jt + 1) * 128],
                                    QTb[0:64, hp * 512:(hp + 1) * 512],
                                    start=True, stop=True)
                                mmB = T.o.matmul(
                                    psE2[:, :512],
                                    KT[64:128, hp * S + jt * 128:hp * S + (jt + 1) * 128],
                                    QTb[64:128, hp * 512:(hp + 1) * 512],
                                    start=True, stop=True)
                                T.mark(mmB)
                                SC.wait(T)
                                exA = SC.o.activation(EB[:, 0:512], psE1[:, :],
                                                      ACT.Exp, scale=0.125,
                                                      bias=cb[:, hA:hA + 1])
                                exB = SC.o.activation(EB[:, 512:1024], psE2[:, :],
                                                      ACT.Exp, scale=0.125,
                                                      bias=cb[:, hB:hB + 1])
                                SC.mark(exB)
                                T.wait(SC)
                                if jt == 0:
                                    T.wait(V)
                                for u in range(4):
                                    T.o.matmul(pacA[:, 65 * u:65 * u + 65],
                                               EB[:, u * 128:(u + 1) * 128],
                                               bass.AP(VA, jt * 1040 + hA * 65,
                                                       [[16 * 1040, 128], [1, 65]]),
                                               start=(jt == 0 and u == 0),
                                               stop=(jt == 15 and u == 3))
                                    mm_ = T.o.matmul(
                                        pacB[:, 65 * u:65 * u + 65],
                                        EB[:, 512 + u * 128:512 + (u + 1) * 128],
                                        bass.AP(VA, jt * 1040 + hB * 65,
                                                [[16 * 1040, 128], [1, 65]]),
                                        start=(jt == 0 and u == 0),
                                        stop=(jt == 15 and u == 3))
                                T.mark(mm_)
                            V.wait(T)
                            r1 = V.o.reciprocal(rzb[:, 0:4],
                                                bass.AP(pacA, 64, [[260, 128], [65, 4]]))
                            r2 = V.o.reciprocal(rzb[:, 4:8],
                                                bass.AP(pacB, 64, [[260, 128], [65, 4]]))
                            V.mark(r2)
                            V.o.drain()
                            for u in range(4):
                                V.o.tensor_scalar(
                                    attK[:, u * 1024 + hA * 64:u * 1024 + hA * 64 + 64],
                                    pacA[:, 65 * u:65 * u + 64],
                                    rzb[:, u:u + 1], None, ALU.mult)
                                nv = V.o.tensor_scalar(
                                    attK[:, u * 1024 + hB * 64:u * 1024 + hB * 64 + 64],
                                    pacB[:, 65 * u:65 * u + 64],
                                    rzb[:, 4 + u:5 + u], None, ALU.mult)
                            V.mark(nv)
                            if C_STOP[0] == 3 and hp == 0:
                                pz = V.o.tensor_copy(dump[:, 0:260], pacA[:, :])
                                V.mark(pz)
                                G.wait(V)
                                SG.put(epark[0:128, 0:1024], EB[:, :])
                                SG.put(epark[128:256, 0:260], dump[:, 0:260])
                                SG.put(epark[256:384, 0:8], rzb)
                                SG.put(epark[384:512, 0:2048], attK[:, 0:2048])
                                SG.put(epark[512:513, 0:64], srow[:1, 0:64])
                                wait_dma(G, SG)
                                return

                        if ib == 0:
                            Y.wait(T)
                            mpb = LY.put(pbr[:1, :], pb1r[:1, :])
                        for u in range(NTILE):
                            for c in range(DC):
                                pb = psW if c % 2 == 0 else psX
                                T.wait(V)
                                tr = T.o.transpose(pb[:, :128],
                                                   attK[:, u * 1024 + c * 128:
                                                        u * 1024 + (c + 1) * 128],
                                                   ident[:, :])
                                T.mark(tr)
                                V.wait(T)
                                cp = V.o.tensor_copy(
                                    attT[:, c * 512 + u * 128:c * 512 + (u + 1) * 128],
                                    pb[:, :128])
                                V.mark(cp)
                        wait_dma(T, LY, mpb)
                        for ob in range(2):
                            Y.wait(T)
                            mw = LW.put(wst[:, :].rearrange("p (c w) -> p c w", w=512),
                                        wp1T[:, ob * 512:(ob + 1) * 512]
                                        .rearrange("(c p) w -> p c w", p=128))
                            wait_dma(T, LW, mw)
                            for u in range(NTILE):
                                pb = psW if u % 2 == 0 else psX
                                T.wait(V)
                                for c in range(DC):
                                    T.o.matmul(pb[:, :512],
                                               attT[:, c * 512 + u * 128:
                                                    c * 512 + (u + 1) * 128],
                                               wst[:, c * 512:c * 512 + 512],
                                               start=(c == 0), stop=False)
                                mm_ = T.o.matmul(pb[:, :512], onerow[:1, :],
                                                 pbr[:1, ob * 512:(ob + 1) * 512],
                                                 start=False, stop=True)
                                T.mark(mm_)
                                V.wait(T)
                                ev = V.o.tensor_tensor(
                                    attK[:, u * 1024 + ob * 512:u * 1024 + (ob + 1) * 512],
                                    pb[:, :512],
                                    xblk[:, u * D + ob * 512:u * D + (ob + 1) * 512],
                                    ALU.add)
                                V.mark(ev)
                        G.wait(V)
                        SG.put(xpark[t0:t0 + TBLK, :]
                               .rearrange("(a p) d -> p a d", p=128),
                               attK[:, :].rearrange("p (a d) -> p a d", d=D))
                    barrier()

        if "C" in phases:
            phase_C(0)
            phase_C(1)

        def phase_D():
            with contextlib.ExitStack() as ph:
                xblk = ph.enter_context(nc.sbuf_tensor("d_xblk", [128, NTILE * D], F32))
                xnT = ph.enter_context(nc.sbuf_tensor("d_xnT", [128, DC * TBLK], F32))
                gbuf = ph.enter_context(nc.sbuf_tensor("d_gbuf", [128, 32 * 512], F32))
                wstD = ph.enter_context(nc.sbuf_tensor("d_wst", [128, 4096], F32))
                lnwb = ph.enter_context(nc.sbuf_tensor("d_lnwb", [2, DFF], F32))
                rowsA = ph.enter_context(nc.sbuf_tensor("d_rowsA", [1, 512], F32))
                rowsB = ph.enter_context(nc.sbuf_tensor("d_rowsB", [2, 512], F32))
                trsb = ph.enter_context(nc.sbuf_tensor("d_trsb", [8, 128], F32))
                ml = LY.put(lnwb[:2, :], lnwb2[:2, :])
                for kq in range(4):
                    LY.put(rowsB[1:2, kq * 128:(kq + 1) * 128], conerow[:1, :])
                for s in range(NSEQ):
                    for tb in range(NTB):
                        t0 = s * S + tb * TBLK
                        with contextlib.ExitStack() as pp:
                            psA = pp.enter_context(
                                nc.psum_tensor(f"d_psA{s}{tb}", [128, 512], F32))
                            psB = pp.enter_context(
                                nc.psum_tensor(f"d_psB{s}{tb}", [128, 512], F32))
                            wait_dma(SC, SG)
                            wait_dma(V, SG)
                            load_ln_transpose(xpark, s, tb, xblk, xnT, psA, psB, LY)
                            for og in range(8):
                                Y.wait(T)
                                mw = LW.put(wstD[:, :]
                                            .rearrange("p (c w) -> p c w", w=512),
                                            tfcT[:, og * 512:(og + 1) * 512]
                                            .rearrange("(c p) w -> p c w", p=128))
                                wait_dma(T, LW, mw)
                                for oo in range(4):
                                    oc = og * 4 + oo
                                    pb = psA if oc % 2 == 0 else psB
                                    T.wait(SC)
                                    for c in range(DC):
                                        mm_ = T.o.matmul(
                                            pb[:, :512],
                                            wstD[:, c * 512 + oo * 128:
                                                 c * 512 + oo * 128 + 128],
                                            xnT[:, c * TBLK:(c + 1) * TBLK],
                                            start=(c == 0), stop=(c == DC - 1))
                                    T.mark(mm_)
                                    SC.wait(T)
                                    ge = SC.o.activation(
                                        gbuf[:, oc * 512:(oc + 1) * 512],
                                        pb[:, :512], ACT.Gelu)
                                    SC.mark(ge)
                            for oc in range(32):
                                SC.wait(T)
                                sq_ = SC.o.activation(dump[:, :512],
                                                      gbuf[:, oc * 512:(oc + 1) * 512],
                                                      ACT.Square)
                                SC.mark(sq_)
                                T.wait(SC)
                                for u in range(4):
                                    T.o.matmul(psA[:, 32 + u:33 + u],
                                               gbuf[:, oc * 512 + u * 128:
                                                    oc * 512 + (u + 1) * 128],
                                               scv[:, 3:4],
                                               start=(oc == 0 and u == 0),
                                               stop=(oc == 31 and u == 3))
                                    mm_ = T.o.matmul(psB[:, 32 + u:33 + u],
                                                     dump[:, u * 128:(u + 1) * 128],
                                                     scv[:, 3:4],
                                                     start=(oc == 0 and u == 0),
                                                     stop=(oc == 31 and u == 3))
                                T.mark(mm_)
                            V.wait(T)
                            V.o.tensor_scalar(stats[:, 32:36], psA[:, 32:36],
                                              1.0 / DFF, None, ALU.mult)
                            V.o.tensor_scalar(stats[:, 36:40], psB[:, 32:36],
                                              1.0 / DFF, None, ALU.mult)
                            V.o.drain()
                            V.o.tensor_tensor(stats[:, 40:44], stats[:, 32:36],
                                              stats[:, 32:36], ALU.mult)
                            V.o.drain()
                            vv = V.o.tensor_tensor(stats[:, 36:40], stats[:, 36:40],
                                                   stats[:, 40:44], ALU.subtract)
                            V.mark(vv)
                            SC.wait(V)
                            sq_ = SC.o.activation(stats[:, 40:44], stats[:, 36:40],
                                                  ACT.Sqrt, bias=EPS_LN)
                            SC.mark(sq_)
                            V.wait(SC)
                            V.o.reciprocal(stats[:, 40:44], stats[:, 40:44])
                            V.o.drain()
                            V.o.tensor_tensor(stats[:, 44:48], stats[:, 32:36],
                                              stats[:, 40:44], ALU.mult)
                            V.o.drain()
                            mr = V.o.tensor_scalar(stats[:, 44:48], stats[:, 44:48],
                                                   -1.0, None, ALU.mult)
                            V.mark(mr)
                            T.wait(V)
                            tr = T.o.transpose(psA[0:8, 0:128], stats[:, 40:48],
                                               ident[:, :])
                            T.mark(tr)
                            V.wait(T)
                            cpr = V.o.tensor_copy(trsb[0:8, :], psA[0:8, 0:128])
                            V.mark(cpr)
                            G.wait(V)
                            mra = SG.put(rowsA[:1, :], trsb[0:4, :])
                            mrb = SG.put(rowsB[0:1, :], trsb[4:8, :])
                            wait_dma(T, SG, mrb)
                            wait_dma(T, LY, ml)
                            for oc in range(32):
                                T.wait(V)
                                mmA = T.o.matmul(psA[:, :512],
                                                 lnwb[0:1, oc * 128:(oc + 1) * 128],
                                                 rowsA[:1, :], start=True, stop=True)
                                mmB = T.o.matmul(psB[:, :512],
                                                 lnwb[0:2, oc * 128:(oc + 1) * 128],
                                                 rowsB[:2, :], start=True, stop=True)
                                T.mark(mmB)
                                V.wait(T)
                                V.o.tensor_tensor(gbuf[:, oc * 512:(oc + 1) * 512],
                                                  gbuf[:, oc * 512:(oc + 1) * 512],
                                                  psA[:, :512], ALU.mult)
                                V.o.drain()
                                nv = V.o.tensor_tensor(gbuf[:, oc * 512:(oc + 1) * 512],
                                                       gbuf[:, oc * 512:(oc + 1) * 512],
                                                       psB[:, :512], ALU.add)
                                V.mark(nv)
                        with contextlib.ExitStack() as pp2:
                            pps = [pp2.enter_context(
                                nc.psum_tensor(f"d_pp{s}{tb}_{i}", [128, 512], F32))
                                for i in range(8)]
                            T.wait(V)
                            for og in range(8):
                                Y.wait(T)
                                mw = LW.put(wstD[:, :]
                                            .rearrange("p (c w) -> p c w", w=1024),
                                            tp2T[og * 512:(og + 1) * 512, :]
                                            .rearrange("(c p) w -> p c w", p=128))
                                wait_dma(T, LW, mw)
                                for oo in range(4):
                                    oc = og * 4 + oo
                                    for ob in range(2):
                                        for u in range(4):
                                            mm_ = T.o.matmul(
                                                pps[ob * 4 + u][:, :512],
                                                gbuf[:, oc * 512 + u * 128:
                                                     oc * 512 + (u + 1) * 128],
                                                wstD[:, oo * 1024 + ob * 512:
                                                     oo * 1024 + (ob + 1) * 512],
                                                start=(oc == 0), stop=(oc == 31))
                                T.mark(mm_)
                            V.wait(T)
                            for ob in range(2):
                                for u in range(4):
                                    ev = V.o.tensor_tensor(
                                        gbuf[:, u * 1024 + ob * 512:
                                             u * 1024 + (ob + 1) * 512],
                                        pps[ob * 4 + u][:, :512],
                                        xblk[:, u * D + ob * 512:u * D + (ob + 1) * 512],
                                        ALU.add)
                            V.mark(ev)
                            G.wait(V)
                            SG.put(yout[t0:t0 + TBLK, :]
                                   .rearrange("(a p) d -> p a d", p=128),
                                   gbuf[:, :NTILE * D]
                                   .rearrange("p (a d) -> p a d", d=D))
                barrier()

        if "D" in phases:
            phase_D()

        if "D" not in phases:
          with nc.sbuf_tensor("st_buf", [128, NTILE * D], F32) as sb:
            for s in range(NSEQ):
                for tb in range(NTB):
                    t0 = s * S + tb * TBLK
                    wait_dma(Y, SG)
                    m = LY.put(sb[:, :].rearrange("p (a d) -> p a d", d=D),
                               xpark[t0:t0 + TBLK, :]
                               .rearrange("(a p) d -> p a d", p=128))
                    wait_dma(G, LY, m)
                    SG.put(yout[t0:t0 + TBLK, :]
                           .rearrange("(a p) d -> p a d", p=128),
                           sb[:, :].rearrange("p (a d) -> p a d", d=D))
          barrier()

        wait_dma(G, SG)
        wait_dma(Y, LY)

    return nc


_NC_CACHE = {}


def _get_program():
    if "nc" not in _NC_CACHE:
        _NC_CACHE["nc"] = build_program()
    return _NC_CACHE["nc"]


def make_in_maps(inputs_by_name):
    x = np.ascontiguousarray(np.asarray(inputs_by_name["L0_attn_x"],
                                        dtype=np.float32))
    shared = {
        "w0": np.ascontiguousarray(inputs_by_name["L0_attn_W"]),
        "b0c": np.ascontiguousarray(
            np.asarray(inputs_by_name["L0_attn_b"]).reshape(24, 128).T),
        "b0r": np.asarray(inputs_by_name["L0_attn_b"]).reshape(1, 3 * D),
        "fw0": np.ascontiguousarray(inputs_by_name["L0_ffn_weight"]),
        "g0c": np.ascontiguousarray(
            np.asarray(inputs_by_name["L0_ffn_gamma"]).reshape(DC, 128).T),
        "be0c": np.ascontiguousarray(
            np.asarray(inputs_by_name["L0_ffn_beta"]).reshape(DC, 128).T),
        "w1": np.ascontiguousarray(inputs_by_name["L1_attn_qkv_w"]),
        "b1c": np.ascontiguousarray(
            np.asarray(inputs_by_name["L1_attn_qkv_b"]).reshape(24, 128).T),
        "b1r": np.asarray(inputs_by_name["L1_attn_qkv_b"]).reshape(1, 3 * D),
        "pw1": np.ascontiguousarray(inputs_by_name["L1_attn_proj_w"]),
        "pb1r": np.asarray(inputs_by_name["L1_attn_proj_b"]).reshape(1, D),
        "fcw1": np.ascontiguousarray(inputs_by_name["L1_ffn_W_fc"]),
        "lnwb2": np.ascontiguousarray(
            np.stack([np.asarray(inputs_by_name["L1_ffn_ln_w"]),
                      np.asarray(inputs_by_name["L1_ffn_ln_b"])])),
        "pw2": np.ascontiguousarray(inputs_by_name["L1_ffn_W_proj"]),
        "cident": np.eye(128, dtype=np.float32),
        "conerow": np.ones((1, 128), dtype=np.float32),
        "cvals": np.broadcast_to(np.array(
            [127.0, EPS_LN, -63.5, 1.0, 0.0, 0.0, 0.0, 0.0],
            dtype=np.float32), (128, 8)).copy(),
    }
    in_maps = []
    for c in range(NCORE):
        im = dict(shared)
        im["xin"] = np.ascontiguousarray(x[2 * c:2 * c + 2].reshape(NSEQ * S, D))
        in_maps.append(im)
    return in_maps


def kernel(L0_attn_x, L0_attn_W, L0_attn_b, L0_ffn_weight, L0_ffn_gamma,
           L0_ffn_beta, L1_attn_qkv_w, L1_attn_qkv_b, L1_attn_proj_w,
           L1_attn_proj_b, L1_ffn_W_fc, L1_ffn_ln_w, L1_ffn_ln_b,
           L1_ffn_W_proj):
    kw = dict(L0_attn_x=L0_attn_x, L0_attn_W=L0_attn_W, L0_attn_b=L0_attn_b,
              L0_ffn_weight=L0_ffn_weight, L0_ffn_gamma=L0_ffn_gamma,
              L0_ffn_beta=L0_ffn_beta, L1_attn_qkv_w=L1_attn_qkv_w,
              L1_attn_qkv_b=L1_attn_qkv_b, L1_attn_proj_w=L1_attn_proj_w,
              L1_attn_proj_b=L1_attn_proj_b, L1_ffn_W_fc=L1_ffn_W_fc,
              L1_ffn_ln_w=L1_ffn_ln_w, L1_ffn_ln_b=L1_ffn_ln_b,
              L1_ffn_W_proj=L1_ffn_W_proj)
    nc = _get_program()
    in_maps = make_in_maps(kw)
    res = run_bass_kernel_spmd(nc, in_maps, core_ids=list(range(NCORE)))
    out = np.concatenate(
        [np.asarray(r["y"]).reshape(NSEQ, S, D) for r in res.results], axis=0)
    return out.astype(np.float32)


# version 03
# speedup vs baseline: 1.1979x; 1.1979x over previous
import contextlib
import numpy as np

import concourse.bass as bass
import concourse.mybir as mybir
from concourse.bass_utils import run_bass_kernel_spmd

F32 = mybir.dt.float32
BF16 = mybir.dt.bfloat16
F16 = mybir.dt.float16
ALU = mybir.AluOpType
ACT = mybir.ActivationFunctionType
AXX = mybir.AxisListType.X

D = 1024
S = 2048
NSEQ = 2
NCORE = 8
DC = D // 128
TBLK = 512
NTB = S // TBLK
NTILE = TBLK // 128
H = 16
HD = 64
DFF = 4096
EPS_LN = 1e-5
ALL_PHASES = ("P", "A", "B", "C", "D")
A_STOP = [4]
A1_NTB = [NTB]
A1_NFC = [16]
LN_PARTS = [3]
DBG_EPARK = [False]
A_SEQS = [2]
C_STOP = [99]


class Eng:
    def __init__(self, obj, sem, name):
        self.o = obj
        self.sem = sem
        self.name = name
        self.n = 0
        self._seen = {}

    def mark(self, instr, k=1):
        assert instr is not None, "bare sem_inc fires before datapath drain"
        instr.then_inc(self.sem, k)
        self.n += k
        return self.n

    def wait(self, other, val=None):
        if val is None:
            val = other.n
        if val <= 0:
            return
        key = other.name
        if self._seen.get(key, -1) >= val:
            return
        self.o.wait_ge(other.sem, val)
        self._seen[key] = val


class DmaStream:

    def __init__(self, eng, sem, name):
        self.eng = eng
        self.sem = sem
        self.name = name
        self.n = 0

    def put(self, out, in_):
        self.eng.o.dma_start(out=out, in_=in_).then_inc(self.sem, 16)
        self.n += 16
        return self.n


def wait_dma(eng, st, val=None):
    if val is None:
        val = st.n
    if val <= 0:
        return
    key = "dma_" + st.name
    if eng._seen.get(key, -1) >= val:
        return
    eng.o.wait_ge(st.sem, val)
    eng._seen[key] = val


def build_program(phases=ALL_PHASES):
    phases = set(phases)
    nc = bass.Bass("TRN2", target_bir_lowering=False,
                   detect_race_conditions=False)

    xin = nc.dram_tensor("xin", [NSEQ * S, D], F32, kind="ExternalInput").ap()
    w0 = nc.dram_tensor("w0", [3 * D, D], F32, kind="ExternalInput").ap()
    b0c = nc.dram_tensor("b0c", [128, 24], F32, kind="ExternalInput").ap()
    b0r = nc.dram_tensor("b0r", [1, 3 * D], F32, kind="ExternalInput").ap()
    fw0 = nc.dram_tensor("fw0", [D, D], F32, kind="ExternalInput").ap()
    g0c = nc.dram_tensor("g0c", [128, DC], F32, kind="ExternalInput").ap()
    be0c = nc.dram_tensor("be0c", [128, DC], F32, kind="ExternalInput").ap()
    w1 = nc.dram_tensor("w1", [3 * D, D], F32, kind="ExternalInput").ap()
    b1c = nc.dram_tensor("b1c", [128, 24], F32, kind="ExternalInput").ap()
    b1r = nc.dram_tensor("b1r", [1, 3 * D], F32, kind="ExternalInput").ap()
    pw1 = nc.dram_tensor("pw1", [D, D], F32, kind="ExternalInput").ap()
    pb1r = nc.dram_tensor("pb1r", [1, D], F32, kind="ExternalInput").ap()
    fcw1 = nc.dram_tensor("fcw1", [DFF, D], F32, kind="ExternalInput").ap()
    lnwb2 = nc.dram_tensor("lnwb2", [2, DFF], F32, kind="ExternalInput").ap()
    pw2 = nc.dram_tensor("pw2", [D, DFF], F32, kind="ExternalInput").ap()
    cident = nc.dram_tensor("cident", [128, 128], F32, kind="ExternalInput").ap()
    conerow = nc.dram_tensor("conerow", [1, 128], F32, kind="ExternalInput").ap()
    cvals = nc.dram_tensor("cvals", [128, 8], F32, kind="ExternalInput").ap()
    yout = nc.dram_tensor("y", [NSEQ * S, D], F32, kind="ExternalOutput").ap()

    xpark = nc.dram_tensor("xpark", [NSEQ * S, D], F32,
                           kind="ExternalOutput" if DBG_EPARK[0] else "Internal").ap()
    epark = nc.dram_tensor("epark", [S, S], F32,
                           kind="ExternalOutput" if DBG_EPARK[0] else "Internal").ap()
    w0T = nc.dram_tensor("w0T", [D, 3 * D], F32).ap()
    w1T = nc.dram_tensor("w1T", [D, 3 * D], F32).ap()
    wp1T = nc.dram_tensor("wp1T", [D, D], F32).ap()
    t0T = nc.dram_tensor("t0T", [D, D], BF16).ap()
    tfcT = nc.dram_tensor("tfcT", [D, DFF], F16).ap()
    tp2T = nc.dram_tensor("tp2T", [DFF, D], F16).ap()
    ccin = nc.dram_tensor("ccin", [1, 128], F32)
    ccout = nc.dram_tensor("ccout", [1, 128], F32)

    es = contextlib.ExitStack()
    with es:
        semT = es.enter_context(nc.semaphore("semT"))
        semV = es.enter_context(nc.semaphore("semV"))
        semS = es.enter_context(nc.semaphore("semS"))
        semG = es.enter_context(nc.semaphore("semG"))
        semY = es.enter_context(nc.semaphore("semY"))
        semLY = es.enter_context(nc.semaphore("semLY"))
        semLW = es.enter_context(nc.semaphore("semLW"))
        semSG = es.enter_context(nc.semaphore("semSG"))
        semLG = es.enter_context(nc.semaphore("semLG"))
        ccsem = es.enter_context(nc.semaphore("ccsem"))

        es.enter_context(nc.Block())

        T = Eng(nc.tensor, semT, "T")
        V = Eng(nc.vector, semV, "V")
        SC = Eng(nc.scalar, semS, "S")
        G = Eng(nc.gpsimd, semG, "G")
        Y = Eng(nc.sync, semY, "Y")
        ALL_E = [T, V, SC, G, Y]

        LY = DmaStream(Y, semLY, "LY")
        LW = DmaStream(Y, semLW, "LW")
        SG = DmaStream(G, semSG, "SG")
        LG = DmaStream(G, semLG, "LG")

        def barrier():
            for e in ALL_E:
                for o in ALL_E:
                    if o is not e:
                        e.wait(o)
                for st_ in (LY, LW, SG, LG):
                    wait_dma(e, st_)

        ident = es.enter_context(nc.sbuf_tensor("ident", [128, 128], F32))
        onerow = es.enter_context(nc.sbuf_tensor("onerow", [1, 128], F32))
        dump = es.enter_context(nc.sbuf_tensor("dump", [128, 1024], F32))
        stats = es.enter_context(nc.sbuf_tensor("stats", [128, 64], F32))
        srow = es.enter_context(nc.sbuf_tensor("srow", [1, 64], F32))
        b0sb = es.enter_context(nc.sbuf_tensor("b0sb", [128, 24], F32))
        b1sb = es.enter_context(nc.sbuf_tensor("b1sb", [128, 24], F32))
        g0sb = es.enter_context(nc.sbuf_tensor("g0sb", [128, DC], F32))
        be0sb = es.enter_context(nc.sbuf_tensor("be0sb", [128, DC], F32))
        scv = es.enter_context(nc.sbuf_tensor("scv", [128, 8], F32))
        qscal = es.enter_context(nc.sbuf_tensor("qscal", [128, 8], F32))

        LY.put(ident[:, :], cident[:, :])
        LY.put(onerow[:1, :], conerow[:1, :])
        LY.put(b0sb[:, :], b0c[:, :])
        LY.put(b1sb[:, :], b1c[:, :])
        LY.put(g0sb[:, :], g0c[:, :])
        LY.put(be0sb[:, :], be0c[:, :])
        LY.put(scv[:, :], cvals[:, :])
        nc.const_aps.register(F32, EPS_LN, scv[:, 1:2])
        nc.const_aps.register(F32, -63.5, scv[:, 2:3])

        def col_to_scalar(col_ap, out_ap, op, ps):
            T.wait(V)
            tr = T.o.transpose(ps[:1, :128], col_ap, ident[:, :])
            T.mark(tr)
            V.wait(T)
            rd = V.o.tensor_reduce(out_ap, ps[:1, :128], axis=AXX, op=op)
            V.mark(rd)

        def bcast_scalar(src_ap, dst_ap, ps):
            w = src_ap.shape[-1]
            T.wait(V)
            mm = T.o.matmul(ps[:, :w], onerow[:1, :], src_ap,
                            start=True, stop=True)
            T.mark(mm)
            V.wait(T)
            cp = V.o.tensor_copy(dst_ap, ps[:, :w])
            V.mark(cp)

        if "P" in phases:
          with contextlib.ExitStack() as ps_:
            win = ps_.enter_context(nc.sbuf_tensor("p_win", [128, 4096], F32))
            wot = ps_.enter_context(nc.sbuf_tensor("p_wot", [128, 8192], F32))
            pst = ps_.enter_context(nc.psum_tensor("p_pst", [128, 512], F32))
            pst2 = ps_.enter_context(nc.psum_tensor("p_pst2", [128, 512], F32))

            def transpose_rows(src, src_rows, src_cols, dstT):
                nrc = src_rows // 128
                ncc = src_cols // 128
                for rc in range(nrc):
                    Y.wait(T)
                    m = LY.put(win[:, :src_cols], src[rc * 128:(rc + 1) * 128, :])
                    wait_dma(T, LY, m)
                    for cc in range(ncc):
                        pb = pst if cc % 2 == 0 else pst2
                        T.wait(V)
                        tr = T.o.transpose(pb[:, :128],
                                           win[:, cc * 128:(cc + 1) * 128],
                                           ident[:, :])
                        T.mark(tr)
                        V.wait(T)
                        wait_dma(V, SG)
                        cp = V.o.tensor_copy(wot[:, cc * 128:(cc + 1) * 128],
                                             pb[:, :128])
                        V.mark(cp)
                        G.wait(V)
                        SG.put(dstT[cc * 128:(cc + 1) * 128,
                                    rc * 128:(rc + 1) * 128],
                               wot[:, cc * 128:(cc + 1) * 128])

            transpose_rows(w0, 3 * D, D, w0T)
            transpose_rows(w1, 3 * D, D, w1T)
            transpose_rows(pw1, D, D, wp1T)

            def gsum_abs(src, rows, cols, out_col):
                nrc = rows // 128
                for rc in range(nrc):
                    Y.wait(V)
                    Y.wait(SC)
                    m = LY.put(win[:, :cols], src[rc * 128:(rc + 1) * 128, :])
                    wait_dma(V, LY, m)
                    V.o.tensor_reduce(stats[:, 62:63], win[:, :cols], axis=AXX,
                                      op=ALU.add, apply_absolute_value=True)
                    V.o.drain()
                    if rc == 0:
                        cp = V.o.tensor_copy(stats[:, out_col:out_col + 1],
                                             stats[:, 62:63])
                    else:
                        cp = V.o.tensor_tensor(stats[:, out_col:out_col + 1],
                                               stats[:, out_col:out_col + 1],
                                               stats[:, 62:63], ALU.add)
                    V.mark(cp)
                    V.o.drain()
                col_to_scalar(stats[:, out_col:out_col + 1],
                              srow[:1, out_col:out_col + 1], ALU.add, pst)

            gsum_abs(fw0, D, D, 0)
            V.o.drain()
            dl_ = V.o.tensor_scalar(srow[:1, 1:2], srow[:1, 0:1],
                                    0.7 / (2.0 ** 20), None, ALU.mult)
            V.mark(dl_)
            bcast_scalar(srow[:1, 1:2], qscal[:, 2:3], pst)
            for rc in range(D // 128):
                Y.wait(V)
                Y.wait(SC)
                m = LY.put(win[:, :D], fw0[rc * 128:(rc + 1) * 128, :])
                wait_dma(SC, LY, m)
                SC.wait(V)
                ab = SC.o.activation(wot[:, :D], win[:, :D], ACT.Abs)
                SC.mark(ab)
                wait_dma(V, LY, m)
                V.wait(SC)
                V.o.tensor_scalar(wot[:, 4096:4096 + D], wot[:, :D],
                                  qscal[:, 2:3], None, ALU.is_gt)
                V.o.drain()
                V.o.tensor_reduce(stats[:, 62:63], wot[:, 4096:4096 + D],
                                  axis=AXX, op=ALU.add)
                V.o.drain()
                if rc == 0:
                    V.o.tensor_copy(stats[:, 3:4], stats[:, 62:63])
                else:
                    V.o.tensor_tensor(stats[:, 3:4], stats[:, 3:4],
                                      stats[:, 62:63], ALU.add)
                V.o.tensor_tensor(wot[:, :D], wot[:, :D], wot[:, 4096:4096 + D],
                                  ALU.mult)
                V.o.drain()
                V.o.tensor_reduce(stats[:, 62:63], wot[:, :D], axis=AXX,
                                  op=ALU.add)
                V.o.drain()
                if rc == 0:
                    cp = V.o.tensor_copy(stats[:, 4:5], stats[:, 62:63])
                else:
                    cp = V.o.tensor_tensor(stats[:, 4:5], stats[:, 4:5],
                                           stats[:, 62:63], ALU.add)
                V.mark(cp)
                V.o.drain()
            col_to_scalar(stats[:, 3:4], srow[:1, 3:4], ALU.add, pst)
            col_to_scalar(stats[:, 4:5], srow[:1, 4:5], ALU.add, pst2)
            V.o.tensor_scalar(srow[:1, 5:6], srow[:1, 3:4], 1.0, None, ALU.max)
            V.o.drain()
            V.o.reciprocal(srow[:1, 5:6], srow[:1, 5:6])
            V.o.drain()
            al = V.o.tensor_tensor(srow[:1, 6:7], srow[:1, 4:5], srow[:1, 5:6],
                                   ALU.mult)
            V.mark(al)
            bcast_scalar(srow[:1, 6:7], qscal[:, 3:4], pst)

            def quantize_transposed(src, rows, cols, dstT, thr_col, scale_col):
                for rc in range(rows // 128):
                    Y.wait(T)
                    m = LY.put(win[:, :cols], src[rc * 128:(rc + 1) * 128, :])
                    wait_dma(T, LY, m)
                    for cc in range(cols // 128):
                        pb = pst if cc % 2 == 0 else pst2
                        T.wait(V)
                        T.wait(SC)
                        tr = T.o.transpose(pb[:, :128],
                                           win[:, cc * 128:(cc + 1) * 128],
                                           ident[:, :])
                        T.mark(tr)
                        SC.wait(T)
                        wait_dma(SC, SG)
                        sg = SC.o.activation(
                            wot[:, 4096 + cc * 128:4096 + (cc + 1) * 128],
                            pb[:, :128], ACT.Sign)
                        ab = SC.o.activation(wot[:, cc * 128:(cc + 1) * 128],
                                             pb[:, :128], ACT.Abs)
                        SC.mark(ab)
                        V.wait(SC)
                        if scale_col is not None:
                            V.o.tensor_scalar(wot[:, cc * 128:(cc + 1) * 128],
                                              wot[:, cc * 128:(cc + 1) * 128],
                                              qscal[:, thr_col:thr_col + 1],
                                              qscal[:, scale_col:scale_col + 1],
                                              ALU.is_gt, ALU.mult)
                        else:
                            V.o.tensor_scalar(wot[:, cc * 128:(cc + 1) * 128],
                                              wot[:, cc * 128:(cc + 1) * 128],
                                              qscal[:, thr_col:thr_col + 1],
                                              None, ALU.is_gt)
                        V.o.drain()
                        cp = V.o.tensor_tensor(
                            wot[:, cc * 128:(cc + 1) * 128],
                            wot[:, cc * 128:(cc + 1) * 128],
                            wot[:, 4096 + cc * 128:4096 + (cc + 1) * 128],
                            ALU.mult)
                        V.mark(cp)
                        G.wait(V)
                        SG.put(dstT[cc * 128:(cc + 1) * 128,
                                    rc * 128:(rc + 1) * 128],
                               wot[:, cc * 128:(cc + 1) * 128])

            quantize_transposed(fw0, D, D, t0T, 2, None)

            def absmean_ternary(src, rows, cols, dstT, statcol):
                gsum_abs(src, rows, cols, statcol)
                cnt = float(rows * cols)
                V.o.drain()
                V.o.tensor_scalar(srow[:1, statcol + 1:statcol + 2],
                                  srow[:1, statcol:statcol + 1],
                                  1.0 / cnt, 1e-6, ALU.mult, ALU.add)
                V.o.drain()
                thr = V.o.tensor_scalar(srow[:1, statcol + 1:statcol + 2],
                                        srow[:1, statcol + 1:statcol + 2],
                                        0.5, None, ALU.mult)
                V.mark(thr)
                bcast_scalar(srow[:1, statcol + 1:statcol + 2],
                             qscal[:, 4:5], pst)
                quantize_transposed(src, rows, cols, dstT, 4, None)

            absmean_ternary(fcw1, DFF, D, tfcT, 8)
            absmean_ternary(pw2, D, DFF, tp2T, 10)
          barrier()

        def load_ln_transpose(src, s, tb, xblk, xnT, psA, psB, ldstream,
                              gamma_col=None, beta_col=None):
            t0 = s * S + tb * TBLK
            ldstream.eng.wait(V)
            ldstream.eng.wait(SC)
            ldstream.eng.wait(T)
            m = ldstream.put(xblk[:, :].rearrange("p (a d) -> p a d", d=D),
                             src[t0:t0 + TBLK, :]
                             .rearrange("(a p) d -> p a d", p=128))
            wait_dma(V, ldstream, m)
            wait_dma(SC, ldstream, m)
            SC.wait(T)
            SC.wait(V)
            for a in range(NTILE):
                c = xblk[:, a * D:(a + 1) * D]
                rd_ = V.o.tensor_reduce(stats[:, 16 + a:17 + a], c, axis=AXX,
                                        op=ALU.add)
                sq_ = SC.o.activation(dump[:, :1024], c, ACT.Square,
                                      accum_out=stats[:, 20 + a:21 + a])
            V.mark(rd_)
            SC.mark(sq_)
            V.wait(SC)
            V.o.drain()
            V.o.tensor_scalar(stats[:, 16:20], stats[:, 16:20], 1.0 / D,
                              None, ALU.mult)
            V.o.tensor_scalar(stats[:, 20:24], stats[:, 20:24], 1.0 / D,
                              None, ALU.mult)
            V.o.drain()
            V.o.tensor_tensor(stats[:, 24:28], stats[:, 16:20],
                              stats[:, 16:20], ALU.mult)
            V.o.drain()
            vv = V.o.tensor_tensor(stats[:, 20:24], stats[:, 20:24],
                                   stats[:, 24:28], ALU.subtract)
            V.mark(vv)
            SC.wait(V)
            sq = SC.o.activation(stats[:, 24:28], stats[:, 20:24], ACT.Sqrt,
                                 bias=EPS_LN)
            SC.mark(sq)
            V.wait(SC)
            rr = V.o.reciprocal(stats[:, 24:28], stats[:, 24:28])
            V.mark(rr)
            if LN_PARTS[0] < 2:
                return
            for a in range(NTILE):
                src_c = xblk[:, a * D:(a + 1) * D]
                V.wait(T)
                nrm = V.o.tensor_scalar(dump[:, :1024], src_c,
                                        stats[:, 16 + a:17 + a],
                                        stats[:, 24 + a:25 + a],
                                        ALU.subtract, ALU.mult)
                V.mark(nrm)
                if LN_PARTS[0] < 3:
                    continue
                for c in range(DC):
                    pb = psA if c % 2 == 0 else psB
                    T.wait(V)
                    tr = T.o.transpose(pb[:, :128], dump[:, c * 128:(c + 1) * 128],
                                       ident[:, :])
                    T.mark(tr)
                    V.wait(T)
                    cp = V.o.tensor_copy(
                        xnT[:, c * TBLK + a * 128:c * TBLK + (a + 1) * 128],
                        pb[:, :128])
                    V.mark(cp)
            if gamma_col is not None:
                for c in range(DC):
                    sc_ = V.o.tensor_scalar(xnT[:, c * TBLK:(c + 1) * TBLK],
                                            xnT[:, c * TBLK:(c + 1) * TBLK],
                                            gamma_col[:, c:c + 1],
                                            beta_col[:, c:c + 1],
                                            ALU.mult, ALU.add)
                    V.mark(sc_)

        def phase_A(s):
            pfx = f"a{s}_"
            with contextlib.ExitStack() as ph:
                KT = ph.enter_context(nc.sbuf_tensor(pfx + "KT", [128, DC * S], F32))
                QT = ph.enter_context(nc.sbuf_tensor(pfx + "QT", [128, DC * S], F32))
                xblk = ph.enter_context(nc.sbuf_tensor(pfx + "xblk", [128, NTILE * D], F32))
                xnT = ph.enter_context(nc.sbuf_tensor(pfx + "xnT", [128, DC * TBLK], F32))
                wrow = ph.enter_context(nc.sbuf_tensor(pfx + "wrow", [128, 2 * 1024], F32))
                brow = ph.enter_context(nc.sbuf_tensor(pfx + "brow", [1, 3 * D], F32))
                zall = ph.enter_context(nc.sbuf_tensor(pfx + "zall", [128, 16], F32))

                ps1 = contextlib.ExitStack()
                psA = ps1.enter_context(nc.psum_tensor(pfx + "psA", [128, 512], F32))
                psB = ps1.enter_context(nc.psum_tensor(pfx + "psB", [128, 512], F32))
                tmark_half = [0, 0]
                for tb in range(A1_NTB[0]):
                    load_ln_transpose(xin, s, tb, xblk, xnT, psA, psB, LY)
                    for fc in range(A1_NFC[0]):
                        dst = QT if fc < 8 else KT
                        freg = fc % 8
                        half = fc % 2
                        pb = psA if half == 0 else psB
                        Y.wait(T, tmark_half[half])
                        m = LW.put(wrow[:, half * 1024:half * 1024 + 128 * DC]
                                   .rearrange("p (c w) -> p c w", w=128),
                                   w0T[:, fc * 128:(fc + 1) * 128]
                                   .rearrange("(c p) w -> p c w", p=128))
                        wait_dma(T, LW, m)
                        T.wait(V)
                        for c in range(DC):
                            mm_ = T.o.matmul(pb[:, :512],
                                             wrow[:, half * 1024 + c * 128:
                                                  half * 1024 + (c + 1) * 128],
                                             xnT[:, c * TBLK:(c + 1) * TBLK],
                                             start=(c == 0), stop=(c == DC - 1))
                        tmark_half[half] = T.mark(mm_)
                        V.wait(T)
                        ev = V.o.tensor_scalar(
                            dst[:, freg * S + tb * TBLK:freg * S + (tb + 1) * TBLK],
                            pb[:, :512], b0sb[:, fc:fc + 1], None, ALU.add)
                        V.mark(ev)
                barrier()
                ps1.close()
                if A_STOP[0] < 2:
                    return

                with contextlib.ExitStack() as p2:
                    psS1 = p2.enter_context(nc.psum_tensor(pfx + "psS1", [128, S], F32))
                    psS2 = p2.enter_context(nc.psum_tensor(pfx + "psS2", [128, S], F32))
                    est = p2.enter_context(nc.sbuf_tensor(pfx + "est", [128, 2 * S], F32))
                    for jt in range(16):
                        pb = psS1 if jt % 2 == 0 else psS2
                        half = jt % 2
                        T.wait(SC)
                        T.wait(V)
                        for ig in range(4):
                            for c in range(DC):
                                mm_ = T.o.matmul(pb[:, ig * 512:(ig + 1) * 512],
                                                 KT[:, c * S + jt * 128:c * S + (jt + 1) * 128],
                                                 QT[:, c * S + ig * 512:c * S + (ig + 1) * 512],
                                                 start=(c == 0), stop=(c == DC - 1))
                        T.mark(mm_)
                        V.wait(T)
                        V.o.tensor_reduce(stats[:, 40:41], pb[:, :], axis=AXX,
                                          op=ALU.max)
                        V.o.drain()
                        nm = V.o.tensor_scalar(stats[:, 41:42], stats[:, 40:41],
                                               -1.0 / 1024.0, None, ALU.mult)
                        V.mark(nm)
                        SC.wait(V)
                        wait_dma(SC, SG)
                        ex = SC.o.activation(est[:, half * S:(half + 1) * S],
                                             pb[:, :], ACT.Exp,
                                             bias=stats[:, 41:42],
                                             scale=1.0 / 1024.0,
                                             accum_out=zall[:, jt:jt + 1])
                        SC.mark(ex)
                        G.wait(SC)
                        SG.put(epark[jt * 128:(jt + 1) * 128, :],
                               est[:, half * S:(half + 1) * S])
                    V.wait(SC)
                    rz = V.o.reciprocal(zall[:, :], zall[:, :])
                    V.mark(rz)
                barrier()
                if A_STOP[0] < 3:
                    return

                Vb = QT
                wvres = KT
                with contextlib.ExitStack() as p3:
                    psA = p3.enter_context(nc.psum_tensor(pfx + "psA2", [128, 512], F32))
                    psB = p3.enter_context(nc.psum_tensor(pfx + "psB2", [128, 512], F32))
                    m = LW.put(wvres[:, :DC * 1024]
                               .rearrange("p (c w) -> p c w", w=1024),
                               w0T[:, 2048:3072]
                               .rearrange("(c p) w -> p c w", p=128))
                    mrow = LY.put(brow[:1, :], b0r[:1, :])
                    for tb in range(NTB):
                        load_ln_transpose(xin, s, tb, xblk, xnT, psA, psB, LY)
                        wait_dma(T, LW, m)
                        wait_dma(T, LY, mrow)
                        for a in range(NTILE):
                            jt = tb * NTILE + a
                            for kb in range(2):
                                pb = psA if kb == 0 else psB
                                T.wait(V)
                                for c in range(DC):
                                    T.o.matmul(pb[:, :512],
                                               xnT[:, c * TBLK + a * 128:
                                                   c * TBLK + (a + 1) * 128],
                                               wvres[:, c * 1024 + kb * 512:
                                                     c * 1024 + (kb + 1) * 512],
                                               start=(c == 0), stop=False)
                                mm_ = T.o.matmul(pb[:, :512], onerow[:1, :],
                                                 brow[:1, 2048 + kb * 512:
                                                      2048 + (kb + 1) * 512],
                                                 start=False, stop=True)
                                T.mark(mm_)
                                V.wait(T)
                                ev = V.o.tensor_scalar(
                                    Vb[:, jt * 1024 + kb * 512:
                                       jt * 1024 + (kb + 1) * 512],
                                    pb[:, :512], zall[:, jt:jt + 1],
                                    None, ALU.mult)
                                V.mark(ev)
                    barrier()
                    if A_STOP[0] < 4:
                        return

                    EC = KT
                    OB = 16 * 512
                    for ib in range(4):
                        t0 = s * S + ib * TBLK
                        Y.wait(T)
                        for jt in range(16):
                            LY.put(EC[:, jt * 512:(jt + 1) * 512],
                                   epark[jt * 128:(jt + 1) * 128,
                                         ib * TBLK:(ib + 1) * TBLK])
                        G.wait(V)
                        mx_ = LG.put(xblk[:, :].rearrange("p (a d) -> p a d", d=D),
                                     xin[t0:t0 + TBLK, :]
                                     .rearrange("(a p) d -> p a d", p=128))
                        wait_dma(T, LY)
                        wait_dma(V, SG)
                        for u in range(4):
                            for kb in range(2):
                                pb = psA if (u * 2 + kb) % 2 == 0 else psB
                                T.wait(V)
                                for jt in range(16):
                                    mm_ = T.o.matmul(pb[:, :512],
                                                     EC[:, jt * 512 + u * 128:
                                                        jt * 512 + u * 128 + 128],
                                                     Vb[:, jt * 1024 + kb * 512:
                                                        jt * 1024 + (kb + 1) * 512],
                                                     start=(jt == 0), stop=(jt == 15))
                                T.mark(mm_)
                                V.wait(T)
                                wait_dma(V, LG, mx_)
                                ev = V.o.tensor_tensor(
                                    KT[:, OB + u * D + kb * 512:
                                       OB + u * D + (kb + 1) * 512],
                                    pb[:, :512],
                                    xblk[:, u * D + kb * 512:u * D + (kb + 1) * 512],
                                    ALU.add)
                                V.mark(ev)
                        G.wait(V)
                        SG.put(xpark[t0:t0 + TBLK, :]
                               .rearrange("(a p) d -> p a d", p=128),
                               KT[:, OB:OB + NTILE * D]
                               .rearrange("p (a d) -> p a d", d=D))
                    barrier()

        if "A" in phases:
            phase_A(0)
            if A_SEQS[0] > 1:
                phase_A(1)
        else:
            with nc.sbuf_tensor("cp_buf", [128, NTILE * D], F32) as sb:
                for s in range(NSEQ):
                    for tb in range(NTB):
                        t0 = s * S + tb * TBLK
                        wait_dma(Y, SG)
                        m = LY.put(sb[:, :].rearrange("p (a d) -> p a d", d=D),
                                   xin[t0:t0 + TBLK, :]
                                   .rearrange("(a p) d -> p a d", p=128))
                        wait_dma(G, LY, m)
                        SG.put(xpark[t0:t0 + TBLK, :]
                               .rearrange("(a p) d -> p a d", p=128),
                               sb[:, :].rearrange("p (a d) -> p a d", d=D))
            barrier()


        def phase_B():
            with contextlib.ExitStack() as ph:
                xblk = ph.enter_context(nc.sbuf_tensor("b_xblk", [128, NTILE * D], F32))
                xnT = ph.enter_context(nc.sbuf_tensor("b_xnT", [128, DC * TBLK], F32))
                zb = ph.enter_context(nc.sbuf_tensor("b_zb", [128, DC * TBLK], F32))
                abuf = ph.enter_context(nc.sbuf_tensor("b_abuf", [128, DC * TBLK], F32))
                sgn = ph.enter_context(nc.sbuf_tensor("b_sgn", [128, DC * TBLK], F32))
                ternsb = ph.enter_context(nc.sbuf_tensor("b_tern", [128, DC * 1024], BF16))
                nb = ph.enter_context(nc.sbuf_tensor("b_nb", [128, DC * TBLK], BF16))
                satl = ph.enter_context(nc.sbuf_tensor("b_satl", [128, 1], F32))
                outb = ph.enter_context(nc.sbuf_tensor("b_outb", [128, NTILE * D], F32))
                ccsb = ph.enter_context(nc.sbuf_tensor("b_ccsb", [1, 128], F32))
                sgst = ph.enter_context(nc.sbuf_tensor("b_sgst", [128, DC * TBLK], F32))
                gpbc = ph.enter_context(nc.sbuf_tensor("b_gpbc", [128, 16], F32))
                psA = ph.enter_context(nc.psum_tensor("b_psA", [128, 512], F32))
                psB = ph.enter_context(nc.psum_tensor("b_psB", [128, 512], F32))

                first = [True]
                for s in range(NSEQ):
                    for tb in range(NTB):
                        load_ln_transpose(xpark, s, tb, xblk, xnT, psA, psB, LY,
                                          gamma_col=g0sb, beta_col=be0sb)
                        V.o.drain()
                        V.o.tensor_reduce(stats[:, 48:49], xnT[:, :], axis=AXX,
                                          op=ALU.max, apply_absolute_value=True)
                        V.o.drain()
                        if first[0]:
                            mxcp = V.o.tensor_copy(stats[:, 49:50], stats[:, 48:49])
                            first[0] = False
                        else:
                            mxcp = V.o.tensor_tensor(stats[:, 49:50], stats[:, 49:50],
                                                     stats[:, 48:49], ALU.max)
                        V.mark(mxcp)
                        V.o.drain()
                col_to_scalar(stats[:, 49:50], srow[:1, 12:13], ALU.max, psA)
                V.o.drain()

                V.o.memset(ccsb[:1, :], 0.0)
                V.o.drain()
                c0 = V.o.tensor_copy(ccsb[:1, 0:1], srow[:1, 12:13])
                V.mark(c0)
                G.wait(V)
                SG.put(ccin.ap()[:1, :], ccsb[:1, :])
                wait_dma(G, SG)
                cc = G.o.collective_compute(
                    "AllReduce", ALU.max, replica_groups=[list(range(NCORE))],
                    ins=[ccin.ap().opt()], outs=[ccout.ap().opt()])
                G.mark(cc)
                G.wait(G)
                LG.put(ccsb[:1, :], ccout.ap()[:1, :])
                wait_dma(V, LG)
                V.o.tensor_reduce(srow[:1, 13:14], ccsb[:1, :], axis=AXX, op=ALU.max)
                V.o.drain()
                gm = V.o.tensor_scalar(srow[:1, 14:15], srow[:1, 13:14], 1e-8,
                                       None, ALU.max)
                V.mark(gm)
                bcast_scalar(srow[:1, 14:15], qscal[:, 5:6], psA)
                V.o.reciprocal(qscal[:, 6:7], scv[:, 0:1])
                V.o.drain()
                V.o.tensor_tensor(qscal[:, 0:1], qscal[:, 5:6], qscal[:, 6:7],
                                  ALU.mult)
                V.o.drain()
                V.o.reciprocal(qscal[:, 1:2], qscal[:, 0:1])
                V.o.drain()
                V.o.tensor_tensor(satl[:, 0:1], qscal[:, 0:1], qscal[:, 3:4],
                                  ALU.mult)
                V.o.drain()
                V.o.tensor_scalar(gpbc[:, 0:8], g0sb[:, :], qscal[:, 1:2],
                                  None, ALU.mult)
                gb = V.o.tensor_scalar(gpbc[:, 8:16], be0sb[:, :], qscal[:, 1:2],
                                       None, ALU.mult)
                V.mark(gb)
                V.o.drain()

                mt = LW.put(ternsb[:, :].rearrange("p (c w) -> p c w", w=1024),
                            t0T[:, :].rearrange("(c p) w -> p c w", p=128))

                for s in range(NSEQ):
                    for tb in range(NTB):
                        t0 = s * S + tb * TBLK
                        wait_dma(V, SG)
                        load_ln_transpose(xpark, s, tb, xblk, xnT, psA, psB, LY,
                                          gamma_col=gpbc[:, 0:8],
                                          beta_col=gpbc[:, 8:16])
                        SC.wait(V)
                        ab_ = SC.o.activation(abuf[:, :], xnT[:, :], ACT.Abs)
                        sg_ = SC.o.activation(sgn[:, :], xnT[:, :], ACT.Sign)
                        SC.mark(sg_)
                        V.wait(SC)
                        cz = V.o.tensor_copy(zb[:, :], abuf[:, :])
                        V.mark(cz)
                        for k in (64, 32, 16, 8, 4, 2, 1):
                            SC.wait(V)
                            sk = SC.o.activation(sgst[:, :], zb[:, :],
                                                 ACT.Sign, bias=-63.5)
                            SC.mark(sk)
                            V.wait(SC)
                            st1 = V.o.scalar_tensor_tensor(
                                zb[:, :], sgst[:, :], -k / 2.0,
                                zb[:, :], ALU.mult, ALU.add)
                            V.mark(st1)
                        V.o.drain()
                        V.o.scalar_tensor_tensor(abuf[:, :], abuf[:, :], 63.5,
                                                 zb[:, :], ALU.add, ALU.subtract)
                        V.o.drain()
                        nf = V.o.tensor_tensor(nb[:, :], abuf[:, :], sgn[:, :],
                                               ALU.mult)
                        V.mark(nf)
                        wait_dma(T, LW, mt)
                        for a in range(NTILE):
                            for kb in range(2):
                                pb = psA if kb == 0 else psB
                                T.wait(V)
                                for c in range(DC):
                                    mm_ = T.o.matmul(
                                        pb[:, :512],
                                        nb[:, c * TBLK + a * 128:
                                           c * TBLK + (a + 1) * 128],
                                        ternsb[:, c * 1024 + kb * 512:
                                               c * 1024 + (kb + 1) * 512],
                                        start=(c == 0), stop=(c == DC - 1))
                                T.mark(mm_)
                                V.wait(T)
                                ev = V.o.scalar_tensor_tensor(
                                    outb[:, a * D + kb * 512:a * D + (kb + 1) * 512],
                                    pb[:, :512], satl[:, 0:1],
                                    xblk[:, a * D + kb * 512:a * D + (kb + 1) * 512],
                                    ALU.mult, ALU.add)
                                V.mark(ev)
                        G.wait(V)
                        SG.put(xpark[t0:t0 + TBLK, :]
                               .rearrange("(a p) d -> p a d", p=128),
                               outb[:, :].rearrange("p (a d) -> p a d", d=D))
                barrier()

        if "B" in phases:
            phase_B()

        def phase_C(s):
            pfx = f"c{s}_"
            with contextlib.ExitStack() as ph:
                KT = ph.enter_context(nc.sbuf_tensor(pfx + "KT", [128, DC * S], F32))
                VA = ph.enter_context(nc.sbuf_tensor(pfx + "VA", [128, 16 * 1040], F32))
                QTb = ph.enter_context(nc.sbuf_tensor(pfx + "QTb", [128, DC * 512], F32))
                xblk = ph.enter_context(nc.sbuf_tensor(pfx + "xblk", [128, NTILE * D], F32))
                xnT = ph.enter_context(nc.sbuf_tensor(pfx + "xnT", [128, DC * TBLK], F32))
                attK = xnT
                wst = ph.enter_context(nc.sbuf_tensor(pfx + "wst", [128, 4096], F32))
                EB = ph.enter_context(nc.sbuf_tensor(pfx + "EB", [128, 1024], F32))
                knb = ph.enter_context(nc.sbuf_tensor(pfx + "knb", [128, 256], F32))
                brow1 = ph.enter_context(nc.sbuf_tensor(pfx + "brow1", [1, 1024], F32))
                pbr = brow1
                cb = stats[:, 0:16]
                rzb = stats[:, 52:60]
                attT = QTb

                mb1 = LY.put(brow1[:1, :], b1r[:1, 2048:3072])
                mpb_holder = [0]

                with contextlib.ExitStack() as p1:
                    psA = p1.enter_context(nc.psum_tensor(pfx + "psA", [128, 512], F32))
                    psB = p1.enter_context(nc.psum_tensor(pfx + "psB", [128, 512], F32))
                    for tb in range(NTB):
                        load_ln_transpose(xpark, s, tb, xblk, xnT, psA, psB, LY)
                        for fc in range(8, 16):
                            freg = fc - 8
                            pb = psA if fc % 2 == 0 else psB
                            Y.wait(T)
                            mw = LW.put(wst[:, :DC * 128]
                                        .rearrange("p (c w) -> p c w", w=128),
                                        w1T[:, fc * 128:(fc + 1) * 128]
                                        .rearrange("(c p) w -> p c w", p=128))
                            wait_dma(T, LW, mw)
                            T.wait(V)
                            for c in range(DC):
                                mm_ = T.o.matmul(pb[:, :512],
                                                 wst[:, c * 128:(c + 1) * 128],
                                                 xnT[:, c * TBLK:(c + 1) * TBLK],
                                                 start=(c == 0), stop=(c == DC - 1))
                            T.mark(mm_)
                            V.wait(T)
                            ev = V.o.tensor_scalar(
                                KT[:, freg * S + tb * TBLK:freg * S + (tb + 1) * TBLK],
                                pb[:, :512], b1sb[:, fc:fc + 1], None, ALU.add)
                            V.mark(ev)
                        wait_dma(T, LY, mb1)
                        for kb in range(2):
                            Y.wait(T)
                            mw = LW.put(wst[:, :].rearrange("p (c w) -> p c w", w=512),
                                        w1T[:, 2048 + kb * 512:2048 + (kb + 1) * 512]
                                        .rearrange("(c p) w -> p c w", p=128))
                            wait_dma(T, LW, mw)
                            for a in range(NTILE):
                                jt = tb * NTILE + a
                                pb = psA if a % 2 == 0 else psB
                                T.wait(V)
                                for c in range(DC):
                                    T.o.matmul(pb[:, :512],
                                               xnT[:, c * TBLK + a * 128:
                                                   c * TBLK + (a + 1) * 128],
                                               wst[:, c * 512:(c + 1) * 512],
                                               start=(c == 0), stop=False)
                                mm_ = T.o.matmul(pb[:, :512], onerow[:1, :],
                                                 brow1[:1, kb * 512:(kb + 1) * 512],
                                                 start=False, stop=True)
                                T.mark(mm_)
                                V.wait(T)
                                dst = bass.AP(VA, jt * 1040 + kb * 520,
                                              [[16 * 1040, 128], [65, 8], [1, 64]])
                                ev = V.o.tensor_copy(
                                    dst, pb[:, :512].rearrange("p (h w) -> p h w", w=64))
                                V.mark(ev)
                    for jt in range(16):
                        on = V.o.memset(
                            bass.AP(VA, jt * 1040 + 64, [[16 * 1040, 128], [65, 16]]),
                            1.0)
                        V.mark(on)
                    for c in range(DC):
                        for half in range(2):
                            SC.wait(T)
                            SC.wait(V)
                            sq_ = SC.o.activation(
                                dump[:, :1024],
                                KT[:, c * S + half * 1024:c * S + (half + 1) * 1024],
                                ACT.Square)
                            SC.mark(sq_)
                            T.wait(SC)
                            T.wait(V)
                            for hh in range(2):
                                for jj in range(8):
                                    jt = half * 8 + jj
                                    base = hh * 64
                                    mm_ = T.o.matmul(
                                        psA[:, jt:jt + 1] if hh == 0 else psB[:, jt:jt + 1],
                                        dump[base:base + 64, jj * 128:(jj + 1) * 128],
                                        scv[base:base + 64, 3:4],
                                        start=(half == 0 and jj == 0),
                                        stop=(half == 1 and jj == 7))
                                T.mark(mm_)
                        V.wait(T)
                        cv1 = V.o.tensor_copy(knb[:, (2 * c) * 16:(2 * c) * 16 + 16],
                                              psA[:, 0:16])
                        cv2 = V.o.tensor_copy(knb[:, (2 * c + 1) * 16:(2 * c + 1) * 16 + 16],
                                              psB[:, 0:16])
                        V.mark(cv2)
                    for h in range(16):
                        V.o.drain()
                        rk = V.o.tensor_reduce(stats[:, 50:51],
                                               knb[:, h * 16:(h + 1) * 16],
                                               axis=AXX, op=ALU.max)
                        V.mark(rk)
                        col_to_scalar(stats[:, 50:51], srow[:1, 16 + h:17 + h],
                                      ALU.max, psA)
                barrier()

                with contextlib.ExitStack() as p2:
                    psE1 = p2.enter_context(nc.psum_tensor(pfx + "psE1", [128, 512], F32))
                    psE2 = p2.enter_context(nc.psum_tensor(pfx + "psE2", [128, 512], F32))
                    pacA = p2.enter_context(nc.psum_tensor(pfx + "pacA", [128, 260], F32))
                    pacB = p2.enter_context(nc.psum_tensor(pfx + "pacB", [128, 260], F32))
                    psW = p2.enter_context(nc.psum_tensor(pfx + "psW", [128, 512], F32))
                    psX = p2.enter_context(nc.psum_tensor(pfx + "psX", [128, 512], F32))
                    for ib in range(4):
                        t0 = s * S + ib * TBLK
                        wait_dma(V, SG)
                        load_ln_transpose(xpark, s, ib, xblk, xnT, psW, psX, LY)
                        for fc in range(8):
                            pb = psW if fc % 2 == 0 else psX
                            Y.wait(T)
                            mw = LW.put(wst[:, :DC * 128]
                                        .rearrange("p (c w) -> p c w", w=128),
                                        w1T[:, fc * 128:(fc + 1) * 128]
                                        .rearrange("(c p) w -> p c w", p=128))
                            wait_dma(T, LW, mw)
                            T.wait(V)
                            for c in range(DC):
                                mm_ = T.o.matmul(pb[:, :512],
                                                 wst[:, c * 128:(c + 1) * 128],
                                                 xnT[:, c * TBLK:(c + 1) * TBLK],
                                                 start=(c == 0), stop=(c == DC - 1))
                            T.mark(mm_)
                            V.wait(T)
                            ev = V.o.tensor_scalar(
                                QTb[:, fc * 512:(fc + 1) * 512],
                                pb[:, :512], b1sb[:, fc:fc + 1], None, ALU.add)
                            V.mark(ev)
                        if C_STOP[0] == 1:
                            G.wait(V)
                            SG.put(epark[0:128, :], QTb[:, 0:2048])
                            SG.put(epark[128:256, :], xnT[:, 0:2048])
                            wait_dma(G, SG)
                            return
                        for c in range(DC):
                            SC.wait(T)
                            SC.wait(V)
                            sq_ = SC.o.activation(dump[:, :512],
                                                  QTb[:, c * 512:(c + 1) * 512],
                                                  ACT.Square)
                            SC.mark(sq_)
                            T.wait(SC)
                            for hh in range(2):
                                base = hh * 64
                                for u in range(4):
                                    mm_ = T.o.matmul(
                                        psW[:, (2 * c + hh) * 4 + u:
                                            (2 * c + hh) * 4 + u + 1],
                                        dump[base:base + 64, u * 128:(u + 1) * 128],
                                        scv[base:base + 64, 3:4],
                                        start=(c == 0 and hh == 0 and u == 0),
                                        stop=(c == DC - 1 and hh == 1 and u == 3))
                            T.mark(mm_)
                        for h in range(16):
                            V.wait(T)
                            rq = V.o.tensor_reduce(stats[:, 51:52],
                                                   psW[:, 4 * h:4 * h + 4],
                                                   axis=AXX, op=ALU.max)
                            V.mark(rq)
                            col_to_scalar(stats[:, 51:52], srow[:1, h:h + 1],
                                          ALU.max, psX)
                        V.o.drain()
                        tt_ = V.o.tensor_tensor(srow[:1, 40:56], srow[:1, 0:16],
                                                srow[:1, 16:32], ALU.mult)
                        V.mark(tt_)
                        SC.wait(V)
                        sq_ = SC.o.activation(srow[:1, 40:56], srow[:1, 40:56],
                                              ACT.Sqrt)
                        SC.mark(sq_)
                        V.wait(SC)
                        ng = V.o.tensor_scalar(srow[:1, 40:56], srow[:1, 40:56],
                                               -0.125, None, ALU.mult)
                        V.mark(ng)
                        V.o.drain()
                        bcast_scalar(srow[:1, 40:56], cb, psW)

                        if C_STOP[0] == 2:
                            G.wait(V)
                            SG.put(epark[0:1, 0:64], srow[:1, 0:64])
                            SG.put(epark[2:130, 0:16], cb)
                            SG.put(epark[256:384, :], KT[:, 0:2048])
                            wait_dma(G, SG)
                            return
                        sc_mark = [0, 0]
                        t_attn_mark = [0, 0]
                        exp_marks = {}
                        for hp in range(8):
                            hA, hB = 2 * hp, 2 * hp + 1
                            for step in range(17):
                                if step < 16:
                                    jt = step
                                    par = jt % 2
                                    sbA, sbB = (psE1, psE2) if par == 0 else (psW, psX)
                                    T.wait(SC, sc_mark[par])
                                    mmA = T.o.matmul(
                                        sbA[:, :512],
                                        KT[0:64, hp * S + jt * 128:hp * S + (jt + 1) * 128],
                                        QTb[0:64, hp * 512:(hp + 1) * 512],
                                        start=True, stop=True)
                                    mmB = T.o.matmul(
                                        sbB[:, :512],
                                        KT[64:128, hp * S + jt * 128:hp * S + (jt + 1) * 128],
                                        QTb[64:128, hp * 512:(hp + 1) * 512],
                                        start=True, stop=True)
                                    smark = T.mark(mmB)
                                    SC.wait(T, smark)
                                    SC.wait(T, t_attn_mark[par])
                                    EBp = EB if par == 0 else dump
                                    exA = SC.o.activation(
                                        EBp[:, 0:512],
                                        sbA[:, :], ACT.Exp, scale=0.125,
                                        bias=cb[:, hA:hA + 1])
                                    exB = SC.o.activation(
                                        EBp[:, 512:1024],
                                        sbB[:, :], ACT.Exp, scale=0.125,
                                        bias=cb[:, hB:hB + 1])
                                    sc_mark[par] = SC.mark(exB)
                                    exp_marks[jt] = sc_mark[par]
                                if step >= 1:
                                    jt = step - 1
                                    par = jt % 2
                                    T.wait(SC, exp_marks[jt])
                                    if jt == 0:
                                        T.wait(V)
                                    EBp = EB if par == 0 else dump
                                    for u in range(4):
                                        T.o.matmul(pacA[:, 65 * u:65 * u + 65],
                                                   EBp[:, u * 128:(u + 1) * 128],
                                                   bass.AP(VA, jt * 1040 + hA * 65,
                                                           [[16 * 1040, 128], [1, 65]]),
                                                   start=(jt == 0 and u == 0),
                                                   stop=(jt == 15 and u == 3))
                                        mm_ = T.o.matmul(
                                            pacB[:, 65 * u:65 * u + 65],
                                            EBp[:, 512 + u * 128:512 + (u + 1) * 128],
                                            bass.AP(VA, jt * 1040 + hB * 65,
                                                    [[16 * 1040, 128], [1, 65]]),
                                            start=(jt == 0 and u == 0),
                                            stop=(jt == 15 and u == 3))
                                    t_attn_mark[par] = T.mark(mm_)
                            V.wait(T)
                            r1 = V.o.reciprocal(rzb[:, 0:4],
                                                bass.AP(pacA, 64, [[260, 128], [65, 4]]))
                            r2 = V.o.reciprocal(rzb[:, 4:8],
                                                bass.AP(pacB, 64, [[260, 128], [65, 4]]))
                            V.mark(r2)
                            V.o.drain()
                            for u in range(4):
                                V.o.tensor_scalar(
                                    attK[:, u * 1024 + hA * 64:u * 1024 + hA * 64 + 64],
                                    pacA[:, 65 * u:65 * u + 64],
                                    rzb[:, u:u + 1], None, ALU.mult)
                                nv = V.o.tensor_scalar(
                                    attK[:, u * 1024 + hB * 64:u * 1024 + hB * 64 + 64],
                                    pacB[:, 65 * u:65 * u + 64],
                                    rzb[:, 4 + u:5 + u], None, ALU.mult)
                            V.mark(nv)
                            if C_STOP[0] == 3 and hp == 0:
                                pz = V.o.tensor_copy(dump[:, 0:260], pacA[:, :])
                                V.mark(pz)
                                G.wait(V)
                                SG.put(epark[0:128, 0:1024], EB[:, :])
                                SG.put(epark[128:256, 0:260], dump[:, 0:260])
                                SG.put(epark[256:384, 0:8], rzb)
                                SG.put(epark[384:512, 0:2048], attK[:, 0:2048])
                                SG.put(epark[512:513, 0:64], srow[:1, 0:64])
                                wait_dma(G, SG)
                                return

                        if ib == 0:
                            Y.wait(T)
                            mpb = LY.put(pbr[:1, :], pb1r[:1, :])
                            mpb_holder[0] = mpb
                        for u in range(NTILE):
                            for c in range(DC):
                                pb = psW if c % 2 == 0 else psX
                                T.wait(V)
                                tr = T.o.transpose(pb[:, :128],
                                                   attK[:, u * 1024 + c * 128:
                                                        u * 1024 + (c + 1) * 128],
                                                   ident[:, :])
                                T.mark(tr)
                                V.wait(T)
                                cp = V.o.tensor_copy(
                                    attT[:, c * 512 + u * 128:c * 512 + (u + 1) * 128],
                                    pb[:, :128])
                                V.mark(cp)
                        wait_dma(T, LY, mpb_holder[0])
                        for ob in range(2):
                            Y.wait(T)
                            mw = LW.put(wst[:, :].rearrange("p (c w) -> p c w", w=512),
                                        wp1T[:, ob * 512:(ob + 1) * 512]
                                        .rearrange("(c p) w -> p c w", p=128))
                            wait_dma(T, LW, mw)
                            for u in range(NTILE):
                                pb = psW if u % 2 == 0 else psX
                                T.wait(V)
                                for c in range(DC):
                                    T.o.matmul(pb[:, :512],
                                               attT[:, c * 512 + u * 128:
                                                    c * 512 + (u + 1) * 128],
                                               wst[:, c * 512:c * 512 + 512],
                                               start=(c == 0), stop=False)
                                mm_ = T.o.matmul(pb[:, :512], onerow[:1, :],
                                                 pbr[:1, ob * 512:(ob + 1) * 512],
                                                 start=False, stop=True)
                                T.mark(mm_)
                                V.wait(T)
                                ev = V.o.tensor_tensor(
                                    attK[:, u * 1024 + ob * 512:u * 1024 + (ob + 1) * 512],
                                    pb[:, :512],
                                    xblk[:, u * D + ob * 512:u * D + (ob + 1) * 512],
                                    ALU.add)
                                V.mark(ev)
                        G.wait(V)
                        SG.put(xpark[t0:t0 + TBLK, :]
                               .rearrange("(a p) d -> p a d", p=128),
                               attK[:, :].rearrange("p (a d) -> p a d", d=D))
                    barrier()

        if "C" in phases:
            phase_C(0)
            phase_C(1)

        def phase_D():
            with contextlib.ExitStack() as ph:
                xblk = ph.enter_context(nc.sbuf_tensor("d_xblk", [128, NTILE * D], F32))
                xnT = ph.enter_context(nc.sbuf_tensor("d_xnT", [128, DC * TBLK], F32))
                gbuf = ph.enter_context(nc.sbuf_tensor("d_gbuf", [128, 32 * 512], F32))
                wstD = ph.enter_context(nc.sbuf_tensor("d_wst", [128, 4096], F16))
                xh = ph.enter_context(nc.sbuf_tensor("d_xh", [128, DC * TBLK], F16))
                xl = ph.enter_context(nc.sbuf_tensor("d_xl", [128, DC * TBLK], F16))
                gh = ph.enter_context(nc.sbuf_tensor("d_gh", [128, 32 * 512], F16))
                gl = ph.enter_context(nc.sbuf_tensor("d_gl", [128, 32 * 512], F16))
                lnwb = ph.enter_context(nc.sbuf_tensor("d_lnwb", [2, 512], F32))
                rowsA = ph.enter_context(nc.sbuf_tensor("d_rowsA", [1, 512], F32))
                rowsB = ph.enter_context(nc.sbuf_tensor("d_rowsB", [2, 512], F32))
                trsb = ph.enter_context(nc.sbuf_tensor("d_trsb", [8, 128], F32))
                for kq in range(4):
                    LY.put(rowsB[1:2, kq * 128:(kq + 1) * 128], conerow[:1, :])
                for s in range(NSEQ):
                    for tb in range(NTB):
                        t0 = s * S + tb * TBLK
                        with contextlib.ExitStack() as pp:
                            psA = pp.enter_context(
                                nc.psum_tensor(f"d_psA{s}{tb}", [128, 512], F32))
                            psB = pp.enter_context(
                                nc.psum_tensor(f"d_psB{s}{tb}", [128, 512], F32))
                            wait_dma(SC, SG)
                            wait_dma(V, SG)
                            load_ln_transpose(xpark, s, tb, xblk, xnT, psA, psB, LY)
                            V.o.tensor_copy(xh[:, :], xnT[:, :])
                            V.o.drain()
                            sp = V.o.tensor_tensor(xl[:, :], xnT[:, :], xh[:, :],
                                                   ALU.subtract)
                            V.mark(sp)
                            for og in range(8):
                                Y.wait(T)
                                mw = LW.put(wstD[:, :]
                                            .rearrange("p (c w) -> p c w", w=512),
                                            tfcT[:, og * 512:(og + 1) * 512]
                                            .rearrange("(c p) w -> p c w", p=128))
                                wait_dma(T, LW, mw)
                                for oo in range(4):
                                    oc = og * 4 + oo
                                    pb = psA if oc % 2 == 0 else psB
                                    T.wait(SC)
                                    T.wait(V)
                                    for c in range(DC):
                                        T.o.matmul(
                                            pb[:, :512],
                                            wstD[:, c * 512 + oo * 128:
                                                 c * 512 + oo * 128 + 128],
                                            xh[:, c * TBLK:(c + 1) * TBLK],
                                            start=(c == 0), stop=False)
                                        mm_ = T.o.matmul(
                                            pb[:, :512],
                                            wstD[:, c * 512 + oo * 128:
                                                 c * 512 + oo * 128 + 128],
                                            xl[:, c * TBLK:(c + 1) * TBLK],
                                            start=False, stop=(c == DC - 1))
                                    T.mark(mm_)
                                    SC.wait(T)
                                    ge = SC.o.activation(
                                        gbuf[:, oc * 512:(oc + 1) * 512],
                                        pb[:, :512], ACT.Gelu)
                                    SC.mark(ge)
                            for oc in range(32):
                                SC.wait(T)
                                sq_ = SC.o.activation(dump[:, :512],
                                                      gbuf[:, oc * 512:(oc + 1) * 512],
                                                      ACT.Square)
                                SC.mark(sq_)
                                T.wait(SC)
                                for u in range(4):
                                    T.o.matmul(psA[:, 32 + u:33 + u],
                                               gbuf[:, oc * 512 + u * 128:
                                                    oc * 512 + (u + 1) * 128],
                                               scv[:, 3:4],
                                               start=(oc == 0 and u == 0),
                                               stop=(oc == 31 and u == 3))
                                    mm_ = T.o.matmul(psB[:, 32 + u:33 + u],
                                                     dump[:, u * 128:(u + 1) * 128],
                                                     scv[:, 3:4],
                                                     start=(oc == 0 and u == 0),
                                                     stop=(oc == 31 and u == 3))
                                T.mark(mm_)
                            V.wait(T)
                            V.o.tensor_scalar(stats[:, 32:36], psA[:, 32:36],
                                              1.0 / DFF, None, ALU.mult)
                            V.o.tensor_scalar(stats[:, 36:40], psB[:, 32:36],
                                              1.0 / DFF, None, ALU.mult)
                            V.o.drain()
                            V.o.tensor_tensor(stats[:, 40:44], stats[:, 32:36],
                                              stats[:, 32:36], ALU.mult)
                            V.o.drain()
                            vv = V.o.tensor_tensor(stats[:, 36:40], stats[:, 36:40],
                                                   stats[:, 40:44], ALU.subtract)
                            V.mark(vv)
                            SC.wait(V)
                            sq_ = SC.o.activation(stats[:, 40:44], stats[:, 36:40],
                                                  ACT.Sqrt, bias=EPS_LN)
                            SC.mark(sq_)
                            V.wait(SC)
                            V.o.reciprocal(stats[:, 40:44], stats[:, 40:44])
                            V.o.drain()
                            V.o.tensor_tensor(stats[:, 44:48], stats[:, 32:36],
                                              stats[:, 40:44], ALU.mult)
                            V.o.drain()
                            mr = V.o.tensor_scalar(stats[:, 44:48], stats[:, 44:48],
                                                   -1.0, None, ALU.mult)
                            V.mark(mr)
                            T.wait(V)
                            tr = T.o.transpose(psA[0:8, 0:128], stats[:, 40:48],
                                               ident[:, :])
                            T.mark(tr)
                            V.wait(T)
                            cpr = V.o.tensor_copy(trsb[0:8, :], psA[0:8, 0:128])
                            V.mark(cpr)
                            G.wait(V)
                            mra = SG.put(rowsA[:1, :], trsb[0:4, :])
                            mrb = SG.put(rowsB[0:1, :], trsb[4:8, :])
                            wait_dma(T, SG, mrb)
                            for oc in range(32):
                                if oc % 4 == 0:
                                    Y.wait(T)
                                    mlw = LY.put(lnwb[:2, :],
                                                 lnwb2[:2, (oc // 4) * 512:
                                                       (oc // 4 + 1) * 512])
                                    wait_dma(T, LY, mlw)
                                oq = oc % 4
                                T.wait(V)
                                mmA = T.o.matmul(psA[:, :512],
                                                 lnwb[0:1, oq * 128:(oq + 1) * 128],
                                                 rowsA[:1, :], start=True, stop=True)
                                mmB = T.o.matmul(psB[:, :512],
                                                 lnwb[0:2, oq * 128:(oq + 1) * 128],
                                                 rowsB[:2, :], start=True, stop=True)
                                T.mark(mmB)
                                V.wait(T)
                                V.o.tensor_tensor(gbuf[:, oc * 512:(oc + 1) * 512],
                                                  gbuf[:, oc * 512:(oc + 1) * 512],
                                                  psA[:, :512], ALU.mult)
                                V.o.drain()
                                nv = V.o.tensor_tensor(gbuf[:, oc * 512:(oc + 1) * 512],
                                                       gbuf[:, oc * 512:(oc + 1) * 512],
                                                       psB[:, :512], ALU.add)
                                V.mark(nv)
                        V.o.drain()
                        V.o.tensor_copy(gh[:, :], gbuf[:, :])
                        V.o.drain()
                        sp2 = V.o.tensor_tensor(gl[:, :], gbuf[:, :], gh[:, :],
                                                ALU.subtract)
                        V.mark(sp2)
                        with contextlib.ExitStack() as pp2:
                            pps = [pp2.enter_context(
                                nc.psum_tensor(f"d_pp{s}{tb}_{i}", [128, 512], F32))
                                for i in range(8)]
                            T.wait(V)
                            for og in range(8):
                                Y.wait(T)
                                mw = LW.put(wstD[:, :]
                                            .rearrange("p (c w) -> p c w", w=1024),
                                            tp2T[og * 512:(og + 1) * 512, :]
                                            .rearrange("(c p) w -> p c w", p=128))
                                wait_dma(T, LW, mw)
                                for oo in range(4):
                                    oc = og * 4 + oo
                                    for ob in range(2):
                                        for u in range(4):
                                            T.o.matmul(
                                                pps[ob * 4 + u][:, :512],
                                                gh[:, oc * 512 + u * 128:
                                                   oc * 512 + (u + 1) * 128],
                                                wstD[:, oo * 1024 + ob * 512:
                                                     oo * 1024 + (ob + 1) * 512],
                                                start=(oc == 0), stop=False)
                                            mm_ = T.o.matmul(
                                                pps[ob * 4 + u][:, :512],
                                                gl[:, oc * 512 + u * 128:
                                                   oc * 512 + (u + 1) * 128],
                                                wstD[:, oo * 1024 + ob * 512:
                                                     oo * 1024 + (ob + 1) * 512],
                                                start=False, stop=(oc == 31))
                                T.mark(mm_)
                            V.wait(T)
                            for ob in range(2):
                                for u in range(4):
                                    ev = V.o.tensor_tensor(
                                        gbuf[:, u * 1024 + ob * 512:
                                             u * 1024 + (ob + 1) * 512],
                                        pps[ob * 4 + u][:, :512],
                                        xblk[:, u * D + ob * 512:u * D + (ob + 1) * 512],
                                        ALU.add)
                            V.mark(ev)
                            G.wait(V)
                            SG.put(yout[t0:t0 + TBLK, :]
                                   .rearrange("(a p) d -> p a d", p=128),
                                   gbuf[:, :NTILE * D]
                                   .rearrange("p (a d) -> p a d", d=D))
                barrier()

        if "D" in phases:
            phase_D()

        if "D" not in phases:
          with nc.sbuf_tensor("st_buf", [128, NTILE * D], F32) as sb:
            for s in range(NSEQ):
                for tb in range(NTB):
                    t0 = s * S + tb * TBLK
                    wait_dma(Y, SG)
                    m = LY.put(sb[:, :].rearrange("p (a d) -> p a d", d=D),
                               xpark[t0:t0 + TBLK, :]
                               .rearrange("(a p) d -> p a d", p=128))
                    wait_dma(G, LY, m)
                    SG.put(yout[t0:t0 + TBLK, :]
                           .rearrange("(a p) d -> p a d", p=128),
                           sb[:, :].rearrange("p (a d) -> p a d", d=D))
          barrier()

        wait_dma(G, SG)
        wait_dma(Y, LY)

    return nc


_NC_CACHE = {}


def _get_program():
    if "nc" not in _NC_CACHE:
        _NC_CACHE["nc"] = build_program()
    return _NC_CACHE["nc"]


def make_in_maps(inputs_by_name):
    x = np.ascontiguousarray(np.asarray(inputs_by_name["L0_attn_x"],
                                        dtype=np.float32))
    shared = {
        "w0": np.ascontiguousarray(inputs_by_name["L0_attn_W"]),
        "b0c": np.ascontiguousarray(
            np.asarray(inputs_by_name["L0_attn_b"]).reshape(24, 128).T),
        "b0r": np.asarray(inputs_by_name["L0_attn_b"]).reshape(1, 3 * D),
        "fw0": np.ascontiguousarray(inputs_by_name["L0_ffn_weight"]),
        "g0c": np.ascontiguousarray(
            np.asarray(inputs_by_name["L0_ffn_gamma"]).reshape(DC, 128).T),
        "be0c": np.ascontiguousarray(
            np.asarray(inputs_by_name["L0_ffn_beta"]).reshape(DC, 128).T),
        "w1": np.ascontiguousarray(inputs_by_name["L1_attn_qkv_w"]),
        "b1c": np.ascontiguousarray(
            np.asarray(inputs_by_name["L1_attn_qkv_b"]).reshape(24, 128).T),
        "b1r": np.asarray(inputs_by_name["L1_attn_qkv_b"]).reshape(1, 3 * D),
        "pw1": np.ascontiguousarray(inputs_by_name["L1_attn_proj_w"]),
        "pb1r": np.asarray(inputs_by_name["L1_attn_proj_b"]).reshape(1, D),
        "fcw1": np.ascontiguousarray(inputs_by_name["L1_ffn_W_fc"]),
        "lnwb2": np.ascontiguousarray(
            np.stack([np.asarray(inputs_by_name["L1_ffn_ln_w"]),
                      np.asarray(inputs_by_name["L1_ffn_ln_b"])])),
        "pw2": np.ascontiguousarray(inputs_by_name["L1_ffn_W_proj"]),
        "cident": np.eye(128, dtype=np.float32),
        "conerow": np.ones((1, 128), dtype=np.float32),
        "cvals": np.broadcast_to(np.array(
            [127.0, EPS_LN, -63.5, 1.0, 0.0, 0.0, 0.0, 0.0],
            dtype=np.float32), (128, 8)).copy(),
    }
    in_maps = []
    for c in range(NCORE):
        im = dict(shared)
        im["xin"] = np.ascontiguousarray(x[2 * c:2 * c + 2].reshape(NSEQ * S, D))
        in_maps.append(im)
    return in_maps


def kernel(L0_attn_x, L0_attn_W, L0_attn_b, L0_ffn_weight, L0_ffn_gamma,
           L0_ffn_beta, L1_attn_qkv_w, L1_attn_qkv_b, L1_attn_proj_w,
           L1_attn_proj_b, L1_ffn_W_fc, L1_ffn_ln_w, L1_ffn_ln_b,
           L1_ffn_W_proj):
    kw = dict(L0_attn_x=L0_attn_x, L0_attn_W=L0_attn_W, L0_attn_b=L0_attn_b,
              L0_ffn_weight=L0_ffn_weight, L0_ffn_gamma=L0_ffn_gamma,
              L0_ffn_beta=L0_ffn_beta, L1_attn_qkv_w=L1_attn_qkv_w,
              L1_attn_qkv_b=L1_attn_qkv_b, L1_attn_proj_w=L1_attn_proj_w,
              L1_attn_proj_b=L1_attn_proj_b, L1_ffn_W_fc=L1_ffn_W_fc,
              L1_ffn_ln_w=L1_ffn_ln_w, L1_ffn_ln_b=L1_ffn_ln_b,
              L1_ffn_W_proj=L1_ffn_W_proj)
    nc = _get_program()
    in_maps = make_in_maps(kw)
    res = run_bass_kernel_spmd(nc, in_maps, core_ids=list(range(NCORE)))
    out = np.concatenate(
        [np.asarray(r["y"]).reshape(NSEQ, S, D) for r in res.results], axis=0)
    return out.astype(np.float32)


# version 04
# speedup vs baseline: 1.3947x; 1.1643x over previous
import contextlib
import numpy as np

import concourse.bass as bass
import concourse.mybir as mybir
from concourse.bass_utils import run_bass_kernel_spmd

F32 = mybir.dt.float32
BF16 = mybir.dt.bfloat16
F16 = mybir.dt.float16
ALU = mybir.AluOpType
ACT = mybir.ActivationFunctionType
AXX = mybir.AxisListType.X

D = 1024
S = 2048
NSEQ = 2
NCORE = 8
DC = D // 128
TBLK = 512
NTB = S // TBLK
NTILE = TBLK // 128
H = 16
HD = 64
DFF = 4096
EPS_LN = 1e-5
ALL_PHASES = ("P", "A", "B", "C", "D")
A_STOP = [4]
A1_NTB = [NTB]
A1_NFC = [16]
LN_PARTS = [3]
DBG_EPARK = [False]
A_SEQS = [2]
C_STOP = [99]


class Eng:
    def __init__(self, obj, sem, name):
        self.o = obj
        self.sem = sem
        self.name = name
        self.n = 0
        self._seen = {}

    def mark(self, instr, k=1):
        assert instr is not None, "bare sem_inc fires before datapath drain"
        instr.then_inc(self.sem, k)
        self.n += k
        return self.n

    def wait(self, other, val=None):
        if val is None:
            val = other.n
        if val <= 0:
            return
        key = other.name
        if self._seen.get(key, -1) >= val:
            return
        self.o.wait_ge(other.sem, val)
        self._seen[key] = val


class DmaStream:

    def __init__(self, eng, sem, name):
        self.eng = eng
        self.sem = sem
        self.name = name
        self.n = 0

    def put(self, out, in_):
        self.eng.o.dma_start(out=out, in_=in_).then_inc(self.sem, 16)
        self.n += 16
        return self.n


def wait_dma(eng, st, val=None):
    if val is None:
        val = st.n
    if val <= 0:
        return
    key = "dma_" + st.name
    if eng._seen.get(key, -1) >= val:
        return
    eng.o.wait_ge(st.sem, val)
    eng._seen[key] = val


def build_program(phases=ALL_PHASES):
    phases = set(phases)
    nc = bass.Bass("TRN2", target_bir_lowering=False,
                   detect_race_conditions=False)

    xin = nc.dram_tensor("xin", [NSEQ * S, D], F32, kind="ExternalInput").ap()
    w0 = nc.dram_tensor("w0", [3 * D, D], F32, kind="ExternalInput").ap()
    b0c = nc.dram_tensor("b0c", [128, 24], F32, kind="ExternalInput").ap()
    b0r = nc.dram_tensor("b0r", [1, 3 * D], F32, kind="ExternalInput").ap()
    fw0 = nc.dram_tensor("fw0", [D, D], F32, kind="ExternalInput").ap()
    g0c = nc.dram_tensor("g0c", [128, DC], F32, kind="ExternalInput").ap()
    be0c = nc.dram_tensor("be0c", [128, DC], F32, kind="ExternalInput").ap()
    w1 = nc.dram_tensor("w1", [3 * D, D], F32, kind="ExternalInput").ap()
    b1c = nc.dram_tensor("b1c", [128, 24], F32, kind="ExternalInput").ap()
    b1r = nc.dram_tensor("b1r", [1, 3 * D], F32, kind="ExternalInput").ap()
    pw1 = nc.dram_tensor("pw1", [D, D], F32, kind="ExternalInput").ap()
    pb1r = nc.dram_tensor("pb1r", [1, D], F32, kind="ExternalInput").ap()
    fcw1 = nc.dram_tensor("fcw1", [DFF, D], F32, kind="ExternalInput").ap()
    lnwb2 = nc.dram_tensor("lnwb2", [2, DFF], F32, kind="ExternalInput").ap()
    lnwc = nc.dram_tensor("lnwc", [128, 64], F32, kind="ExternalInput").ap()
    pw2 = nc.dram_tensor("pw2", [D, DFF], F32, kind="ExternalInput").ap()
    cident = nc.dram_tensor("cident", [128, 128], F32, kind="ExternalInput").ap()
    conerow = nc.dram_tensor("conerow", [1, 128], F32, kind="ExternalInput").ap()
    cvals = nc.dram_tensor("cvals", [128, 8], F32, kind="ExternalInput").ap()
    yout = nc.dram_tensor("y", [NSEQ * S, D], F32, kind="ExternalOutput").ap()

    xpark = nc.dram_tensor("xpark", [NSEQ * S, D], F32,
                           kind="ExternalOutput" if DBG_EPARK[0] else "Internal").ap()
    epark = nc.dram_tensor("epark", [S, S], F32,
                           kind="ExternalOutput" if DBG_EPARK[0] else "Internal").ap()
    w0T = nc.dram_tensor("w0T", [D, 3 * D], F32).ap()
    w1T = nc.dram_tensor("w1T", [D, 3 * D], F32).ap()
    wp1T = nc.dram_tensor("wp1T", [D, D], F32).ap()
    t0T = nc.dram_tensor("t0T", [D, D], BF16).ap()
    tfcT = nc.dram_tensor("tfcT", [D, DFF], F16).ap()
    tp2T = nc.dram_tensor("tp2T", [DFF, D], F16).ap()
    ccin = nc.dram_tensor("ccin", [1, 128], F32)
    ccout = nc.dram_tensor("ccout", [1, 128], F32)

    es = contextlib.ExitStack()
    with es:
        semT = es.enter_context(nc.semaphore("semT"))
        semV = es.enter_context(nc.semaphore("semV"))
        semS = es.enter_context(nc.semaphore("semS"))
        semG = es.enter_context(nc.semaphore("semG"))
        semY = es.enter_context(nc.semaphore("semY"))
        semLY = es.enter_context(nc.semaphore("semLY"))
        semLW = es.enter_context(nc.semaphore("semLW"))
        semSG = es.enter_context(nc.semaphore("semSG"))
        semLG = es.enter_context(nc.semaphore("semLG"))
        ccsem = es.enter_context(nc.semaphore("ccsem"))

        es.enter_context(nc.Block())

        T = Eng(nc.tensor, semT, "T")
        V = Eng(nc.vector, semV, "V")
        SC = Eng(nc.scalar, semS, "S")
        G = Eng(nc.gpsimd, semG, "G")
        Y = Eng(nc.sync, semY, "Y")
        ALL_E = [T, V, SC, G, Y]

        LY = DmaStream(Y, semLY, "LY")
        LW = DmaStream(Y, semLW, "LW")
        SG = DmaStream(G, semSG, "SG")
        LG = DmaStream(G, semLG, "LG")

        def barrier():
            for e in ALL_E:
                for o in ALL_E:
                    if o is not e:
                        e.wait(o)
                for st_ in (LY, LW, SG, LG):
                    wait_dma(e, st_)

        ident = es.enter_context(nc.sbuf_tensor("ident", [128, 128], F32))
        onerow = es.enter_context(nc.sbuf_tensor("onerow", [1, 128], F32))
        dump = es.enter_context(nc.sbuf_tensor("dump", [128, 1024], F32))
        stats = es.enter_context(nc.sbuf_tensor("stats", [128, 64], F32))
        srow = es.enter_context(nc.sbuf_tensor("srow", [1, 64], F32))
        b0sb = es.enter_context(nc.sbuf_tensor("b0sb", [128, 24], F32))
        b1sb = es.enter_context(nc.sbuf_tensor("b1sb", [128, 24], F32))
        g0sb = es.enter_context(nc.sbuf_tensor("g0sb", [128, DC], F32))
        be0sb = es.enter_context(nc.sbuf_tensor("be0sb", [128, DC], F32))
        scv = es.enter_context(nc.sbuf_tensor("scv", [128, 8], F32))
        qscal = es.enter_context(nc.sbuf_tensor("qscal", [128, 8], F32))

        LY.put(ident[:, :], cident[:, :])
        LY.put(onerow[:1, :], conerow[:1, :])
        LY.put(b0sb[:, :], b0c[:, :])
        LY.put(b1sb[:, :], b1c[:, :])
        LY.put(g0sb[:, :], g0c[:, :])
        LY.put(be0sb[:, :], be0c[:, :])
        LY.put(scv[:, :], cvals[:, :])
        nc.const_aps.register(F32, EPS_LN, scv[:, 1:2])
        nc.const_aps.register(F32, -63.5, scv[:, 2:3])

        def col_to_scalar(col_ap, out_ap, op, ps):
            T.wait(V)
            tr = T.o.transpose(ps[:1, :128], col_ap, ident[:, :])
            T.mark(tr)
            V.wait(T)
            rd = V.o.tensor_reduce(out_ap, ps[:1, :128], axis=AXX, op=op)
            V.mark(rd)

        def bcast_scalar(src_ap, dst_ap, ps):
            w = src_ap.shape[-1]
            T.wait(V)
            mm = T.o.matmul(ps[:, :w], onerow[:1, :], src_ap,
                            start=True, stop=True)
            T.mark(mm)
            V.wait(T)
            cp = V.o.tensor_copy(dst_ap, ps[:, :w])
            V.mark(cp)

        if "P" in phases:
          with contextlib.ExitStack() as ps_:
            win = ps_.enter_context(nc.sbuf_tensor("p_win", [128, 4096], F32))
            wot = ps_.enter_context(nc.sbuf_tensor("p_wot", [128, 8192], F32))
            pst = ps_.enter_context(nc.psum_tensor("p_pst", [128, 512], F32))
            pst2 = ps_.enter_context(nc.psum_tensor("p_pst2", [128, 512], F32))

            def transpose_rows(src, src_rows, src_cols, dstT):
                nrc = src_rows // 128
                ncc = src_cols // 128
                evmark = {}
                for rc in range(nrc):
                    Y.wait(T)
                    m = LY.put(win[:, :src_cols], src[rc * 128:(rc + 1) * 128, :])
                    wait_dma(T, LY, m)
                    for cc in range(ncc):
                        pb = pst if cc % 2 == 0 else pst2
                        ev = V if cc % 2 == 0 else SC
                        if cc >= 2:
                            T.wait(ev, evmark.get(cc % 2, 0))
                        tr = T.o.transpose(pb[:, :128],
                                           win[:, cc * 128:(cc + 1) * 128],
                                           ident[:, :])
                        tm = T.mark(tr)
                        ev.wait(T, tm)
                        wait_dma(ev, SG)
                        if ev is V:
                            cp = V.o.tensor_copy(wot[:, cc * 128:(cc + 1) * 128],
                                                 pb[:, :128])
                        else:
                            cp = SC.o.copy(wot[:, cc * 128:(cc + 1) * 128],
                                           pb[:, :128])
                        evmark[cc % 2] = ev.mark(cp)
                    G.wait(V)
                    G.wait(SC)
                    SG.put(dstT[:, rc * 128:(rc + 1) * 128]
                           .rearrange("(c p) w -> p c w", p=128),
                           wot[:, :src_cols].rearrange("p (c w) -> p c w", w=128))

            transpose_rows(w0, 3 * D, D, w0T)
            transpose_rows(w1, 3 * D, D, w1T)
            transpose_rows(pw1, D, D, wp1T)

            def gsum_abs(src, rows, cols, out_col):
                nrc = rows // 128
                for rc in range(nrc):
                    Y.wait(V)
                    Y.wait(SC)
                    m = LY.put(win[:, :cols], src[rc * 128:(rc + 1) * 128, :])
                    wait_dma(V, LY, m)
                    V.o.tensor_reduce(stats[:, 62:63], win[:, :cols], axis=AXX,
                                      op=ALU.add, apply_absolute_value=True)
                    V.o.drain()
                    if rc == 0:
                        cp = V.o.tensor_copy(stats[:, out_col:out_col + 1],
                                             stats[:, 62:63])
                    else:
                        cp = V.o.tensor_tensor(stats[:, out_col:out_col + 1],
                                               stats[:, out_col:out_col + 1],
                                               stats[:, 62:63], ALU.add)
                    V.mark(cp)
                    V.o.drain()
                col_to_scalar(stats[:, out_col:out_col + 1],
                              srow[:1, out_col:out_col + 1], ALU.add, pst)

            gsum_abs(fw0, D, D, 0)
            V.o.drain()
            dl_ = V.o.tensor_scalar(srow[:1, 1:2], srow[:1, 0:1],
                                    0.7 / (2.0 ** 20), None, ALU.mult)
            V.mark(dl_)
            bcast_scalar(srow[:1, 1:2], qscal[:, 2:3], pst)
            for rc in range(D // 128):
                Y.wait(V)
                Y.wait(SC)
                m = LY.put(win[:, :D], fw0[rc * 128:(rc + 1) * 128, :])
                wait_dma(SC, LY, m)
                SC.wait(V)
                ab = SC.o.activation(wot[:, :D], win[:, :D], ACT.Abs)
                SC.mark(ab)
                wait_dma(V, LY, m)
                V.wait(SC)
                V.o.tensor_scalar(wot[:, 4096:4096 + D], wot[:, :D],
                                  qscal[:, 2:3], None, ALU.is_gt)
                V.o.drain()
                V.o.tensor_reduce(stats[:, 62:63], wot[:, 4096:4096 + D],
                                  axis=AXX, op=ALU.add)
                V.o.drain()
                if rc == 0:
                    V.o.tensor_copy(stats[:, 3:4], stats[:, 62:63])
                else:
                    V.o.tensor_tensor(stats[:, 3:4], stats[:, 3:4],
                                      stats[:, 62:63], ALU.add)
                V.o.tensor_tensor(wot[:, :D], wot[:, :D], wot[:, 4096:4096 + D],
                                  ALU.mult)
                V.o.drain()
                V.o.tensor_reduce(stats[:, 62:63], wot[:, :D], axis=AXX,
                                  op=ALU.add)
                V.o.drain()
                if rc == 0:
                    cp = V.o.tensor_copy(stats[:, 4:5], stats[:, 62:63])
                else:
                    cp = V.o.tensor_tensor(stats[:, 4:5], stats[:, 4:5],
                                           stats[:, 62:63], ALU.add)
                V.mark(cp)
                V.o.drain()
            col_to_scalar(stats[:, 3:4], srow[:1, 3:4], ALU.add, pst)
            col_to_scalar(stats[:, 4:5], srow[:1, 4:5], ALU.add, pst2)
            V.o.tensor_scalar(srow[:1, 5:6], srow[:1, 3:4], 1.0, None, ALU.max)
            V.o.drain()
            V.o.reciprocal(srow[:1, 5:6], srow[:1, 5:6])
            V.o.drain()
            al = V.o.tensor_tensor(srow[:1, 6:7], srow[:1, 4:5], srow[:1, 5:6],
                                   ALU.mult)
            V.mark(al)
            bcast_scalar(srow[:1, 6:7], qscal[:, 3:4], pst)

            def quantize_transposed(src, rows, cols, dstT, thr_col, scale_col):
                vmark = {}
                for rc in range(rows // 128):
                    Y.wait(T)
                    m = LY.put(win[:, :cols], src[rc * 128:(rc + 1) * 128, :])
                    wait_dma(T, LY, m)
                    for cc in range(cols // 128):
                        pb = pst if cc % 2 == 0 else pst2
                        if cc >= 2:
                            T.wait(SC, vmark.get(("s", cc % 2), 0))
                        tr = T.o.transpose(pb[:, :128],
                                           win[:, cc * 128:(cc + 1) * 128],
                                           ident[:, :])
                        tm = T.mark(tr)
                        SC.wait(T, tm)
                        wait_dma(SC, SG)
                        SC.wait(V, vmark.get(("v", cc % 2), 0))
                        sg = SC.o.activation(
                            wot[:, 4096 + cc * 128:4096 + (cc + 1) * 128],
                            pb[:, :128], ACT.Sign)
                        ab = SC.o.activation(wot[:, cc * 128:(cc + 1) * 128],
                                             pb[:, :128], ACT.Abs)
                        vmark[("s", cc % 2)] = SC.mark(ab)
                        V.wait(SC, vmark[("s", cc % 2)])
                        if scale_col is not None:
                            V.o.tensor_scalar(wot[:, cc * 128:(cc + 1) * 128],
                                              wot[:, cc * 128:(cc + 1) * 128],
                                              qscal[:, thr_col:thr_col + 1],
                                              qscal[:, scale_col:scale_col + 1],
                                              ALU.is_gt, ALU.mult)
                        else:
                            V.o.tensor_scalar(wot[:, cc * 128:(cc + 1) * 128],
                                              wot[:, cc * 128:(cc + 1) * 128],
                                              qscal[:, thr_col:thr_col + 1],
                                              None, ALU.is_gt)
                        V.o.drain()
                        cp = V.o.tensor_tensor(
                            wot[:, cc * 128:(cc + 1) * 128],
                            wot[:, cc * 128:(cc + 1) * 128],
                            wot[:, 4096 + cc * 128:4096 + (cc + 1) * 128],
                            ALU.mult)
                        vmark[("v", cc % 2)] = V.mark(cp)
                    G.wait(V)
                    SG.put(dstT[:, rc * 128:(rc + 1) * 128]
                           .rearrange("(c p) w -> p c w", p=128),
                           wot[:, :cols].rearrange("p (c w) -> p c w", w=128))

            quantize_transposed(fw0, D, D, t0T, 2, None)

            def absmean_ternary(src, rows, cols, dstT, statcol):
                gsum_abs(src, rows, cols, statcol)
                cnt = float(rows * cols)
                V.o.drain()
                V.o.tensor_scalar(srow[:1, statcol + 1:statcol + 2],
                                  srow[:1, statcol:statcol + 1],
                                  1.0 / cnt, 1e-6, ALU.mult, ALU.add)
                V.o.drain()
                thr = V.o.tensor_scalar(srow[:1, statcol + 1:statcol + 2],
                                        srow[:1, statcol + 1:statcol + 2],
                                        0.5, None, ALU.mult)
                V.mark(thr)
                bcast_scalar(srow[:1, statcol + 1:statcol + 2],
                             qscal[:, 4:5], pst)
                quantize_transposed(src, rows, cols, dstT, 4, None)

            absmean_ternary(fcw1, DFF, D, tfcT, 8)
            absmean_ternary(pw2, D, DFF, tp2T, 10)
          barrier()

        def load_ln_transpose(src, s, tb, xblk, xnT, psA, psB, ldstream,
                              gamma_col=None, beta_col=None):
            t0 = s * S + tb * TBLK
            ldstream.eng.wait(V)
            ldstream.eng.wait(SC)
            ldstream.eng.wait(T)
            m = ldstream.put(xblk[:, :].rearrange("p (a d) -> p a d", d=D),
                             src[t0:t0 + TBLK, :]
                             .rearrange("(a p) d -> p a d", p=128))
            wait_dma(V, ldstream, m)
            wait_dma(SC, ldstream, m)
            SC.wait(T)
            SC.wait(V)
            for a in range(NTILE):
                c = xblk[:, a * D:(a + 1) * D]
                rd_ = V.o.tensor_reduce(stats[:, 16 + a:17 + a], c, axis=AXX,
                                        op=ALU.add)
                sq_ = SC.o.activation(dump[:, :1024], c, ACT.Square,
                                      accum_out=stats[:, 20 + a:21 + a])
            V.mark(rd_)
            SC.mark(sq_)
            V.wait(SC)
            V.o.drain()
            V.o.tensor_scalar(stats[:, 16:20], stats[:, 16:20], 1.0 / D,
                              None, ALU.mult)
            V.o.tensor_scalar(stats[:, 20:24], stats[:, 20:24], 1.0 / D,
                              None, ALU.mult)
            V.o.drain()
            V.o.tensor_tensor(stats[:, 24:28], stats[:, 16:20],
                              stats[:, 16:20], ALU.mult)
            V.o.drain()
            vv = V.o.tensor_tensor(stats[:, 20:24], stats[:, 20:24],
                                   stats[:, 24:28], ALU.subtract)
            V.mark(vv)
            SC.wait(V)
            sq = SC.o.activation(stats[:, 24:28], stats[:, 20:24], ACT.Sqrt,
                                 bias=EPS_LN)
            SC.mark(sq)
            V.wait(SC)
            rr = V.o.reciprocal(stats[:, 24:28], stats[:, 24:28])
            V.mark(rr)
            if LN_PARTS[0] < 2:
                return
            for a in range(NTILE):
                src_c = xblk[:, a * D:(a + 1) * D]
                V.wait(T)
                nrm = V.o.tensor_scalar(dump[:, :1024], src_c,
                                        stats[:, 16 + a:17 + a],
                                        stats[:, 24 + a:25 + a],
                                        ALU.subtract, ALU.mult)
                V.mark(nrm)
                if LN_PARTS[0] < 3:
                    continue
                for c in range(DC):
                    pb = psA if c % 2 == 0 else psB
                    T.wait(V)
                    tr = T.o.transpose(pb[:, :128], dump[:, c * 128:(c + 1) * 128],
                                       ident[:, :])
                    T.mark(tr)
                    V.wait(T)
                    cp = V.o.tensor_copy(
                        xnT[:, c * TBLK + a * 128:c * TBLK + (a + 1) * 128],
                        pb[:, :128])
                    V.mark(cp)
            if gamma_col is not None:
                for c in range(DC):
                    sc_ = V.o.tensor_scalar(xnT[:, c * TBLK:(c + 1) * TBLK],
                                            xnT[:, c * TBLK:(c + 1) * TBLK],
                                            gamma_col[:, c:c + 1],
                                            beta_col[:, c:c + 1],
                                            ALU.mult, ALU.add)
                    V.mark(sc_)

        def phase_A(s):
            pfx = f"a{s}_"
            with contextlib.ExitStack() as ph:
                KT = ph.enter_context(nc.sbuf_tensor(pfx + "KT", [128, DC * S], F32))
                QT = ph.enter_context(nc.sbuf_tensor(pfx + "QT", [128, DC * S], F32))
                xblk = ph.enter_context(nc.sbuf_tensor(pfx + "xblk", [128, NTILE * D], F32))
                xnT = ph.enter_context(nc.sbuf_tensor(pfx + "xnT", [128, DC * TBLK], F32))
                wrow = ph.enter_context(nc.sbuf_tensor(pfx + "wrow", [128, 2 * 1024], F32))
                brow = ph.enter_context(nc.sbuf_tensor(pfx + "brow", [1, 3 * D], F32))
                zall = ph.enter_context(nc.sbuf_tensor(pfx + "zall", [128, 16], F32))

                ps1 = contextlib.ExitStack()
                psA = ps1.enter_context(nc.psum_tensor(pfx + "psA", [128, 512], F32))
                psB = ps1.enter_context(nc.psum_tensor(pfx + "psB", [128, 512], F32))
                tmark_half = [0, 0]
                for tb in range(A1_NTB[0]):
                    load_ln_transpose(xin, s, tb, xblk, xnT, psA, psB, LY)
                    for fc in range(A1_NFC[0]):
                        dst = QT if fc < 8 else KT
                        freg = fc % 8
                        half = fc % 2
                        pb = psA if half == 0 else psB
                        Y.wait(T, tmark_half[half])
                        m = LW.put(wrow[:, half * 1024:half * 1024 + 128 * DC]
                                   .rearrange("p (c w) -> p c w", w=128),
                                   w0T[:, fc * 128:(fc + 1) * 128]
                                   .rearrange("(c p) w -> p c w", p=128))
                        wait_dma(T, LW, m)
                        T.wait(V)
                        for c in range(DC):
                            mm_ = T.o.matmul(pb[:, :512],
                                             wrow[:, half * 1024 + c * 128:
                                                  half * 1024 + (c + 1) * 128],
                                             xnT[:, c * TBLK:(c + 1) * TBLK],
                                             start=(c == 0), stop=(c == DC - 1))
                        tmark_half[half] = T.mark(mm_)
                        V.wait(T)
                        ev = V.o.tensor_scalar(
                            dst[:, freg * S + tb * TBLK:freg * S + (tb + 1) * TBLK],
                            pb[:, :512], b0sb[:, fc:fc + 1], None, ALU.add)
                        V.mark(ev)
                barrier()
                ps1.close()
                if A_STOP[0] < 2:
                    return

                with contextlib.ExitStack() as p2:
                    psS1 = p2.enter_context(nc.psum_tensor(pfx + "psS1", [128, S], F32))
                    psS2 = p2.enter_context(nc.psum_tensor(pfx + "psS2", [128, S], F32))
                    est = p2.enter_context(nc.sbuf_tensor(pfx + "est", [128, 2 * S], F32))
                    for jt in range(16):
                        pb = psS1 if jt % 2 == 0 else psS2
                        half = jt % 2
                        T.wait(SC)
                        T.wait(V)
                        for ig in range(4):
                            for c in range(DC):
                                mm_ = T.o.matmul(pb[:, ig * 512:(ig + 1) * 512],
                                                 KT[:, c * S + jt * 128:c * S + (jt + 1) * 128],
                                                 QT[:, c * S + ig * 512:c * S + (ig + 1) * 512],
                                                 start=(c == 0), stop=(c == DC - 1))
                        T.mark(mm_)
                        V.wait(T)
                        V.o.tensor_reduce(stats[:, 40:41], pb[:, :], axis=AXX,
                                          op=ALU.max)
                        V.o.drain()
                        nm = V.o.tensor_scalar(stats[:, 41:42], stats[:, 40:41],
                                               -1.0 / 1024.0, None, ALU.mult)
                        V.mark(nm)
                        SC.wait(V)
                        wait_dma(SC, SG)
                        ex = SC.o.activation(est[:, half * S:(half + 1) * S],
                                             pb[:, :], ACT.Exp,
                                             bias=stats[:, 41:42],
                                             scale=1.0 / 1024.0,
                                             accum_out=zall[:, jt:jt + 1])
                        SC.mark(ex)
                        G.wait(SC)
                        SG.put(epark[jt * 128:(jt + 1) * 128, :],
                               est[:, half * S:(half + 1) * S])
                    V.wait(SC)
                    rz = V.o.reciprocal(zall[:, :], zall[:, :])
                    V.mark(rz)
                barrier()
                if A_STOP[0] < 3:
                    return

                Vb = QT
                wvres = KT
                with contextlib.ExitStack() as p3:
                    psA = p3.enter_context(nc.psum_tensor(pfx + "psA2", [128, 512], F32))
                    psB = p3.enter_context(nc.psum_tensor(pfx + "psB2", [128, 512], F32))
                    m = LW.put(wvres[:, :DC * 1024]
                               .rearrange("p (c w) -> p c w", w=1024),
                               w0T[:, 2048:3072]
                               .rearrange("(c p) w -> p c w", p=128))
                    mrow = LY.put(brow[:1, :], b0r[:1, :])
                    for tb in range(NTB):
                        load_ln_transpose(xin, s, tb, xblk, xnT, psA, psB, LY)
                        wait_dma(T, LW, m)
                        wait_dma(T, LY, mrow)
                        for a in range(NTILE):
                            jt = tb * NTILE + a
                            for kb in range(2):
                                pb = psA if kb == 0 else psB
                                T.wait(V)
                                for c in range(DC):
                                    T.o.matmul(pb[:, :512],
                                               xnT[:, c * TBLK + a * 128:
                                                   c * TBLK + (a + 1) * 128],
                                               wvres[:, c * 1024 + kb * 512:
                                                     c * 1024 + (kb + 1) * 512],
                                               start=(c == 0), stop=False)
                                mm_ = T.o.matmul(pb[:, :512], onerow[:1, :],
                                                 brow[:1, 2048 + kb * 512:
                                                      2048 + (kb + 1) * 512],
                                                 start=False, stop=True)
                                T.mark(mm_)
                                V.wait(T)
                                ev = V.o.tensor_scalar(
                                    Vb[:, jt * 1024 + kb * 512:
                                       jt * 1024 + (kb + 1) * 512],
                                    pb[:, :512], zall[:, jt:jt + 1],
                                    None, ALU.mult)
                                V.mark(ev)
                    barrier()
                    if A_STOP[0] < 4:
                        return

                    EC = KT
                    OB = 16 * 512
                    for ib in range(4):
                        t0 = s * S + ib * TBLK
                        Y.wait(T)
                        for jt in range(16):
                            LY.put(EC[:, jt * 512:(jt + 1) * 512],
                                   epark[jt * 128:(jt + 1) * 128,
                                         ib * TBLK:(ib + 1) * TBLK])
                        G.wait(V)
                        mx_ = LG.put(xblk[:, :].rearrange("p (a d) -> p a d", d=D),
                                     xin[t0:t0 + TBLK, :]
                                     .rearrange("(a p) d -> p a d", p=128))
                        wait_dma(T, LY)
                        wait_dma(V, SG)
                        for u in range(4):
                            for kb in range(2):
                                pb = psA if (u * 2 + kb) % 2 == 0 else psB
                                T.wait(V)
                                for jt in range(16):
                                    mm_ = T.o.matmul(pb[:, :512],
                                                     EC[:, jt * 512 + u * 128:
                                                        jt * 512 + u * 128 + 128],
                                                     Vb[:, jt * 1024 + kb * 512:
                                                        jt * 1024 + (kb + 1) * 512],
                                                     start=(jt == 0), stop=(jt == 15))
                                T.mark(mm_)
                                V.wait(T)
                                wait_dma(V, LG, mx_)
                                ev = V.o.tensor_tensor(
                                    KT[:, OB + u * D + kb * 512:
                                       OB + u * D + (kb + 1) * 512],
                                    pb[:, :512],
                                    xblk[:, u * D + kb * 512:u * D + (kb + 1) * 512],
                                    ALU.add)
                                V.mark(ev)
                        G.wait(V)
                        SG.put(xpark[t0:t0 + TBLK, :]
                               .rearrange("(a p) d -> p a d", p=128),
                               KT[:, OB:OB + NTILE * D]
                               .rearrange("p (a d) -> p a d", d=D))
                    barrier()

        if "A" in phases:
            phase_A(0)
            if A_SEQS[0] > 1:
                phase_A(1)
        else:
            with nc.sbuf_tensor("cp_buf", [128, NTILE * D], F32) as sb:
                for s in range(NSEQ):
                    for tb in range(NTB):
                        t0 = s * S + tb * TBLK
                        wait_dma(Y, SG)
                        m = LY.put(sb[:, :].rearrange("p (a d) -> p a d", d=D),
                                   xin[t0:t0 + TBLK, :]
                                   .rearrange("(a p) d -> p a d", p=128))
                        wait_dma(G, LY, m)
                        SG.put(xpark[t0:t0 + TBLK, :]
                               .rearrange("(a p) d -> p a d", p=128),
                               sb[:, :].rearrange("p (a d) -> p a d", d=D))
            barrier()


        def phase_B():
            with contextlib.ExitStack() as ph:
                xblk = ph.enter_context(nc.sbuf_tensor("b_xblk", [128, NTILE * D], F32))
                xnT = ph.enter_context(nc.sbuf_tensor("b_xnT", [128, DC * TBLK], F32))
                zb = ph.enter_context(nc.sbuf_tensor("b_zb", [128, DC * TBLK], F32))
                abuf = ph.enter_context(nc.sbuf_tensor("b_abuf", [128, DC * TBLK], F32))
                sgn = ph.enter_context(nc.sbuf_tensor("b_sgn", [128, DC * TBLK], F32))
                ternsb = ph.enter_context(nc.sbuf_tensor("b_tern", [128, DC * 1024], BF16))
                nb = ph.enter_context(nc.sbuf_tensor("b_nb", [128, DC * TBLK], BF16))
                satl = ph.enter_context(nc.sbuf_tensor("b_satl", [128, 1], F32))
                outb = ph.enter_context(nc.sbuf_tensor("b_outb", [128, NTILE * D], F32))
                ccsb = ph.enter_context(nc.sbuf_tensor("b_ccsb", [1, 128], F32))
                sgst = ph.enter_context(nc.sbuf_tensor("b_sgst", [128, DC * TBLK], F32))
                gpbc = ph.enter_context(nc.sbuf_tensor("b_gpbc", [128, 16], F32))
                psA = ph.enter_context(nc.psum_tensor("b_psA", [128, 512], F32))
                psB = ph.enter_context(nc.psum_tensor("b_psB", [128, 512], F32))

                first = [True]
                for s in range(NSEQ):
                    for tb in range(NTB):
                        load_ln_transpose(xpark, s, tb, xblk, xnT, psA, psB, LY,
                                          gamma_col=g0sb, beta_col=be0sb)
                        V.o.drain()
                        V.o.tensor_reduce(stats[:, 48:49], xnT[:, :], axis=AXX,
                                          op=ALU.max, apply_absolute_value=True)
                        V.o.drain()
                        if first[0]:
                            mxcp = V.o.tensor_copy(stats[:, 49:50], stats[:, 48:49])
                            first[0] = False
                        else:
                            mxcp = V.o.tensor_tensor(stats[:, 49:50], stats[:, 49:50],
                                                     stats[:, 48:49], ALU.max)
                        V.mark(mxcp)
                        V.o.drain()
                col_to_scalar(stats[:, 49:50], srow[:1, 12:13], ALU.max, psA)
                V.o.drain()

                V.o.memset(ccsb[:1, :], 0.0)
                V.o.drain()
                c0 = V.o.tensor_copy(ccsb[:1, 0:1], srow[:1, 12:13])
                V.mark(c0)
                G.wait(V)
                SG.put(ccin.ap()[:1, :], ccsb[:1, :])
                wait_dma(G, SG)
                cc = G.o.collective_compute(
                    "AllReduce", ALU.max, replica_groups=[list(range(NCORE))],
                    ins=[ccin.ap().opt()], outs=[ccout.ap().opt()])
                G.mark(cc)
                G.wait(G)
                LG.put(ccsb[:1, :], ccout.ap()[:1, :])
                wait_dma(V, LG)
                V.o.tensor_reduce(srow[:1, 13:14], ccsb[:1, :], axis=AXX, op=ALU.max)
                V.o.drain()
                gm = V.o.tensor_scalar(srow[:1, 14:15], srow[:1, 13:14], 1e-8,
                                       None, ALU.max)
                V.mark(gm)
                bcast_scalar(srow[:1, 14:15], qscal[:, 5:6], psA)
                V.o.reciprocal(qscal[:, 6:7], scv[:, 0:1])
                V.o.drain()
                V.o.tensor_tensor(qscal[:, 0:1], qscal[:, 5:6], qscal[:, 6:7],
                                  ALU.mult)
                V.o.drain()
                V.o.reciprocal(qscal[:, 1:2], qscal[:, 0:1])
                V.o.drain()
                V.o.tensor_tensor(satl[:, 0:1], qscal[:, 0:1], qscal[:, 3:4],
                                  ALU.mult)
                V.o.drain()
                V.o.tensor_scalar(gpbc[:, 0:8], g0sb[:, :], qscal[:, 1:2],
                                  None, ALU.mult)
                gb = V.o.tensor_scalar(gpbc[:, 8:16], be0sb[:, :], qscal[:, 1:2],
                                       None, ALU.mult)
                V.mark(gb)
                V.o.drain()

                mt = LW.put(ternsb[:, :].rearrange("p (c w) -> p c w", w=1024),
                            t0T[:, :].rearrange("(c p) w -> p c w", p=128))

                for s in range(NSEQ):
                    for tb in range(NTB):
                        t0 = s * S + tb * TBLK
                        wait_dma(V, SG)
                        load_ln_transpose(xpark, s, tb, xblk, xnT, psA, psB, LY,
                                          gamma_col=gpbc[:, 0:8],
                                          beta_col=gpbc[:, 8:16])
                        SC.wait(V)
                        ab_ = SC.o.activation(abuf[:, :], xnT[:, :], ACT.Abs)
                        sg_ = SC.o.activation(sgn[:, :], xnT[:, :], ACT.Sign)
                        SC.mark(sg_)
                        V.wait(SC)
                        cz = V.o.tensor_copy(zb[:, :], abuf[:, :])
                        V.mark(cz)
                        for k in (64, 32, 16, 8, 4, 2, 1):
                            SC.wait(V)
                            sk = SC.o.activation(sgst[:, :], zb[:, :],
                                                 ACT.Sign, bias=-63.5)
                            SC.mark(sk)
                            V.wait(SC)
                            st1 = V.o.scalar_tensor_tensor(
                                zb[:, :], sgst[:, :], -k / 2.0,
                                zb[:, :], ALU.mult, ALU.add)
                            V.mark(st1)
                        V.o.drain()
                        V.o.scalar_tensor_tensor(abuf[:, :], abuf[:, :], 63.5,
                                                 zb[:, :], ALU.add, ALU.subtract)
                        V.o.drain()
                        nf = V.o.tensor_tensor(nb[:, :], abuf[:, :], sgn[:, :],
                                               ALU.mult)
                        V.mark(nf)
                        wait_dma(T, LW, mt)
                        for a in range(NTILE):
                            for kb in range(2):
                                pb = psA if kb == 0 else psB
                                T.wait(V)
                                for c in range(DC):
                                    mm_ = T.o.matmul(
                                        pb[:, :512],
                                        nb[:, c * TBLK + a * 128:
                                           c * TBLK + (a + 1) * 128],
                                        ternsb[:, c * 1024 + kb * 512:
                                               c * 1024 + (kb + 1) * 512],
                                        start=(c == 0), stop=(c == DC - 1))
                                T.mark(mm_)
                                V.wait(T)
                                ev = V.o.scalar_tensor_tensor(
                                    outb[:, a * D + kb * 512:a * D + (kb + 1) * 512],
                                    pb[:, :512], satl[:, 0:1],
                                    xblk[:, a * D + kb * 512:a * D + (kb + 1) * 512],
                                    ALU.mult, ALU.add)
                                V.mark(ev)
                        G.wait(V)
                        SG.put(xpark[t0:t0 + TBLK, :]
                               .rearrange("(a p) d -> p a d", p=128),
                               outb[:, :].rearrange("p (a d) -> p a d", d=D))
                barrier()

        if "B" in phases:
            phase_B()

        def phase_C(s):
            pfx = f"c{s}_"
            with contextlib.ExitStack() as ph:
                KT = ph.enter_context(nc.sbuf_tensor(pfx + "KT", [128, DC * S], F32))
                VA = ph.enter_context(nc.sbuf_tensor(pfx + "VA", [128, 16 * 1040], F32))
                QTb = ph.enter_context(nc.sbuf_tensor(pfx + "QTb", [128, DC * 512], F32))
                xblk = ph.enter_context(nc.sbuf_tensor(pfx + "xblk", [128, NTILE * D], F32))
                xnT = ph.enter_context(nc.sbuf_tensor(pfx + "xnT", [128, DC * TBLK], F32))
                attK = xnT
                wst = ph.enter_context(nc.sbuf_tensor(pfx + "wst", [128, 4096], F32))
                EB = ph.enter_context(nc.sbuf_tensor(pfx + "EB", [128, 1024], F32))
                knb = ph.enter_context(nc.sbuf_tensor(pfx + "knb", [128, 256], F32))
                brow1 = ph.enter_context(nc.sbuf_tensor(pfx + "brow1", [1, 1024], F32))
                pbr = brow1
                cb = stats[:, 0:16]
                rzb = stats[:, 52:60]
                attT = QTb

                mb1 = LY.put(brow1[:1, :], b1r[:1, 2048:3072])
                mpb_holder = [0]

                with contextlib.ExitStack() as p1:
                    psA = p1.enter_context(nc.psum_tensor(pfx + "psA", [128, 512], F32))
                    psB = p1.enter_context(nc.psum_tensor(pfx + "psB", [128, 512], F32))
                    for tb in range(NTB):
                        load_ln_transpose(xpark, s, tb, xblk, xnT, psA, psB, LY)
                        for fc in range(8, 16):
                            freg = fc - 8
                            pb = psA if fc % 2 == 0 else psB
                            Y.wait(T)
                            mw = LW.put(wst[:, :DC * 128]
                                        .rearrange("p (c w) -> p c w", w=128),
                                        w1T[:, fc * 128:(fc + 1) * 128]
                                        .rearrange("(c p) w -> p c w", p=128))
                            wait_dma(T, LW, mw)
                            T.wait(V)
                            for c in range(DC):
                                mm_ = T.o.matmul(pb[:, :512],
                                                 wst[:, c * 128:(c + 1) * 128],
                                                 xnT[:, c * TBLK:(c + 1) * TBLK],
                                                 start=(c == 0), stop=(c == DC - 1))
                            T.mark(mm_)
                            V.wait(T)
                            ev = V.o.tensor_scalar(
                                KT[:, freg * S + tb * TBLK:freg * S + (tb + 1) * TBLK],
                                pb[:, :512], b1sb[:, fc:fc + 1], None, ALU.add)
                            V.mark(ev)
                        wait_dma(T, LY, mb1)
                        for kb in range(2):
                            Y.wait(T)
                            mw = LW.put(wst[:, :].rearrange("p (c w) -> p c w", w=512),
                                        w1T[:, 2048 + kb * 512:2048 + (kb + 1) * 512]
                                        .rearrange("(c p) w -> p c w", p=128))
                            wait_dma(T, LW, mw)
                            for a in range(NTILE):
                                jt = tb * NTILE + a
                                pb = psA if a % 2 == 0 else psB
                                T.wait(V)
                                for c in range(DC):
                                    T.o.matmul(pb[:, :512],
                                               xnT[:, c * TBLK + a * 128:
                                                   c * TBLK + (a + 1) * 128],
                                               wst[:, c * 512:(c + 1) * 512],
                                               start=(c == 0), stop=False)
                                mm_ = T.o.matmul(pb[:, :512], onerow[:1, :],
                                                 brow1[:1, kb * 512:(kb + 1) * 512],
                                                 start=False, stop=True)
                                T.mark(mm_)
                                V.wait(T)
                                dst = bass.AP(VA, jt * 1040 + kb * 520,
                                              [[16 * 1040, 128], [65, 8], [1, 64]])
                                ev = V.o.tensor_copy(
                                    dst, pb[:, :512].rearrange("p (h w) -> p h w", w=64))
                                V.mark(ev)
                    for jt in range(16):
                        on = V.o.memset(
                            bass.AP(VA, jt * 1040 + 64, [[16 * 1040, 128], [65, 16]]),
                            1.0)
                        V.mark(on)
                    for c in range(DC):
                        for half in range(2):
                            SC.wait(T)
                            SC.wait(V)
                            sq_ = SC.o.activation(
                                dump[:, :1024],
                                KT[:, c * S + half * 1024:c * S + (half + 1) * 1024],
                                ACT.Square)
                            SC.mark(sq_)
                            T.wait(SC)
                            T.wait(V)
                            for hh in range(2):
                                for jj in range(8):
                                    jt = half * 8 + jj
                                    base = hh * 64
                                    mm_ = T.o.matmul(
                                        psA[:, jt:jt + 1] if hh == 0 else psB[:, jt:jt + 1],
                                        dump[base:base + 64, jj * 128:(jj + 1) * 128],
                                        scv[base:base + 64, 3:4],
                                        start=(half == 0 and jj == 0),
                                        stop=(half == 1 and jj == 7))
                                T.mark(mm_)
                        V.wait(T)
                        cv1 = V.o.tensor_copy(knb[:, (2 * c) * 16:(2 * c) * 16 + 16],
                                              psA[:, 0:16])
                        cv2 = V.o.tensor_copy(knb[:, (2 * c + 1) * 16:(2 * c + 1) * 16 + 16],
                                              psB[:, 0:16])
                        V.mark(cv2)
                    for h in range(16):
                        V.o.drain()
                        rk = V.o.tensor_reduce(stats[:, 50:51],
                                               knb[:, h * 16:(h + 1) * 16],
                                               axis=AXX, op=ALU.max)
                        V.mark(rk)
                        col_to_scalar(stats[:, 50:51], srow[:1, 16 + h:17 + h],
                                      ALU.max, psA)
                barrier()

                with contextlib.ExitStack() as p2:
                    psE1 = p2.enter_context(nc.psum_tensor(pfx + "psE1", [128, 512], F32))
                    psE2 = p2.enter_context(nc.psum_tensor(pfx + "psE2", [128, 512], F32))
                    pacA = p2.enter_context(nc.psum_tensor(pfx + "pacA", [128, 260], F32))
                    pacB = p2.enter_context(nc.psum_tensor(pfx + "pacB", [128, 260], F32))
                    psW = p2.enter_context(nc.psum_tensor(pfx + "psW", [128, 512], F32))
                    psX = p2.enter_context(nc.psum_tensor(pfx + "psX", [128, 512], F32))
                    for ib in range(4):
                        t0 = s * S + ib * TBLK
                        wait_dma(V, SG)
                        load_ln_transpose(xpark, s, ib, xblk, xnT, psW, psX, LY)
                        for fc in range(8):
                            pb = psW if fc % 2 == 0 else psX
                            Y.wait(T)
                            mw = LW.put(wst[:, :DC * 128]
                                        .rearrange("p (c w) -> p c w", w=128),
                                        w1T[:, fc * 128:(fc + 1) * 128]
                                        .rearrange("(c p) w -> p c w", p=128))
                            wait_dma(T, LW, mw)
                            T.wait(V)
                            for c in range(DC):
                                mm_ = T.o.matmul(pb[:, :512],
                                                 wst[:, c * 128:(c + 1) * 128],
                                                 xnT[:, c * TBLK:(c + 1) * TBLK],
                                                 start=(c == 0), stop=(c == DC - 1))
                            T.mark(mm_)
                            V.wait(T)
                            ev = V.o.tensor_scalar(
                                QTb[:, fc * 512:(fc + 1) * 512],
                                pb[:, :512], b1sb[:, fc:fc + 1], None, ALU.add)
                            V.mark(ev)
                        if C_STOP[0] == 1:
                            G.wait(V)
                            SG.put(epark[0:128, :], QTb[:, 0:2048])
                            SG.put(epark[128:256, :], xnT[:, 0:2048])
                            wait_dma(G, SG)
                            return
                        for c in range(DC):
                            SC.wait(T)
                            SC.wait(V)
                            sq_ = SC.o.activation(dump[:, :512],
                                                  QTb[:, c * 512:(c + 1) * 512],
                                                  ACT.Square)
                            SC.mark(sq_)
                            T.wait(SC)
                            for hh in range(2):
                                base = hh * 64
                                for u in range(4):
                                    mm_ = T.o.matmul(
                                        psW[:, (2 * c + hh) * 4 + u:
                                            (2 * c + hh) * 4 + u + 1],
                                        dump[base:base + 64, u * 128:(u + 1) * 128],
                                        scv[base:base + 64, 3:4],
                                        start=(c == 0 and hh == 0 and u == 0),
                                        stop=(c == DC - 1 and hh == 1 and u == 3))
                            T.mark(mm_)
                        for h in range(16):
                            V.wait(T)
                            rq = V.o.tensor_reduce(stats[:, 51:52],
                                                   psW[:, 4 * h:4 * h + 4],
                                                   axis=AXX, op=ALU.max)
                            V.mark(rq)
                            col_to_scalar(stats[:, 51:52], srow[:1, h:h + 1],
                                          ALU.max, psX)
                        V.o.drain()
                        tt_ = V.o.tensor_tensor(srow[:1, 40:56], srow[:1, 0:16],
                                                srow[:1, 16:32], ALU.mult)
                        V.mark(tt_)
                        SC.wait(V)
                        sq_ = SC.o.activation(srow[:1, 40:56], srow[:1, 40:56],
                                              ACT.Sqrt)
                        SC.mark(sq_)
                        V.wait(SC)
                        ng = V.o.tensor_scalar(srow[:1, 40:56], srow[:1, 40:56],
                                               -0.125, None, ALU.mult)
                        V.mark(ng)
                        V.o.drain()
                        bcast_scalar(srow[:1, 40:56], cb, psW)

                        if C_STOP[0] == 2:
                            G.wait(V)
                            SG.put(epark[0:1, 0:64], srow[:1, 0:64])
                            SG.put(epark[2:130, 0:16], cb)
                            SG.put(epark[256:384, :], KT[:, 0:2048])
                            wait_dma(G, SG)
                            return
                        sc_mark = [0, 0]
                        t_attn_mark = [0, 0]
                        exp_marks = {}
                        for hp in range(8):
                            hA, hB = 2 * hp, 2 * hp + 1
                            for step in range(17):
                                if step < 16:
                                    jt = step
                                    par = jt % 2
                                    sbA, sbB = (psE1, psE2) if par == 0 else (psW, psX)
                                    T.wait(SC, sc_mark[par])
                                    mmA = T.o.matmul(
                                        sbA[:, :512],
                                        KT[0:64, hp * S + jt * 128:hp * S + (jt + 1) * 128],
                                        QTb[0:64, hp * 512:(hp + 1) * 512],
                                        start=True, stop=True)
                                    mmB = T.o.matmul(
                                        sbB[:, :512],
                                        KT[64:128, hp * S + jt * 128:hp * S + (jt + 1) * 128],
                                        QTb[64:128, hp * 512:(hp + 1) * 512],
                                        start=True, stop=True)
                                    smark = T.mark(mmB)
                                    SC.wait(T, smark)
                                    SC.wait(T, t_attn_mark[par])
                                    EBp = EB if par == 0 else dump
                                    exA = SC.o.activation(
                                        EBp[:, 0:512],
                                        sbA[:, :], ACT.Exp, scale=0.125,
                                        bias=cb[:, hA:hA + 1])
                                    exB = SC.o.activation(
                                        EBp[:, 512:1024],
                                        sbB[:, :], ACT.Exp, scale=0.125,
                                        bias=cb[:, hB:hB + 1])
                                    sc_mark[par] = SC.mark(exB)
                                    exp_marks[jt] = sc_mark[par]
                                if step >= 1:
                                    jt = step - 1
                                    par = jt % 2
                                    T.wait(SC, exp_marks[jt])
                                    if jt == 0:
                                        T.wait(V)
                                    EBp = EB if par == 0 else dump
                                    for u in range(4):
                                        T.o.matmul(pacA[:, 65 * u:65 * u + 65],
                                                   EBp[:, u * 128:(u + 1) * 128],
                                                   bass.AP(VA, jt * 1040 + hA * 65,
                                                           [[16 * 1040, 128], [1, 65]]),
                                                   start=(jt == 0 and u == 0),
                                                   stop=(jt == 15 and u == 3))
                                        mm_ = T.o.matmul(
                                            pacB[:, 65 * u:65 * u + 65],
                                            EBp[:, 512 + u * 128:512 + (u + 1) * 128],
                                            bass.AP(VA, jt * 1040 + hB * 65,
                                                    [[16 * 1040, 128], [1, 65]]),
                                            start=(jt == 0 and u == 0),
                                            stop=(jt == 15 and u == 3))
                                    t_attn_mark[par] = T.mark(mm_)
                            V.wait(T)
                            r1 = V.o.reciprocal(rzb[:, 0:4],
                                                bass.AP(pacA, 64, [[260, 128], [65, 4]]))
                            r2 = V.o.reciprocal(rzb[:, 4:8],
                                                bass.AP(pacB, 64, [[260, 128], [65, 4]]))
                            V.mark(r2)
                            V.o.drain()
                            for u in range(4):
                                V.o.tensor_scalar(
                                    attK[:, u * 1024 + hA * 64:u * 1024 + hA * 64 + 64],
                                    pacA[:, 65 * u:65 * u + 64],
                                    rzb[:, u:u + 1], None, ALU.mult)
                                nv = V.o.tensor_scalar(
                                    attK[:, u * 1024 + hB * 64:u * 1024 + hB * 64 + 64],
                                    pacB[:, 65 * u:65 * u + 64],
                                    rzb[:, 4 + u:5 + u], None, ALU.mult)
                            V.mark(nv)
                            if C_STOP[0] == 3 and hp == 0:
                                pz = V.o.tensor_copy(dump[:, 0:260], pacA[:, :])
                                V.mark(pz)
                                G.wait(V)
                                SG.put(epark[0:128, 0:1024], EB[:, :])
                                SG.put(epark[128:256, 0:260], dump[:, 0:260])
                                SG.put(epark[256:384, 0:8], rzb)
                                SG.put(epark[384:512, 0:2048], attK[:, 0:2048])
                                SG.put(epark[512:513, 0:64], srow[:1, 0:64])
                                wait_dma(G, SG)
                                return

                        if ib == 0:
                            Y.wait(T)
                            mpb = LY.put(pbr[:1, :], pb1r[:1, :])
                            mpb_holder[0] = mpb
                        for u in range(NTILE):
                            for c in range(DC):
                                pb = psW if c % 2 == 0 else psX
                                T.wait(V)
                                tr = T.o.transpose(pb[:, :128],
                                                   attK[:, u * 1024 + c * 128:
                                                        u * 1024 + (c + 1) * 128],
                                                   ident[:, :])
                                T.mark(tr)
                                V.wait(T)
                                cp = V.o.tensor_copy(
                                    attT[:, c * 512 + u * 128:c * 512 + (u + 1) * 128],
                                    pb[:, :128])
                                V.mark(cp)
                        wait_dma(T, LY, mpb_holder[0])
                        for ob in range(2):
                            Y.wait(T)
                            mw = LW.put(wst[:, :].rearrange("p (c w) -> p c w", w=512),
                                        wp1T[:, ob * 512:(ob + 1) * 512]
                                        .rearrange("(c p) w -> p c w", p=128))
                            wait_dma(T, LW, mw)
                            for u in range(NTILE):
                                pb = psW if u % 2 == 0 else psX
                                T.wait(V)
                                for c in range(DC):
                                    T.o.matmul(pb[:, :512],
                                               attT[:, c * 512 + u * 128:
                                                    c * 512 + (u + 1) * 128],
                                               wst[:, c * 512:c * 512 + 512],
                                               start=(c == 0), stop=False)
                                mm_ = T.o.matmul(pb[:, :512], onerow[:1, :],
                                                 pbr[:1, ob * 512:(ob + 1) * 512],
                                                 start=False, stop=True)
                                T.mark(mm_)
                                V.wait(T)
                                ev = V.o.tensor_tensor(
                                    attK[:, u * 1024 + ob * 512:u * 1024 + (ob + 1) * 512],
                                    pb[:, :512],
                                    xblk[:, u * D + ob * 512:u * D + (ob + 1) * 512],
                                    ALU.add)
                                V.mark(ev)
                        G.wait(V)
                        SG.put(xpark[t0:t0 + TBLK, :]
                               .rearrange("(a p) d -> p a d", p=128),
                               attK[:, :].rearrange("p (a d) -> p a d", d=D))
                    barrier()

        if "C" in phases:
            phase_C(0)
            phase_C(1)

        def phase_D():
            with contextlib.ExitStack() as ph:
                xblk = ph.enter_context(nc.sbuf_tensor("d_xblk", [128, NTILE * D], F32))
                xnT = ph.enter_context(nc.sbuf_tensor("d_xnT", [128, DC * TBLK], F32))
                gbuf = ph.enter_context(nc.sbuf_tensor("d_gbuf", [128, 32 * 512], F32))
                wstD = ph.enter_context(nc.sbuf_tensor("d_wst", [128, 4096], F16))
                xh = ph.enter_context(nc.sbuf_tensor("d_xh", [128, DC * TBLK], F16))
                xl = ph.enter_context(nc.sbuf_tensor("d_xl", [128, DC * TBLK], F16))
                gh = ph.enter_context(nc.sbuf_tensor("d_gh", [128, 32 * 512], F16))
                gl = ph.enter_context(nc.sbuf_tensor("d_gl", [128, 32 * 512], F16))
                lnwcs = ph.enter_context(nc.sbuf_tensor("d_lnwc", [128, 64], F32))
                rhosb = ph.enter_context(nc.sbuf_tensor("d_rho", [128, 512], F32))
                mursb = ph.enter_context(nc.sbuf_tensor("d_mur", [128, 512], F32))
                rowsA = ph.enter_context(nc.sbuf_tensor("d_rowsA", [1, 512], F32))
                rowsB = ph.enter_context(nc.sbuf_tensor("d_rowsB", [1, 512], F32))
                trsb = ph.enter_context(nc.sbuf_tensor("d_trsb", [8, 128], F32))
                mlc = LY.put(lnwcs[:, :], lnwc[:, :])
                for s in range(NSEQ):
                    for tb in range(NTB):
                        t0 = s * S + tb * TBLK
                        with contextlib.ExitStack() as pp:
                            psA = pp.enter_context(
                                nc.psum_tensor(f"d_psA{s}{tb}", [128, 512], F32))
                            psB = pp.enter_context(
                                nc.psum_tensor(f"d_psB{s}{tb}", [128, 512], F32))
                            wait_dma(SC, SG)
                            wait_dma(V, SG)
                            load_ln_transpose(xpark, s, tb, xblk, xnT, psA, psB, LY)
                            V.o.tensor_copy(xh[:, :], xnT[:, :])
                            V.o.drain()
                            sp = V.o.tensor_tensor(xl[:, :], xnT[:, :], xh[:, :],
                                                   ALU.subtract)
                            V.mark(sp)
                            for og in range(8):
                                Y.wait(T)
                                mw = LW.put(wstD[:, :]
                                            .rearrange("p (c w) -> p c w", w=512),
                                            tfcT[:, og * 512:(og + 1) * 512]
                                            .rearrange("(c p) w -> p c w", p=128))
                                wait_dma(T, LW, mw)
                                for oo in range(4):
                                    oc = og * 4 + oo
                                    pb = psA if oc % 2 == 0 else psB
                                    T.wait(SC)
                                    T.wait(V)
                                    for c in range(DC):
                                        T.o.matmul(
                                            pb[:, :512],
                                            wstD[:, c * 512 + oo * 128:
                                                 c * 512 + oo * 128 + 128],
                                            xh[:, c * TBLK:(c + 1) * TBLK],
                                            start=(c == 0), stop=False)
                                        mm_ = T.o.matmul(
                                            pb[:, :512],
                                            wstD[:, c * 512 + oo * 128:
                                                 c * 512 + oo * 128 + 128],
                                            xl[:, c * TBLK:(c + 1) * TBLK],
                                            start=False, stop=(c == DC - 1))
                                    T.mark(mm_)
                                    SC.wait(T)
                                    ge = SC.o.activation(
                                        gbuf[:, oc * 512:(oc + 1) * 512],
                                        pb[:, :512], ACT.Gelu)
                                    SC.mark(ge)
                            for oc in range(32):
                                SC.wait(T)
                                sq_ = SC.o.activation(dump[:, :512],
                                                      gbuf[:, oc * 512:(oc + 1) * 512],
                                                      ACT.Square)
                                SC.mark(sq_)
                                T.wait(SC)
                                for u in range(4):
                                    T.o.matmul(psA[:, 32 + u:33 + u],
                                               gbuf[:, oc * 512 + u * 128:
                                                    oc * 512 + (u + 1) * 128],
                                               scv[:, 3:4],
                                               start=(oc == 0 and u == 0),
                                               stop=(oc == 31 and u == 3))
                                    mm_ = T.o.matmul(psB[:, 32 + u:33 + u],
                                                     dump[:, u * 128:(u + 1) * 128],
                                                     scv[:, 3:4],
                                                     start=(oc == 0 and u == 0),
                                                     stop=(oc == 31 and u == 3))
                                T.mark(mm_)
                            V.wait(T)
                            V.o.tensor_scalar(stats[:, 32:36], psA[:, 32:36],
                                              1.0 / DFF, None, ALU.mult)
                            V.o.tensor_scalar(stats[:, 36:40], psB[:, 32:36],
                                              1.0 / DFF, None, ALU.mult)
                            V.o.drain()
                            V.o.tensor_tensor(stats[:, 40:44], stats[:, 32:36],
                                              stats[:, 32:36], ALU.mult)
                            V.o.drain()
                            vv = V.o.tensor_tensor(stats[:, 36:40], stats[:, 36:40],
                                                   stats[:, 40:44], ALU.subtract)
                            V.mark(vv)
                            SC.wait(V)
                            sq_ = SC.o.activation(stats[:, 40:44], stats[:, 36:40],
                                                  ACT.Sqrt, bias=EPS_LN)
                            SC.mark(sq_)
                            V.wait(SC)
                            V.o.reciprocal(stats[:, 40:44], stats[:, 40:44])
                            V.o.drain()
                            V.o.tensor_tensor(stats[:, 44:48], stats[:, 32:36],
                                              stats[:, 40:44], ALU.mult)
                            V.o.drain()
                            mr = V.o.tensor_scalar(stats[:, 44:48], stats[:, 44:48],
                                                   -1.0, None, ALU.mult)
                            V.mark(mr)
                            T.wait(V)
                            tr = T.o.transpose(psA[0:8, 0:128], stats[:, 40:48],
                                               ident[:, :])
                            T.mark(tr)
                            V.wait(T)
                            cpr = V.o.tensor_copy(trsb[0:8, :], psA[0:8, 0:128])
                            V.mark(cpr)
                            G.wait(V)
                            mra = SG.put(rowsA[:1, :], trsb[0:4, :])
                            mrb = SG.put(rowsB[0:1, :], trsb[4:8, :])
                            wait_dma(T, SG, mrb)
                            wait_dma(V, LY, mlc)
                            T.wait(V)
                            bA = T.o.matmul(psA[:, :512], onerow[:1, :],
                                            rowsA[:1, :], start=True, stop=True)
                            bB = T.o.matmul(psB[:, :512], onerow[:1, :],
                                            rowsB[:1, :], start=True, stop=True)
                            bm = T.mark(bB)
                            V.wait(T, bm)
                            V.o.tensor_copy(rhosb[:, :], psA[:, :512])
                            bc = V.o.tensor_copy(mursb[:, :], psB[:, :512])
                            V.mark(bc)
                            for oc in range(32):
                                V.o.tensor_tensor(gbuf[:, oc * 512:(oc + 1) * 512],
                                                  gbuf[:, oc * 512:(oc + 1) * 512],
                                                  rhosb[:, :], ALU.mult)
                                V.o.drain()
                                V.o.tensor_tensor(gbuf[:, oc * 512:(oc + 1) * 512],
                                                  gbuf[:, oc * 512:(oc + 1) * 512],
                                                  mursb[:, :], ALU.add)
                                V.o.drain()
                                nv = V.o.tensor_scalar(
                                    gbuf[:, oc * 512:(oc + 1) * 512],
                                    gbuf[:, oc * 512:(oc + 1) * 512],
                                    lnwcs[:, oc:oc + 1],
                                    lnwcs[:, 32 + oc:33 + oc],
                                    ALU.mult, ALU.add)
                                V.mark(nv)
                        V.o.drain()
                        V.o.tensor_copy(gh[:, :], gbuf[:, :])
                        V.o.drain()
                        sp2 = V.o.tensor_tensor(gl[:, :], gbuf[:, :], gh[:, :],
                                                ALU.subtract)
                        V.mark(sp2)
                        with contextlib.ExitStack() as pp2:
                            pps = [pp2.enter_context(
                                nc.psum_tensor(f"d_pp{s}{tb}_{i}", [128, 512], F32))
                                for i in range(8)]
                            T.wait(V)
                            for og in range(8):
                                Y.wait(T)
                                mw = LW.put(wstD[:, :]
                                            .rearrange("p (c w) -> p c w", w=1024),
                                            tp2T[og * 512:(og + 1) * 512, :]
                                            .rearrange("(c p) w -> p c w", p=128))
                                wait_dma(T, LW, mw)
                                for oo in range(4):
                                    oc = og * 4 + oo
                                    for ob in range(2):
                                        for u in range(4):
                                            T.o.matmul(
                                                pps[ob * 4 + u][:, :512],
                                                gh[:, oc * 512 + u * 128:
                                                   oc * 512 + (u + 1) * 128],
                                                wstD[:, oo * 1024 + ob * 512:
                                                     oo * 1024 + (ob + 1) * 512],
                                                start=(oc == 0), stop=False)
                                            mm_ = T.o.matmul(
                                                pps[ob * 4 + u][:, :512],
                                                gl[:, oc * 512 + u * 128:
                                                   oc * 512 + (u + 1) * 128],
                                                wstD[:, oo * 1024 + ob * 512:
                                                     oo * 1024 + (ob + 1) * 512],
                                                start=False, stop=(oc == 31))
                                T.mark(mm_)
                            V.wait(T)
                            for ob in range(2):
                                for u in range(4):
                                    ev = V.o.tensor_tensor(
                                        gbuf[:, u * 1024 + ob * 512:
                                             u * 1024 + (ob + 1) * 512],
                                        pps[ob * 4 + u][:, :512],
                                        xblk[:, u * D + ob * 512:u * D + (ob + 1) * 512],
                                        ALU.add)
                            V.mark(ev)
                            G.wait(V)
                            SG.put(yout[t0:t0 + TBLK, :]
                                   .rearrange("(a p) d -> p a d", p=128),
                                   gbuf[:, :NTILE * D]
                                   .rearrange("p (a d) -> p a d", d=D))
                barrier()

        if "D" in phases:
            phase_D()

        if "D" not in phases:
          with nc.sbuf_tensor("st_buf", [128, NTILE * D], F32) as sb:
            for s in range(NSEQ):
                for tb in range(NTB):
                    t0 = s * S + tb * TBLK
                    wait_dma(Y, SG)
                    m = LY.put(sb[:, :].rearrange("p (a d) -> p a d", d=D),
                               xpark[t0:t0 + TBLK, :]
                               .rearrange("(a p) d -> p a d", p=128))
                    wait_dma(G, LY, m)
                    SG.put(yout[t0:t0 + TBLK, :]
                           .rearrange("(a p) d -> p a d", p=128),
                           sb[:, :].rearrange("p (a d) -> p a d", d=D))
          barrier()

        wait_dma(G, SG)
        wait_dma(Y, LY)

    return nc


_NC_CACHE = {}


def _get_program():
    if "nc" not in _NC_CACHE:
        _NC_CACHE["nc"] = build_program()
    return _NC_CACHE["nc"]


def make_in_maps(inputs_by_name):
    x = np.ascontiguousarray(np.asarray(inputs_by_name["L0_attn_x"],
                                        dtype=np.float32))
    shared = {
        "w0": np.ascontiguousarray(inputs_by_name["L0_attn_W"]),
        "b0c": np.ascontiguousarray(
            np.asarray(inputs_by_name["L0_attn_b"]).reshape(24, 128).T),
        "b0r": np.asarray(inputs_by_name["L0_attn_b"]).reshape(1, 3 * D),
        "fw0": np.ascontiguousarray(inputs_by_name["L0_ffn_weight"]),
        "g0c": np.ascontiguousarray(
            np.asarray(inputs_by_name["L0_ffn_gamma"]).reshape(DC, 128).T),
        "be0c": np.ascontiguousarray(
            np.asarray(inputs_by_name["L0_ffn_beta"]).reshape(DC, 128).T),
        "w1": np.ascontiguousarray(inputs_by_name["L1_attn_qkv_w"]),
        "b1c": np.ascontiguousarray(
            np.asarray(inputs_by_name["L1_attn_qkv_b"]).reshape(24, 128).T),
        "b1r": np.asarray(inputs_by_name["L1_attn_qkv_b"]).reshape(1, 3 * D),
        "pw1": np.ascontiguousarray(inputs_by_name["L1_attn_proj_w"]),
        "pb1r": np.asarray(inputs_by_name["L1_attn_proj_b"]).reshape(1, D),
        "fcw1": np.ascontiguousarray(inputs_by_name["L1_ffn_W_fc"]),
        "lnwb2": np.ascontiguousarray(
            np.stack([np.asarray(inputs_by_name["L1_ffn_ln_w"]),
                      np.asarray(inputs_by_name["L1_ffn_ln_b"])])),
        "lnwc": np.ascontiguousarray(np.concatenate(
            [np.asarray(inputs_by_name["L1_ffn_ln_w"]).reshape(32, 128).T,
             np.asarray(inputs_by_name["L1_ffn_ln_b"]).reshape(32, 128).T],
            axis=1)),
        "pw2": np.ascontiguousarray(inputs_by_name["L1_ffn_W_proj"]),
        "cident": np.eye(128, dtype=np.float32),
        "conerow": np.ones((1, 128), dtype=np.float32),
        "cvals": np.broadcast_to(np.array(
            [127.0, EPS_LN, -63.5, 1.0, 0.0, 0.0, 0.0, 0.0],
            dtype=np.float32), (128, 8)).copy(),
    }
    in_maps = []
    for c in range(NCORE):
        im = dict(shared)
        im["xin"] = np.ascontiguousarray(x[2 * c:2 * c + 2].reshape(NSEQ * S, D))
        in_maps.append(im)
    return in_maps


def kernel(L0_attn_x, L0_attn_W, L0_attn_b, L0_ffn_weight, L0_ffn_gamma,
           L0_ffn_beta, L1_attn_qkv_w, L1_attn_qkv_b, L1_attn_proj_w,
           L1_attn_proj_b, L1_ffn_W_fc, L1_ffn_ln_w, L1_ffn_ln_b,
           L1_ffn_W_proj):
    kw = dict(L0_attn_x=L0_attn_x, L0_attn_W=L0_attn_W, L0_attn_b=L0_attn_b,
              L0_ffn_weight=L0_ffn_weight, L0_ffn_gamma=L0_ffn_gamma,
              L0_ffn_beta=L0_ffn_beta, L1_attn_qkv_w=L1_attn_qkv_w,
              L1_attn_qkv_b=L1_attn_qkv_b, L1_attn_proj_w=L1_attn_proj_w,
              L1_attn_proj_b=L1_attn_proj_b, L1_ffn_W_fc=L1_ffn_W_fc,
              L1_ffn_ln_w=L1_ffn_ln_w, L1_ffn_ln_b=L1_ffn_ln_b,
              L1_ffn_W_proj=L1_ffn_W_proj)
    nc = _get_program()
    in_maps = make_in_maps(kw)
    res = run_bass_kernel_spmd(nc, in_maps, core_ids=list(range(NCORE)))
    out = np.concatenate(
        [np.asarray(r["y"]).reshape(NSEQ, S, D) for r in res.results], axis=0)
    return out.astype(np.float32)


# version 05
# speedup vs baseline: 1.4536x; 1.0423x over previous
import contextlib
import numpy as np

import concourse.bass as bass
import concourse.mybir as mybir
from concourse.bass_utils import run_bass_kernel_spmd

F32 = mybir.dt.float32
BF16 = mybir.dt.bfloat16
F16 = mybir.dt.float16
ALU = mybir.AluOpType
ACT = mybir.ActivationFunctionType
AXX = mybir.AxisListType.X

D = 1024
S = 2048
NSEQ = 2
NCORE = 8
DC = D // 128
TBLK = 512
NTB = S // TBLK
NTILE = TBLK // 128
H = 16
HD = 64
DFF = 4096
EPS_LN = 1e-5
ALL_PHASES = ("P", "A", "B", "C", "D")
A_STOP = [4]
A1_NTB = [NTB]
A1_NFC = [16]
LN_PARTS = [3]
DBG_EPARK = [False]
A_SEQS = [2]
C_STOP = [99]


class Eng:
    def __init__(self, obj, sem, name):
        self.o = obj
        self.sem = sem
        self.name = name
        self.n = 0
        self._seen = {}

    def mark(self, instr, k=1):
        assert instr is not None, "bare sem_inc fires before datapath drain"
        instr.then_inc(self.sem, k)
        self.n += k
        return self.n

    def wait(self, other, val=None):
        if val is None:
            val = other.n
        if val <= 0:
            return
        key = other.name
        if self._seen.get(key, -1) >= val:
            return
        self.o.wait_ge(other.sem, val)
        self._seen[key] = val


class DmaStream:

    def __init__(self, eng, sem, name):
        self.eng = eng
        self.sem = sem
        self.name = name
        self.n = 0

    def put(self, out, in_):
        self.eng.o.dma_start(out=out, in_=in_).then_inc(self.sem, 16)
        self.n += 16
        return self.n


def wait_dma(eng, st, val=None):
    if val is None:
        val = st.n
    if val <= 0:
        return
    key = "dma_" + st.name
    if eng._seen.get(key, -1) >= val:
        return
    eng.o.wait_ge(st.sem, val)
    eng._seen[key] = val


def build_program(phases=ALL_PHASES):
    phases = set(phases)
    nc = bass.Bass("TRN2", target_bir_lowering=False,
                   detect_race_conditions=False)

    xin = nc.dram_tensor("xin", [NSEQ * S, D], F32, kind="ExternalInput").ap()
    w0 = nc.dram_tensor("w0", [3 * D, D], F32, kind="ExternalInput").ap()
    b0c = nc.dram_tensor("b0c", [128, 24], F32, kind="ExternalInput").ap()
    b0r = nc.dram_tensor("b0r", [1, 3 * D], F32, kind="ExternalInput").ap()
    fw0 = nc.dram_tensor("fw0", [D, D], F32, kind="ExternalInput").ap()
    g0c = nc.dram_tensor("g0c", [128, DC], F32, kind="ExternalInput").ap()
    be0c = nc.dram_tensor("be0c", [128, DC], F32, kind="ExternalInput").ap()
    w1 = nc.dram_tensor("w1", [3 * D, D], F32, kind="ExternalInput").ap()
    b1c = nc.dram_tensor("b1c", [128, 24], F32, kind="ExternalInput").ap()
    b1r = nc.dram_tensor("b1r", [1, 3 * D], F32, kind="ExternalInput").ap()
    pw1 = nc.dram_tensor("pw1", [D, D], F32, kind="ExternalInput").ap()
    pb1r = nc.dram_tensor("pb1r", [1, D], F32, kind="ExternalInput").ap()
    fcw1 = nc.dram_tensor("fcw1", [DFF, D], F32, kind="ExternalInput").ap()
    lnwb2 = nc.dram_tensor("lnwb2", [2, DFF], F32, kind="ExternalInput").ap()
    lnwc = nc.dram_tensor("lnwc", [128, 64], F32, kind="ExternalInput").ap()
    pw2 = nc.dram_tensor("pw2", [D, DFF], F32, kind="ExternalInput").ap()
    cident = nc.dram_tensor("cident", [128, 128], F32, kind="ExternalInput").ap()
    conerow = nc.dram_tensor("conerow", [1, 128], F32, kind="ExternalInput").ap()
    cvals = nc.dram_tensor("cvals", [128, 8], F32, kind="ExternalInput").ap()
    yout = nc.dram_tensor("y", [NSEQ * S, D], F32, kind="ExternalOutput").ap()

    xpark = nc.dram_tensor("xpark", [NSEQ * S, D], F32,
                           kind="ExternalOutput" if DBG_EPARK[0] else "Internal").ap()
    epark = nc.dram_tensor("epark", [S, S], F32,
                           kind="ExternalOutput" if DBG_EPARK[0] else "Internal").ap()
    w0T = nc.dram_tensor("w0T", [D, 3 * D], F32).ap()
    w1T = nc.dram_tensor("w1T", [D, 3 * D], F32).ap()
    wp1T = nc.dram_tensor("wp1T", [D, D], F32).ap()
    t0T = nc.dram_tensor("t0T", [D, D], BF16).ap()
    tfcT = nc.dram_tensor("tfcT", [D, DFF], F16).ap()
    tp2T = nc.dram_tensor("tp2T", [DFF, D], F16).ap()
    ccin = nc.dram_tensor("ccin", [1, 128], F32)
    ccout = nc.dram_tensor("ccout", [1, 128], F32)

    es = contextlib.ExitStack()
    with es:
        semT = es.enter_context(nc.semaphore("semT"))
        semV = es.enter_context(nc.semaphore("semV"))
        semS = es.enter_context(nc.semaphore("semS"))
        semG = es.enter_context(nc.semaphore("semG"))
        semY = es.enter_context(nc.semaphore("semY"))
        semLY = es.enter_context(nc.semaphore("semLY"))
        semLW = es.enter_context(nc.semaphore("semLW"))
        semSG = es.enter_context(nc.semaphore("semSG"))
        semLG = es.enter_context(nc.semaphore("semLG"))
        ccsem = es.enter_context(nc.semaphore("ccsem"))

        es.enter_context(nc.Block())

        T = Eng(nc.tensor, semT, "T")
        V = Eng(nc.vector, semV, "V")
        SC = Eng(nc.scalar, semS, "S")
        G = Eng(nc.gpsimd, semG, "G")
        Y = Eng(nc.sync, semY, "Y")
        ALL_E = [T, V, SC, G, Y]

        LY = DmaStream(Y, semLY, "LY")
        LW = DmaStream(Y, semLW, "LW")
        SG = DmaStream(G, semSG, "SG")
        LG = DmaStream(G, semLG, "LG")

        def barrier():
            for e in ALL_E:
                for o in ALL_E:
                    if o is not e:
                        e.wait(o)
                for st_ in (LY, LW, SG, LG):
                    wait_dma(e, st_)

        ident = es.enter_context(nc.sbuf_tensor("ident", [128, 128], F32))
        onerow = es.enter_context(nc.sbuf_tensor("onerow", [1, 128], F32))
        dump = es.enter_context(nc.sbuf_tensor("dump", [128, 1024], F32))
        stats = es.enter_context(nc.sbuf_tensor("stats", [128, 64], F32))
        srow = es.enter_context(nc.sbuf_tensor("srow", [1, 64], F32))
        b0sb = es.enter_context(nc.sbuf_tensor("b0sb", [128, 24], F32))
        b1sb = es.enter_context(nc.sbuf_tensor("b1sb", [128, 24], F32))
        g0sb = es.enter_context(nc.sbuf_tensor("g0sb", [128, DC], F32))
        be0sb = es.enter_context(nc.sbuf_tensor("be0sb", [128, DC], F32))
        scv = es.enter_context(nc.sbuf_tensor("scv", [128, 8], F32))
        qscal = es.enter_context(nc.sbuf_tensor("qscal", [128, 8], F32))

        LY.put(ident[:, :], cident[:, :])
        LY.put(onerow[:1, :], conerow[:1, :])
        LY.put(b0sb[:, :], b0c[:, :])
        LY.put(b1sb[:, :], b1c[:, :])
        LY.put(g0sb[:, :], g0c[:, :])
        LY.put(be0sb[:, :], be0c[:, :])
        LY.put(scv[:, :], cvals[:, :])
        nc.const_aps.register(F32, EPS_LN, scv[:, 1:2])
        nc.const_aps.register(F32, -63.5, scv[:, 2:3])

        def col_to_scalar(col_ap, out_ap, op, ps):
            T.wait(V)
            tr = T.o.transpose(ps[:1, :128], col_ap, ident[:, :])
            T.mark(tr)
            V.wait(T)
            rd = V.o.tensor_reduce(out_ap, ps[:1, :128], axis=AXX, op=op)
            V.mark(rd)

        def bcast_scalar(src_ap, dst_ap, ps):
            w = src_ap.shape[-1]
            T.wait(V)
            mm = T.o.matmul(ps[:, :w], onerow[:1, :], src_ap,
                            start=True, stop=True)
            T.mark(mm)
            V.wait(T)
            cp = V.o.tensor_copy(dst_ap, ps[:, :w])
            V.mark(cp)

        if "P" in phases:
          with contextlib.ExitStack() as ps_:
            win = ps_.enter_context(nc.sbuf_tensor("p_win", [128, 4096], F32))
            wot = ps_.enter_context(nc.sbuf_tensor("p_wot", [128, 8192], F32))
            pst = ps_.enter_context(nc.psum_tensor("p_pst", [128, 512], F32))
            pst2 = ps_.enter_context(nc.psum_tensor("p_pst2", [128, 512], F32))

            def transpose_rows(src, src_rows, src_cols, dstT):
                nrc = src_rows // 128
                ncc = src_cols // 128
                evmark = {}
                for rc in range(nrc):
                    Y.wait(T)
                    m = LY.put(win[:, :src_cols], src[rc * 128:(rc + 1) * 128, :])
                    wait_dma(T, LY, m)
                    for cc in range(ncc):
                        pb = pst if cc % 2 == 0 else pst2
                        ev = V if cc % 2 == 0 else SC
                        if cc >= 2:
                            T.wait(ev, evmark.get(cc % 2, 0))
                        tr = T.o.transpose(pb[:, :128],
                                           win[:, cc * 128:(cc + 1) * 128],
                                           ident[:, :])
                        tm = T.mark(tr)
                        ev.wait(T, tm)
                        wait_dma(ev, SG)
                        if ev is V:
                            cp = V.o.tensor_copy(wot[:, cc * 128:(cc + 1) * 128],
                                                 pb[:, :128])
                        else:
                            cp = SC.o.copy(wot[:, cc * 128:(cc + 1) * 128],
                                           pb[:, :128])
                        evmark[cc % 2] = ev.mark(cp)
                    G.wait(V)
                    G.wait(SC)
                    SG.put(dstT[:, rc * 128:(rc + 1) * 128]
                           .rearrange("(c p) w -> p c w", p=128),
                           wot[:, :src_cols].rearrange("p (c w) -> p c w", w=128))

            transpose_rows(w0, 3 * D, D, w0T)
            transpose_rows(w1, 3 * D, D, w1T)
            transpose_rows(pw1, D, D, wp1T)

            def gsum_abs(src, rows, cols, out_col):
                nrc = rows // 128
                for rc in range(nrc):
                    Y.wait(V)
                    Y.wait(SC)
                    m = LY.put(win[:, :cols], src[rc * 128:(rc + 1) * 128, :])
                    wait_dma(V, LY, m)
                    V.o.tensor_reduce(stats[:, 62:63], win[:, :cols], axis=AXX,
                                      op=ALU.add, apply_absolute_value=True)
                    V.o.drain()
                    if rc == 0:
                        cp = V.o.tensor_copy(stats[:, out_col:out_col + 1],
                                             stats[:, 62:63])
                    else:
                        cp = V.o.tensor_tensor(stats[:, out_col:out_col + 1],
                                               stats[:, out_col:out_col + 1],
                                               stats[:, 62:63], ALU.add)
                    V.mark(cp)
                    V.o.drain()
                col_to_scalar(stats[:, out_col:out_col + 1],
                              srow[:1, out_col:out_col + 1], ALU.add, pst)

            gsum_abs(fw0, D, D, 0)
            V.o.drain()
            dl_ = V.o.tensor_scalar(srow[:1, 1:2], srow[:1, 0:1],
                                    0.7 / (2.0 ** 20), None, ALU.mult)
            V.mark(dl_)
            bcast_scalar(srow[:1, 1:2], qscal[:, 2:3], pst)
            for rc in range(D // 128):
                Y.wait(V)
                Y.wait(SC)
                m = LY.put(win[:, :D], fw0[rc * 128:(rc + 1) * 128, :])
                wait_dma(SC, LY, m)
                SC.wait(V)
                ab = SC.o.activation(wot[:, :D], win[:, :D], ACT.Abs)
                SC.mark(ab)
                wait_dma(V, LY, m)
                V.wait(SC)
                V.o.tensor_scalar(wot[:, 4096:4096 + D], wot[:, :D],
                                  qscal[:, 2:3], None, ALU.is_gt)
                V.o.drain()
                V.o.tensor_reduce(stats[:, 62:63], wot[:, 4096:4096 + D],
                                  axis=AXX, op=ALU.add)
                V.o.drain()
                if rc == 0:
                    V.o.tensor_copy(stats[:, 3:4], stats[:, 62:63])
                else:
                    V.o.tensor_tensor(stats[:, 3:4], stats[:, 3:4],
                                      stats[:, 62:63], ALU.add)
                V.o.tensor_tensor(wot[:, :D], wot[:, :D], wot[:, 4096:4096 + D],
                                  ALU.mult)
                V.o.drain()
                V.o.tensor_reduce(stats[:, 62:63], wot[:, :D], axis=AXX,
                                  op=ALU.add)
                V.o.drain()
                if rc == 0:
                    cp = V.o.tensor_copy(stats[:, 4:5], stats[:, 62:63])
                else:
                    cp = V.o.tensor_tensor(stats[:, 4:5], stats[:, 4:5],
                                           stats[:, 62:63], ALU.add)
                V.mark(cp)
                V.o.drain()
            col_to_scalar(stats[:, 3:4], srow[:1, 3:4], ALU.add, pst)
            col_to_scalar(stats[:, 4:5], srow[:1, 4:5], ALU.add, pst2)
            V.o.tensor_scalar(srow[:1, 5:6], srow[:1, 3:4], 1.0, None, ALU.max)
            V.o.drain()
            V.o.reciprocal(srow[:1, 5:6], srow[:1, 5:6])
            V.o.drain()
            al = V.o.tensor_tensor(srow[:1, 6:7], srow[:1, 4:5], srow[:1, 5:6],
                                   ALU.mult)
            V.mark(al)
            bcast_scalar(srow[:1, 6:7], qscal[:, 3:4], pst)

            def quantize_transposed(src, rows, cols, dstT, thr_col, scale_col):
                vmark = {}
                for rc in range(rows // 128):
                    Y.wait(T)
                    m = LY.put(win[:, :cols], src[rc * 128:(rc + 1) * 128, :])
                    wait_dma(T, LY, m)
                    for cc in range(cols // 128):
                        pb = pst if cc % 2 == 0 else pst2
                        if cc >= 2:
                            T.wait(SC, vmark.get(("s", cc % 2), 0))
                        tr = T.o.transpose(pb[:, :128],
                                           win[:, cc * 128:(cc + 1) * 128],
                                           ident[:, :])
                        tm = T.mark(tr)
                        SC.wait(T, tm)
                        wait_dma(SC, SG)
                        SC.wait(V, vmark.get(("v", cc % 2), 0))
                        sg = SC.o.activation(
                            wot[:, 4096 + cc * 128:4096 + (cc + 1) * 128],
                            pb[:, :128], ACT.Sign)
                        ab = SC.o.activation(wot[:, cc * 128:(cc + 1) * 128],
                                             pb[:, :128], ACT.Abs)
                        vmark[("s", cc % 2)] = SC.mark(ab)
                        V.wait(SC, vmark[("s", cc % 2)])
                        if scale_col is not None:
                            V.o.tensor_scalar(wot[:, cc * 128:(cc + 1) * 128],
                                              wot[:, cc * 128:(cc + 1) * 128],
                                              qscal[:, thr_col:thr_col + 1],
                                              qscal[:, scale_col:scale_col + 1],
                                              ALU.is_gt, ALU.mult)
                        else:
                            V.o.tensor_scalar(wot[:, cc * 128:(cc + 1) * 128],
                                              wot[:, cc * 128:(cc + 1) * 128],
                                              qscal[:, thr_col:thr_col + 1],
                                              None, ALU.is_gt)
                        V.o.drain()
                        cp = V.o.tensor_tensor(
                            wot[:, cc * 128:(cc + 1) * 128],
                            wot[:, cc * 128:(cc + 1) * 128],
                            wot[:, 4096 + cc * 128:4096 + (cc + 1) * 128],
                            ALU.mult)
                        vmark[("v", cc % 2)] = V.mark(cp)
                    G.wait(V)
                    SG.put(dstT[:, rc * 128:(rc + 1) * 128]
                           .rearrange("(c p) w -> p c w", p=128),
                           wot[:, :cols].rearrange("p (c w) -> p c w", w=128))

            quantize_transposed(fw0, D, D, t0T, 2, None)

            def absmean_ternary(src, rows, cols, dstT, statcol):
                gsum_abs(src, rows, cols, statcol)
                cnt = float(rows * cols)
                V.o.drain()
                V.o.tensor_scalar(srow[:1, statcol + 1:statcol + 2],
                                  srow[:1, statcol:statcol + 1],
                                  1.0 / cnt, 1e-6, ALU.mult, ALU.add)
                V.o.drain()
                thr = V.o.tensor_scalar(srow[:1, statcol + 1:statcol + 2],
                                        srow[:1, statcol + 1:statcol + 2],
                                        0.5, None, ALU.mult)
                V.mark(thr)
                bcast_scalar(srow[:1, statcol + 1:statcol + 2],
                             qscal[:, 4:5], pst)
                quantize_transposed(src, rows, cols, dstT, 4, None)

            absmean_ternary(fcw1, DFF, D, tfcT, 8)
            absmean_ternary(pw2, D, DFF, tp2T, 10)
          barrier()

        def load_ln_transpose(src, s, tb, xblk, xnT, psA, psB, ldstream,
                              gamma_col=None, beta_col=None):
            t0 = s * S + tb * TBLK
            ldstream.eng.wait(V)
            ldstream.eng.wait(SC)
            ldstream.eng.wait(T)
            m = ldstream.put(xblk[:, :].rearrange("p (a d) -> p a d", d=D),
                             src[t0:t0 + TBLK, :]
                             .rearrange("(a p) d -> p a d", p=128))
            wait_dma(V, ldstream, m)
            wait_dma(SC, ldstream, m)
            SC.wait(T)
            SC.wait(V)
            for a in range(NTILE):
                c = xblk[:, a * D:(a + 1) * D]
                rd_ = V.o.tensor_reduce(stats[:, 16 + a:17 + a], c, axis=AXX,
                                        op=ALU.add)
                sq_ = SC.o.activation(dump[:, :1024], c, ACT.Square,
                                      accum_out=stats[:, 20 + a:21 + a])
            V.mark(rd_)
            SC.mark(sq_)
            V.wait(SC)
            V.o.drain()
            V.o.tensor_scalar(stats[:, 16:20], stats[:, 16:20], 1.0 / D,
                              None, ALU.mult)
            V.o.tensor_scalar(stats[:, 20:24], stats[:, 20:24], 1.0 / D,
                              None, ALU.mult)
            V.o.drain()
            V.o.tensor_tensor(stats[:, 24:28], stats[:, 16:20],
                              stats[:, 16:20], ALU.mult)
            V.o.drain()
            vv = V.o.tensor_tensor(stats[:, 20:24], stats[:, 20:24],
                                   stats[:, 24:28], ALU.subtract)
            V.mark(vv)
            SC.wait(V)
            sq = SC.o.activation(stats[:, 24:28], stats[:, 20:24], ACT.Sqrt,
                                 bias=EPS_LN)
            SC.mark(sq)
            V.wait(SC)
            rr = V.o.reciprocal(stats[:, 24:28], stats[:, 24:28])
            V.mark(rr)
            if LN_PARTS[0] < 2:
                return
            for a in range(NTILE):
                src_c = xblk[:, a * D:(a + 1) * D]
                V.wait(T)
                nrm = V.o.tensor_scalar(dump[:, :1024], src_c,
                                        stats[:, 16 + a:17 + a],
                                        stats[:, 24 + a:25 + a],
                                        ALU.subtract, ALU.mult)
                V.mark(nrm)
                if LN_PARTS[0] < 3:
                    continue
                for c in range(DC):
                    pb = psA if c % 2 == 0 else psB
                    T.wait(V)
                    tr = T.o.transpose(pb[:, :128], dump[:, c * 128:(c + 1) * 128],
                                       ident[:, :])
                    T.mark(tr)
                    V.wait(T)
                    cp = V.o.tensor_copy(
                        xnT[:, c * TBLK + a * 128:c * TBLK + (a + 1) * 128],
                        pb[:, :128])
                    V.mark(cp)
            if gamma_col is not None:
                for c in range(DC):
                    sc_ = V.o.tensor_scalar(xnT[:, c * TBLK:(c + 1) * TBLK],
                                            xnT[:, c * TBLK:(c + 1) * TBLK],
                                            gamma_col[:, c:c + 1],
                                            beta_col[:, c:c + 1],
                                            ALU.mult, ALU.add)
                    V.mark(sc_)

        def phase_A(s):
            pfx = f"a{s}_"
            with contextlib.ExitStack() as ph:
                KT = ph.enter_context(nc.sbuf_tensor(pfx + "KT", [128, DC * S], F32))
                QT = ph.enter_context(nc.sbuf_tensor(pfx + "QT", [128, DC * S], F32))
                xblk = ph.enter_context(nc.sbuf_tensor(pfx + "xblk", [128, NTILE * D], F32))
                xnT = ph.enter_context(nc.sbuf_tensor(pfx + "xnT", [128, DC * TBLK], F32))
                wrow = ph.enter_context(nc.sbuf_tensor(pfx + "wrow", [128, 2 * 1024], F32))
                brow = ph.enter_context(nc.sbuf_tensor(pfx + "brow", [1, 3 * D], F32))
                zall = ph.enter_context(nc.sbuf_tensor(pfx + "zall", [128, 16], F32))

                ps1 = contextlib.ExitStack()
                psA = ps1.enter_context(nc.psum_tensor(pfx + "psA", [128, 512], F32))
                psB = ps1.enter_context(nc.psum_tensor(pfx + "psB", [128, 512], F32))
                tmark_half = [0, 0]
                for tb in range(A1_NTB[0]):
                    load_ln_transpose(xin, s, tb, xblk, xnT, psA, psB, LY)
                    for fc in range(A1_NFC[0]):
                        dst = QT if fc < 8 else KT
                        freg = fc % 8
                        half = fc % 2
                        pb = psA if half == 0 else psB
                        Y.wait(T, tmark_half[half])
                        m = LW.put(wrow[:, half * 1024:half * 1024 + 128 * DC]
                                   .rearrange("p (c w) -> p c w", w=128),
                                   w0T[:, fc * 128:(fc + 1) * 128]
                                   .rearrange("(c p) w -> p c w", p=128))
                        wait_dma(T, LW, m)
                        T.wait(V)
                        for c in range(DC):
                            mm_ = T.o.matmul(pb[:, :512],
                                             wrow[:, half * 1024 + c * 128:
                                                  half * 1024 + (c + 1) * 128],
                                             xnT[:, c * TBLK:(c + 1) * TBLK],
                                             start=(c == 0), stop=(c == DC - 1))
                        tmark_half[half] = T.mark(mm_)
                        V.wait(T)
                        ev = V.o.tensor_scalar(
                            dst[:, freg * S + tb * TBLK:freg * S + (tb + 1) * TBLK],
                            pb[:, :512], b0sb[:, fc:fc + 1], None, ALU.add)
                        V.mark(ev)
                barrier()
                ps1.close()
                if A_STOP[0] < 2:
                    return

                with contextlib.ExitStack() as p2:
                    psS1 = p2.enter_context(nc.psum_tensor(pfx + "psS1", [128, S], F32))
                    psS2 = p2.enter_context(nc.psum_tensor(pfx + "psS2", [128, S], F32))
                    est = p2.enter_context(nc.sbuf_tensor(pfx + "est", [128, 2 * S], F32))
                    for jt in range(16):
                        pb = psS1 if jt % 2 == 0 else psS2
                        half = jt % 2
                        T.wait(SC)
                        T.wait(V)
                        for ig in range(4):
                            for c in range(DC):
                                mm_ = T.o.matmul(pb[:, ig * 512:(ig + 1) * 512],
                                                 KT[:, c * S + jt * 128:c * S + (jt + 1) * 128],
                                                 QT[:, c * S + ig * 512:c * S + (ig + 1) * 512],
                                                 start=(c == 0), stop=(c == DC - 1))
                        T.mark(mm_)
                        V.wait(T)
                        V.o.tensor_reduce(stats[:, 40:41], pb[:, :], axis=AXX,
                                          op=ALU.max)
                        V.o.drain()
                        nm = V.o.tensor_scalar(stats[:, 41:42], stats[:, 40:41],
                                               -1.0 / 1024.0, None, ALU.mult)
                        V.mark(nm)
                        SC.wait(V)
                        wait_dma(SC, SG)
                        ex = SC.o.activation(est[:, half * S:(half + 1) * S],
                                             pb[:, :], ACT.Exp,
                                             bias=stats[:, 41:42],
                                             scale=1.0 / 1024.0,
                                             accum_out=zall[:, jt:jt + 1])
                        SC.mark(ex)
                        G.wait(SC)
                        SG.put(epark[jt * 128:(jt + 1) * 128, :],
                               est[:, half * S:(half + 1) * S])
                    V.wait(SC)
                    rz = V.o.reciprocal(zall[:, :], zall[:, :])
                    V.mark(rz)
                barrier()
                if A_STOP[0] < 3:
                    return

                Vb = QT
                wvres = KT
                with contextlib.ExitStack() as p3:
                    psA = p3.enter_context(nc.psum_tensor(pfx + "psA2", [128, 512], F32))
                    psB = p3.enter_context(nc.psum_tensor(pfx + "psB2", [128, 512], F32))
                    m = LW.put(wvres[:, :DC * 1024]
                               .rearrange("p (c w) -> p c w", w=1024),
                               w0T[:, 2048:3072]
                               .rearrange("(c p) w -> p c w", p=128))
                    mrow = LY.put(brow[:1, :], b0r[:1, :])
                    for tb in range(NTB):
                        load_ln_transpose(xin, s, tb, xblk, xnT, psA, psB, LY)
                        wait_dma(T, LW, m)
                        wait_dma(T, LY, mrow)
                        for a in range(NTILE):
                            jt = tb * NTILE + a
                            for kb in range(2):
                                pb = psA if kb == 0 else psB
                                T.wait(V)
                                for c in range(DC):
                                    T.o.matmul(pb[:, :512],
                                               xnT[:, c * TBLK + a * 128:
                                                   c * TBLK + (a + 1) * 128],
                                               wvres[:, c * 1024 + kb * 512:
                                                     c * 1024 + (kb + 1) * 512],
                                               start=(c == 0), stop=False)
                                mm_ = T.o.matmul(pb[:, :512], onerow[:1, :],
                                                 brow[:1, 2048 + kb * 512:
                                                      2048 + (kb + 1) * 512],
                                                 start=False, stop=True)
                                T.mark(mm_)
                                V.wait(T)
                                ev = V.o.tensor_scalar(
                                    Vb[:, jt * 1024 + kb * 512:
                                       jt * 1024 + (kb + 1) * 512],
                                    pb[:, :512], zall[:, jt:jt + 1],
                                    None, ALU.mult)
                                V.mark(ev)
                    barrier()
                    if A_STOP[0] < 4:
                        return

                    EC = KT
                    OB = 16 * 512
                    for ib in range(4):
                        t0 = s * S + ib * TBLK
                        Y.wait(T)
                        for jt in range(16):
                            LY.put(EC[:, jt * 512:(jt + 1) * 512],
                                   epark[jt * 128:(jt + 1) * 128,
                                         ib * TBLK:(ib + 1) * TBLK])
                        G.wait(V)
                        mx_ = LG.put(xblk[:, :].rearrange("p (a d) -> p a d", d=D),
                                     xin[t0:t0 + TBLK, :]
                                     .rearrange("(a p) d -> p a d", p=128))
                        wait_dma(T, LY)
                        wait_dma(V, SG)
                        for u in range(4):
                            for kb in range(2):
                                pb = psA if (u * 2 + kb) % 2 == 0 else psB
                                T.wait(V)
                                for jt in range(16):
                                    mm_ = T.o.matmul(pb[:, :512],
                                                     EC[:, jt * 512 + u * 128:
                                                        jt * 512 + u * 128 + 128],
                                                     Vb[:, jt * 1024 + kb * 512:
                                                        jt * 1024 + (kb + 1) * 512],
                                                     start=(jt == 0), stop=(jt == 15))
                                T.mark(mm_)
                                V.wait(T)
                                wait_dma(V, LG, mx_)
                                ev = V.o.tensor_tensor(
                                    KT[:, OB + u * D + kb * 512:
                                       OB + u * D + (kb + 1) * 512],
                                    pb[:, :512],
                                    xblk[:, u * D + kb * 512:u * D + (kb + 1) * 512],
                                    ALU.add)
                                V.mark(ev)
                        G.wait(V)
                        SG.put(xpark[t0:t0 + TBLK, :]
                               .rearrange("(a p) d -> p a d", p=128),
                               KT[:, OB:OB + NTILE * D]
                               .rearrange("p (a d) -> p a d", d=D))
                    barrier()

        if "A" in phases:
            phase_A(0)
            if A_SEQS[0] > 1:
                phase_A(1)
        else:
            with nc.sbuf_tensor("cp_buf", [128, NTILE * D], F32) as sb:
                for s in range(NSEQ):
                    for tb in range(NTB):
                        t0 = s * S + tb * TBLK
                        wait_dma(Y, SG)
                        m = LY.put(sb[:, :].rearrange("p (a d) -> p a d", d=D),
                                   xin[t0:t0 + TBLK, :]
                                   .rearrange("(a p) d -> p a d", p=128))
                        wait_dma(G, LY, m)
                        SG.put(xpark[t0:t0 + TBLK, :]
                               .rearrange("(a p) d -> p a d", p=128),
                               sb[:, :].rearrange("p (a d) -> p a d", d=D))
            barrier()


        def phase_B():
            with contextlib.ExitStack() as ph:
                xblk = ph.enter_context(nc.sbuf_tensor("b_xblk", [128, NTILE * D], F32))
                xnT = ph.enter_context(nc.sbuf_tensor("b_xnT", [128, DC * TBLK], F32))
                zb = ph.enter_context(nc.sbuf_tensor("b_zb", [128, DC * TBLK], F32))
                abuf = ph.enter_context(nc.sbuf_tensor("b_abuf", [128, DC * TBLK], F32))
                sgn = ph.enter_context(nc.sbuf_tensor("b_sgn", [128, DC * TBLK], F32))
                ternsb = ph.enter_context(nc.sbuf_tensor("b_tern", [128, DC * 1024], BF16))
                nb = ph.enter_context(nc.sbuf_tensor("b_nb", [128, DC * TBLK], BF16))
                satl = ph.enter_context(nc.sbuf_tensor("b_satl", [128, 1], F32))
                outb = ph.enter_context(nc.sbuf_tensor("b_outb", [128, NTILE * D], F32))
                ccsb = ph.enter_context(nc.sbuf_tensor("b_ccsb", [1, 128], F32))
                sgst = ph.enter_context(nc.sbuf_tensor("b_sgst", [128, DC * TBLK], F32))
                gpbc = ph.enter_context(nc.sbuf_tensor("b_gpbc", [128, 16], F32))
                psA = ph.enter_context(nc.psum_tensor("b_psA", [128, 512], F32))
                psB = ph.enter_context(nc.psum_tensor("b_psB", [128, 512], F32))

                first = [True]
                for s in range(NSEQ):
                    for tb in range(NTB):
                        load_ln_transpose(xpark, s, tb, xblk, xnT, psA, psB, LY,
                                          gamma_col=g0sb, beta_col=be0sb)
                        V.o.drain()
                        V.o.tensor_reduce(stats[:, 48:49], xnT[:, :], axis=AXX,
                                          op=ALU.max, apply_absolute_value=True)
                        V.o.drain()
                        if first[0]:
                            mxcp = V.o.tensor_copy(stats[:, 49:50], stats[:, 48:49])
                            first[0] = False
                        else:
                            mxcp = V.o.tensor_tensor(stats[:, 49:50], stats[:, 49:50],
                                                     stats[:, 48:49], ALU.max)
                        V.mark(mxcp)
                        V.o.drain()
                col_to_scalar(stats[:, 49:50], srow[:1, 12:13], ALU.max, psA)
                V.o.drain()

                V.o.memset(ccsb[:1, :], 0.0)
                V.o.drain()
                c0 = V.o.tensor_copy(ccsb[:1, 0:1], srow[:1, 12:13])
                V.mark(c0)
                G.wait(V)
                SG.put(ccin.ap()[:1, :], ccsb[:1, :])
                wait_dma(G, SG)
                cc = G.o.collective_compute(
                    "AllReduce", ALU.max, replica_groups=[list(range(NCORE))],
                    ins=[ccin.ap().opt()], outs=[ccout.ap().opt()])
                G.mark(cc)
                G.wait(G)
                LG.put(ccsb[:1, :], ccout.ap()[:1, :])
                wait_dma(V, LG)
                V.o.tensor_reduce(srow[:1, 13:14], ccsb[:1, :], axis=AXX, op=ALU.max)
                V.o.drain()
                gm = V.o.tensor_scalar(srow[:1, 14:15], srow[:1, 13:14], 1e-8,
                                       None, ALU.max)
                V.mark(gm)
                bcast_scalar(srow[:1, 14:15], qscal[:, 5:6], psA)
                V.o.reciprocal(qscal[:, 6:7], scv[:, 0:1])
                V.o.drain()
                V.o.tensor_tensor(qscal[:, 0:1], qscal[:, 5:6], qscal[:, 6:7],
                                  ALU.mult)
                V.o.drain()
                V.o.reciprocal(qscal[:, 1:2], qscal[:, 0:1])
                V.o.drain()
                V.o.tensor_tensor(satl[:, 0:1], qscal[:, 0:1], qscal[:, 3:4],
                                  ALU.mult)
                V.o.drain()
                V.o.tensor_scalar(gpbc[:, 0:8], g0sb[:, :], qscal[:, 1:2],
                                  None, ALU.mult)
                gb = V.o.tensor_scalar(gpbc[:, 8:16], be0sb[:, :], qscal[:, 1:2],
                                       None, ALU.mult)
                V.mark(gb)
                V.o.drain()

                mt = LW.put(ternsb[:, :].rearrange("p (c w) -> p c w", w=1024),
                            t0T[:, :].rearrange("(c p) w -> p c w", p=128))

                for s in range(NSEQ):
                    for tb in range(NTB):
                        t0 = s * S + tb * TBLK
                        wait_dma(V, SG)
                        load_ln_transpose(xpark, s, tb, xblk, xnT, psA, psB, LY,
                                          gamma_col=gpbc[:, 0:8],
                                          beta_col=gpbc[:, 8:16])
                        SC.wait(V)
                        ab_ = SC.o.activation(abuf[:, :], xnT[:, :], ACT.Abs)
                        sg_ = SC.o.activation(sgn[:, :], xnT[:, :], ACT.Sign)
                        SC.mark(sg_)
                        V.wait(SC)
                        cz = V.o.tensor_copy(zb[:, :], abuf[:, :])
                        V.mark(cz)
                        for k in (64, 32, 16, 8, 4, 2, 1):
                            SC.wait(V)
                            sk = SC.o.activation(sgst[:, :], zb[:, :],
                                                 ACT.Sign, bias=-63.5)
                            SC.mark(sk)
                            V.wait(SC)
                            st1 = V.o.scalar_tensor_tensor(
                                zb[:, :], sgst[:, :], -k / 2.0,
                                zb[:, :], ALU.mult, ALU.add)
                            V.mark(st1)
                        V.o.drain()
                        V.o.scalar_tensor_tensor(abuf[:, :], abuf[:, :], 63.5,
                                                 zb[:, :], ALU.add, ALU.subtract)
                        V.o.drain()
                        nf = V.o.tensor_tensor(nb[:, :], abuf[:, :], sgn[:, :],
                                               ALU.mult)
                        V.mark(nf)
                        wait_dma(T, LW, mt)
                        for a in range(NTILE):
                            for kb in range(2):
                                pb = psA if kb == 0 else psB
                                T.wait(V)
                                for c in range(DC):
                                    mm_ = T.o.matmul(
                                        pb[:, :512],
                                        nb[:, c * TBLK + a * 128:
                                           c * TBLK + (a + 1) * 128],
                                        ternsb[:, c * 1024 + kb * 512:
                                               c * 1024 + (kb + 1) * 512],
                                        start=(c == 0), stop=(c == DC - 1))
                                T.mark(mm_)
                                V.wait(T)
                                ev = V.o.scalar_tensor_tensor(
                                    outb[:, a * D + kb * 512:a * D + (kb + 1) * 512],
                                    pb[:, :512], satl[:, 0:1],
                                    xblk[:, a * D + kb * 512:a * D + (kb + 1) * 512],
                                    ALU.mult, ALU.add)
                                V.mark(ev)
                        G.wait(V)
                        SG.put(xpark[t0:t0 + TBLK, :]
                               .rearrange("(a p) d -> p a d", p=128),
                               outb[:, :].rearrange("p (a d) -> p a d", d=D))
                barrier()

        if "B" in phases:
            phase_B()

        def phase_C(s):
            pfx = f"c{s}_"
            with contextlib.ExitStack() as ph:
                KT = ph.enter_context(nc.sbuf_tensor(pfx + "KT", [128, DC * S], F32))
                VA = ph.enter_context(nc.sbuf_tensor(pfx + "VA", [128, 16 * 1040], F32))
                QTb = ph.enter_context(nc.sbuf_tensor(pfx + "QTb", [128, DC * 512], F32))
                xblk = ph.enter_context(nc.sbuf_tensor(pfx + "xblk", [128, NTILE * D], F32))
                xnT = ph.enter_context(nc.sbuf_tensor(pfx + "xnT", [128, DC * TBLK], F32))
                attK = xnT
                wst = ph.enter_context(nc.sbuf_tensor(pfx + "wst", [128, 4096], F32))
                EB = ph.enter_context(nc.sbuf_tensor(pfx + "EB", [128, 1024], F32))
                knb = ph.enter_context(nc.sbuf_tensor(pfx + "knb", [128, 256], F32))
                brow1 = ph.enter_context(nc.sbuf_tensor(pfx + "brow1", [1, 1024], F32))
                pbr = brow1
                cb = stats[:, 0:16]
                rzb = stats[:, 52:60]
                attT = QTb

                mb1 = LY.put(brow1[:1, :], b1r[:1, 2048:3072])
                mpb_holder = [0]

                with contextlib.ExitStack() as p1:
                    psA = p1.enter_context(nc.psum_tensor(pfx + "psA", [128, 512], F32))
                    psB = p1.enter_context(nc.psum_tensor(pfx + "psB", [128, 512], F32))
                    for tb in range(NTB):
                        load_ln_transpose(xpark, s, tb, xblk, xnT, psA, psB, LY)
                        for fc in range(8, 16):
                            freg = fc - 8
                            pb = psA if fc % 2 == 0 else psB
                            Y.wait(T)
                            mw = LW.put(wst[:, :DC * 128]
                                        .rearrange("p (c w) -> p c w", w=128),
                                        w1T[:, fc * 128:(fc + 1) * 128]
                                        .rearrange("(c p) w -> p c w", p=128))
                            wait_dma(T, LW, mw)
                            T.wait(V)
                            for c in range(DC):
                                mm_ = T.o.matmul(pb[:, :512],
                                                 wst[:, c * 128:(c + 1) * 128],
                                                 xnT[:, c * TBLK:(c + 1) * TBLK],
                                                 start=(c == 0), stop=(c == DC - 1))
                            T.mark(mm_)
                            V.wait(T)
                            ev = V.o.tensor_scalar(
                                KT[:, freg * S + tb * TBLK:freg * S + (tb + 1) * TBLK],
                                pb[:, :512], b1sb[:, fc:fc + 1], None, ALU.add)
                            V.mark(ev)
                        wait_dma(T, LY, mb1)
                        for kb in range(2):
                            Y.wait(T)
                            mw = LW.put(wst[:, :].rearrange("p (c w) -> p c w", w=512),
                                        w1T[:, 2048 + kb * 512:2048 + (kb + 1) * 512]
                                        .rearrange("(c p) w -> p c w", p=128))
                            wait_dma(T, LW, mw)
                            for a in range(NTILE):
                                jt = tb * NTILE + a
                                pb = psA if a % 2 == 0 else psB
                                T.wait(V)
                                for c in range(DC):
                                    T.o.matmul(pb[:, :512],
                                               xnT[:, c * TBLK + a * 128:
                                                   c * TBLK + (a + 1) * 128],
                                               wst[:, c * 512:(c + 1) * 512],
                                               start=(c == 0), stop=False)
                                mm_ = T.o.matmul(pb[:, :512], onerow[:1, :],
                                                 brow1[:1, kb * 512:(kb + 1) * 512],
                                                 start=False, stop=True)
                                T.mark(mm_)
                                V.wait(T)
                                dst = bass.AP(VA, jt * 1040 + kb * 520,
                                              [[16 * 1040, 128], [65, 8], [1, 64]])
                                ev = V.o.tensor_copy(
                                    dst, pb[:, :512].rearrange("p (h w) -> p h w", w=64))
                                V.mark(ev)
                    for jt in range(16):
                        on = V.o.memset(
                            bass.AP(VA, jt * 1040 + 64, [[16 * 1040, 128], [65, 16]]),
                            1.0)
                        V.mark(on)
                    for c in range(DC):
                        for half in range(2):
                            SC.wait(T)
                            SC.wait(V)
                            sq_ = SC.o.activation(
                                dump[:, :1024],
                                KT[:, c * S + half * 1024:c * S + (half + 1) * 1024],
                                ACT.Square)
                            SC.mark(sq_)
                            T.wait(SC)
                            T.wait(V)
                            for hh in range(2):
                                for jj in range(8):
                                    jt = half * 8 + jj
                                    base = hh * 64
                                    mm_ = T.o.matmul(
                                        psA[:, jt:jt + 1] if hh == 0 else psB[:, jt:jt + 1],
                                        dump[base:base + 64, jj * 128:(jj + 1) * 128],
                                        scv[base:base + 64, 3:4],
                                        start=(half == 0 and jj == 0),
                                        stop=(half == 1 and jj == 7))
                                T.mark(mm_)
                        V.wait(T)
                        cv1 = V.o.tensor_copy(knb[:, (2 * c) * 16:(2 * c) * 16 + 16],
                                              psA[:, 0:16])
                        cv2 = V.o.tensor_copy(knb[:, (2 * c + 1) * 16:(2 * c + 1) * 16 + 16],
                                              psB[:, 0:16])
                        V.mark(cv2)
                    for h in range(16):
                        V.o.drain()
                        rk = V.o.tensor_reduce(stats[:, 50:51],
                                               knb[:, h * 16:(h + 1) * 16],
                                               axis=AXX, op=ALU.max)
                        V.mark(rk)
                        col_to_scalar(stats[:, 50:51], srow[:1, 16 + h:17 + h],
                                      ALU.max, psA)
                barrier()

                with contextlib.ExitStack() as p2:
                    psE1 = p2.enter_context(nc.psum_tensor(pfx + "psE1", [128, 512], F32))
                    psE2 = p2.enter_context(nc.psum_tensor(pfx + "psE2", [128, 512], F32))
                    pacA = p2.enter_context(nc.psum_tensor(pfx + "pacA", [128, 260], F32))
                    pacB = p2.enter_context(nc.psum_tensor(pfx + "pacB", [128, 260], F32))
                    psW = p2.enter_context(nc.psum_tensor(pfx + "psW", [128, 512], F32))
                    psX = p2.enter_context(nc.psum_tensor(pfx + "psX", [128, 512], F32))
                    for ib in range(4):
                        t0 = s * S + ib * TBLK
                        wait_dma(V, SG)
                        load_ln_transpose(xpark, s, ib, xblk, xnT, psW, psX, LY)
                        for fc in range(8):
                            pb = psW if fc % 2 == 0 else psX
                            Y.wait(T)
                            mw = LW.put(wst[:, :DC * 128]
                                        .rearrange("p (c w) -> p c w", w=128),
                                        w1T[:, fc * 128:(fc + 1) * 128]
                                        .rearrange("(c p) w -> p c w", p=128))
                            wait_dma(T, LW, mw)
                            T.wait(V)
                            for c in range(DC):
                                mm_ = T.o.matmul(pb[:, :512],
                                                 wst[:, c * 128:(c + 1) * 128],
                                                 xnT[:, c * TBLK:(c + 1) * TBLK],
                                                 start=(c == 0), stop=(c == DC - 1))
                            T.mark(mm_)
                            V.wait(T)
                            ev = V.o.tensor_scalar(
                                QTb[:, fc * 512:(fc + 1) * 512],
                                pb[:, :512], b1sb[:, fc:fc + 1], None, ALU.add)
                            V.mark(ev)
                        if C_STOP[0] == 1:
                            G.wait(V)
                            SG.put(epark[0:128, :], QTb[:, 0:2048])
                            SG.put(epark[128:256, :], xnT[:, 0:2048])
                            wait_dma(G, SG)
                            return
                        for c in range(DC):
                            SC.wait(T)
                            SC.wait(V)
                            sq_ = SC.o.activation(dump[:, :512],
                                                  QTb[:, c * 512:(c + 1) * 512],
                                                  ACT.Square)
                            SC.mark(sq_)
                            T.wait(SC)
                            for hh in range(2):
                                base = hh * 64
                                for u in range(4):
                                    mm_ = T.o.matmul(
                                        psW[:, (2 * c + hh) * 4 + u:
                                            (2 * c + hh) * 4 + u + 1],
                                        dump[base:base + 64, u * 128:(u + 1) * 128],
                                        scv[base:base + 64, 3:4],
                                        start=(c == 0 and hh == 0 and u == 0),
                                        stop=(c == DC - 1 and hh == 1 and u == 3))
                            T.mark(mm_)
                        for h in range(16):
                            V.wait(T)
                            rq = V.o.tensor_reduce(stats[:, 51:52],
                                                   psW[:, 4 * h:4 * h + 4],
                                                   axis=AXX, op=ALU.max)
                            V.mark(rq)
                            col_to_scalar(stats[:, 51:52], srow[:1, h:h + 1],
                                          ALU.max, psX)
                        V.o.drain()
                        tt_ = V.o.tensor_tensor(srow[:1, 40:56], srow[:1, 0:16],
                                                srow[:1, 16:32], ALU.mult)
                        V.mark(tt_)
                        SC.wait(V)
                        sq_ = SC.o.activation(srow[:1, 40:56], srow[:1, 40:56],
                                              ACT.Sqrt)
                        SC.mark(sq_)
                        V.wait(SC)
                        ng = V.o.tensor_scalar(srow[:1, 40:56], srow[:1, 40:56],
                                               -0.125, None, ALU.mult)
                        V.mark(ng)
                        V.o.drain()
                        bcast_scalar(srow[:1, 40:56], cb, psW)

                        if C_STOP[0] == 2:
                            G.wait(V)
                            SG.put(epark[0:1, 0:64], srow[:1, 0:64])
                            SG.put(epark[2:130, 0:16], cb)
                            SG.put(epark[256:384, :], KT[:, 0:2048])
                            wait_dma(G, SG)
                            return
                        sc_mark = [0, 0]
                        t_attn_mark = [0, 0]
                        exp_marks = {}
                        for hp in range(8):
                            hA, hB = 2 * hp, 2 * hp + 1
                            for step in range(17):
                                if step < 16:
                                    jt = step
                                    par = jt % 2
                                    sbA, sbB = (psE1, psE2) if par == 0 else (psW, psX)
                                    T.wait(SC, sc_mark[par])
                                    mmA = T.o.matmul(
                                        sbA[:, :512],
                                        KT[0:64, hp * S + jt * 128:hp * S + (jt + 1) * 128],
                                        QTb[0:64, hp * 512:(hp + 1) * 512],
                                        start=True, stop=True)
                                    mmB = T.o.matmul(
                                        sbB[:, :512],
                                        KT[64:128, hp * S + jt * 128:hp * S + (jt + 1) * 128],
                                        QTb[64:128, hp * 512:(hp + 1) * 512],
                                        start=True, stop=True)
                                    smark = T.mark(mmB)
                                    SC.wait(T, smark)
                                    SC.wait(T, t_attn_mark[par])
                                    EBp = EB if par == 0 else dump
                                    exA = SC.o.activation(
                                        EBp[:, 0:512],
                                        sbA[:, :], ACT.Exp, scale=0.125,
                                        bias=cb[:, hA:hA + 1])
                                    exB = SC.o.activation(
                                        EBp[:, 512:1024],
                                        sbB[:, :], ACT.Exp, scale=0.125,
                                        bias=cb[:, hB:hB + 1])
                                    sc_mark[par] = SC.mark(exB)
                                    exp_marks[jt] = sc_mark[par]
                                if step >= 1:
                                    jt = step - 1
                                    par = jt % 2
                                    T.wait(SC, exp_marks[jt])
                                    if jt == 0:
                                        T.wait(V)
                                    EBp = EB if par == 0 else dump
                                    for u in range(4):
                                        T.o.matmul(pacA[:, 65 * u:65 * u + 65],
                                                   EBp[:, u * 128:(u + 1) * 128],
                                                   bass.AP(VA, jt * 1040 + hA * 65,
                                                           [[16 * 1040, 128], [1, 65]]),
                                                   start=(jt == 0 and u == 0),
                                                   stop=(jt == 15 and u == 3))
                                        mm_ = T.o.matmul(
                                            pacB[:, 65 * u:65 * u + 65],
                                            EBp[:, 512 + u * 128:512 + (u + 1) * 128],
                                            bass.AP(VA, jt * 1040 + hB * 65,
                                                    [[16 * 1040, 128], [1, 65]]),
                                            start=(jt == 0 and u == 0),
                                            stop=(jt == 15 and u == 3))
                                    t_attn_mark[par] = T.mark(mm_)
                            V.wait(T)
                            r1 = V.o.reciprocal(rzb[:, 0:4],
                                                bass.AP(pacA, 64, [[260, 128], [65, 4]]))
                            r2 = V.o.reciprocal(rzb[:, 4:8],
                                                bass.AP(pacB, 64, [[260, 128], [65, 4]]))
                            V.mark(r2)
                            V.o.drain()
                            for u in range(4):
                                V.o.tensor_scalar(
                                    attK[:, u * 1024 + hA * 64:u * 1024 + hA * 64 + 64],
                                    pacA[:, 65 * u:65 * u + 64],
                                    rzb[:, u:u + 1], None, ALU.mult)
                                nv = V.o.tensor_scalar(
                                    attK[:, u * 1024 + hB * 64:u * 1024 + hB * 64 + 64],
                                    pacB[:, 65 * u:65 * u + 64],
                                    rzb[:, 4 + u:5 + u], None, ALU.mult)
                            V.mark(nv)
                            if C_STOP[0] == 3 and hp == 0:
                                pz = V.o.tensor_copy(dump[:, 0:260], pacA[:, :])
                                V.mark(pz)
                                G.wait(V)
                                SG.put(epark[0:128, 0:1024], EB[:, :])
                                SG.put(epark[128:256, 0:260], dump[:, 0:260])
                                SG.put(epark[256:384, 0:8], rzb)
                                SG.put(epark[384:512, 0:2048], attK[:, 0:2048])
                                SG.put(epark[512:513, 0:64], srow[:1, 0:64])
                                wait_dma(G, SG)
                                return

                        if ib == 0:
                            Y.wait(T)
                            mpb = LY.put(pbr[:1, :], pb1r[:1, :])
                            mpb_holder[0] = mpb
                        for u in range(NTILE):
                            for c in range(DC):
                                pb = psW if c % 2 == 0 else psX
                                T.wait(V)
                                tr = T.o.transpose(pb[:, :128],
                                                   attK[:, u * 1024 + c * 128:
                                                        u * 1024 + (c + 1) * 128],
                                                   ident[:, :])
                                T.mark(tr)
                                V.wait(T)
                                cp = V.o.tensor_copy(
                                    attT[:, c * 512 + u * 128:c * 512 + (u + 1) * 128],
                                    pb[:, :128])
                                V.mark(cp)
                        wait_dma(T, LY, mpb_holder[0])
                        for ob in range(2):
                            Y.wait(T)
                            mw = LW.put(wst[:, :].rearrange("p (c w) -> p c w", w=512),
                                        wp1T[:, ob * 512:(ob + 1) * 512]
                                        .rearrange("(c p) w -> p c w", p=128))
                            wait_dma(T, LW, mw)
                            for u in range(NTILE):
                                pb = psW if u % 2 == 0 else psX
                                T.wait(V)
                                for c in range(DC):
                                    T.o.matmul(pb[:, :512],
                                               attT[:, c * 512 + u * 128:
                                                    c * 512 + (u + 1) * 128],
                                               wst[:, c * 512:c * 512 + 512],
                                               start=(c == 0), stop=False)
                                mm_ = T.o.matmul(pb[:, :512], onerow[:1, :],
                                                 pbr[:1, ob * 512:(ob + 1) * 512],
                                                 start=False, stop=True)
                                T.mark(mm_)
                                V.wait(T)
                                ev = V.o.tensor_tensor(
                                    attK[:, u * 1024 + ob * 512:u * 1024 + (ob + 1) * 512],
                                    pb[:, :512],
                                    xblk[:, u * D + ob * 512:u * D + (ob + 1) * 512],
                                    ALU.add)
                                V.mark(ev)
                        G.wait(V)
                        SG.put(xpark[t0:t0 + TBLK, :]
                               .rearrange("(a p) d -> p a d", p=128),
                               attK[:, :].rearrange("p (a d) -> p a d", d=D))
                    barrier()

        if "C" in phases:
            phase_C(0)
            phase_C(1)

        def phase_D():
            with contextlib.ExitStack() as ph:
                xblk = ph.enter_context(nc.sbuf_tensor("d_xblk", [128, NTILE * D], F32))
                xnT = ph.enter_context(nc.sbuf_tensor("d_xnT", [128, DC * TBLK], F32))
                gbuf = ph.enter_context(nc.sbuf_tensor("d_gbuf", [128, 32 * 512], F32))
                wstD = ph.enter_context(nc.sbuf_tensor("d_wst", [128, 4096], F16))
                xh = ph.enter_context(nc.sbuf_tensor("d_xh", [128, DC * TBLK], F16))
                xl = ph.enter_context(nc.sbuf_tensor("d_xl", [128, DC * TBLK], F16))
                gh = ph.enter_context(nc.sbuf_tensor("d_gh", [128, 32 * 512], F16))
                gl = ph.enter_context(nc.sbuf_tensor("d_gl", [128, 32 * 512], F16))
                lnwcs = ph.enter_context(nc.sbuf_tensor("d_lnwc", [128, 64], F32))
                rhosb = ph.enter_context(nc.sbuf_tensor("d_rho", [128, 512], F32))
                mursb = ph.enter_context(nc.sbuf_tensor("d_mur", [128, 512], F32))
                rowsA = ph.enter_context(nc.sbuf_tensor("d_rowsA", [1, 512], F32))
                rowsB = ph.enter_context(nc.sbuf_tensor("d_rowsB", [1, 512], F32))
                trsb = ph.enter_context(nc.sbuf_tensor("d_trsb", [8, 128], F32))
                mlc = LY.put(lnwcs[:, :], lnwc[:, :])
                for s in range(NSEQ):
                    for tb in range(NTB):
                        t0 = s * S + tb * TBLK
                        with contextlib.ExitStack() as pp:
                            psA = pp.enter_context(
                                nc.psum_tensor(f"d_psA{s}{tb}", [128, 512], F32))
                            psB = pp.enter_context(
                                nc.psum_tensor(f"d_psB{s}{tb}", [128, 512], F32))
                            wait_dma(SC, SG)
                            wait_dma(V, SG)
                            load_ln_transpose(xpark, s, tb, xblk, xnT, psA, psB, LY)
                            V.o.tensor_copy(xh[:, :], xnT[:, :])
                            V.o.drain()
                            sp = V.o.tensor_tensor(xl[:, :], xnT[:, :], xh[:, :],
                                                   ALU.subtract)
                            spl_mark = [V.mark(sp)]
                            gel_mark = {}
                            for og in range(8):
                                Y.wait(T)
                                mw = LW.put(wstD[:, :]
                                            .rearrange("p (c w) -> p c w", w=512),
                                            tfcT[:, og * 512:(og + 1) * 512]
                                            .rearrange("(c p) w -> p c w", p=128))
                                wait_dma(T, LW, mw)
                                for oo in range(4):
                                    oc = og * 4 + oo
                                    pb = psA if oc % 2 == 0 else psB
                                    T.wait(SC, gel_mark.get(oc % 2, 0))
                                    T.wait(V, spl_mark[0])
                                    for c in range(DC):
                                        T.o.matmul(
                                            pb[:, :512],
                                            wstD[:, c * 512 + oo * 128:
                                                 c * 512 + oo * 128 + 128],
                                            xh[:, c * TBLK:(c + 1) * TBLK],
                                            start=(c == 0), stop=False)
                                        mm_ = T.o.matmul(
                                            pb[:, :512],
                                            wstD[:, c * 512 + oo * 128:
                                                 c * 512 + oo * 128 + 128],
                                            xl[:, c * TBLK:(c + 1) * TBLK],
                                            start=False, stop=(c == DC - 1))
                                    fm = T.mark(mm_)
                                    SC.wait(T, fm)
                                    ge = SC.o.activation(
                                        gbuf[:, oc * 512:(oc + 1) * 512],
                                        pb[:, :512], ACT.Gelu)
                                    gel_mark[oc % 2] = SC.mark(ge)
                            sqm = {}
                            stm = {}
                            for oc in range(32):
                                SC.wait(T, stm.get(oc % 2, 0))
                                half = (oc % 2) * 512
                                sq_ = SC.o.activation(dump[:, half:half + 512],
                                                      gbuf[:, oc * 512:(oc + 1) * 512],
                                                      ACT.Square)
                                sqm[oc % 2] = SC.mark(sq_)
                                T.wait(SC, sqm[oc % 2])
                                for u in range(4):
                                    T.o.matmul(psA[:, 32 + u:33 + u],
                                               gbuf[:, oc * 512 + u * 128:
                                                    oc * 512 + (u + 1) * 128],
                                               scv[:, 3:4],
                                               start=(oc == 0 and u == 0),
                                               stop=(oc == 31 and u == 3))
                                    mm_ = T.o.matmul(psB[:, 32 + u:33 + u],
                                                     dump[:, half + u * 128:
                                                          half + (u + 1) * 128],
                                                     scv[:, 3:4],
                                                     start=(oc == 0 and u == 0),
                                                     stop=(oc == 31 and u == 3))
                                stm[oc % 2] = T.mark(mm_)
                            V.wait(T)
                            V.o.tensor_scalar(stats[:, 32:36], psA[:, 32:36],
                                              1.0 / DFF, None, ALU.mult)
                            V.o.tensor_scalar(stats[:, 36:40], psB[:, 32:36],
                                              1.0 / DFF, None, ALU.mult)
                            V.o.drain()
                            V.o.tensor_tensor(stats[:, 40:44], stats[:, 32:36],
                                              stats[:, 32:36], ALU.mult)
                            V.o.drain()
                            vv = V.o.tensor_tensor(stats[:, 36:40], stats[:, 36:40],
                                                   stats[:, 40:44], ALU.subtract)
                            V.mark(vv)
                            SC.wait(V)
                            sq_ = SC.o.activation(stats[:, 40:44], stats[:, 36:40],
                                                  ACT.Sqrt, bias=EPS_LN)
                            SC.mark(sq_)
                            V.wait(SC)
                            V.o.reciprocal(stats[:, 40:44], stats[:, 40:44])
                            V.o.drain()
                            V.o.tensor_tensor(stats[:, 44:48], stats[:, 32:36],
                                              stats[:, 40:44], ALU.mult)
                            V.o.drain()
                            mr = V.o.tensor_scalar(stats[:, 44:48], stats[:, 44:48],
                                                   -1.0, None, ALU.mult)
                            V.mark(mr)
                            T.wait(V)
                            tr = T.o.transpose(psA[0:8, 0:128], stats[:, 40:48],
                                               ident[:, :])
                            T.mark(tr)
                            V.wait(T)
                            cpr = V.o.tensor_copy(trsb[0:8, :], psA[0:8, 0:128])
                            V.mark(cpr)
                            G.wait(V)
                            mra = SG.put(rowsA[:1, :], trsb[0:4, :])
                            mrb = SG.put(rowsB[0:1, :], trsb[4:8, :])
                            wait_dma(T, SG, mrb)
                            wait_dma(V, LY, mlc)
                            T.wait(V)
                            bA = T.o.matmul(psA[:, :512], onerow[:1, :],
                                            rowsA[:1, :], start=True, stop=True)
                            bB = T.o.matmul(psB[:, :512], onerow[:1, :],
                                            rowsB[:1, :], start=True, stop=True)
                            bm = T.mark(bB)
                            V.wait(T, bm)
                            V.o.tensor_copy(rhosb[:, :], psA[:, :512])
                            bc = V.o.tensor_copy(mursb[:, :], psB[:, :512])
                            V.mark(bc)
                        with contextlib.ExitStack() as pp2:
                            pps = [pp2.enter_context(
                                nc.psum_tensor(f"d_pp{s}{tb}_{i}", [128, 512], F32))
                                for i in range(8)]
                            for og in range(8):
                                Y.wait(T)
                                mw = LW.put(wstD[:, :]
                                            .rearrange("p (c w) -> p c w", w=1024),
                                            tp2T[og * 512:(og + 1) * 512, :]
                                            .rearrange("(c p) w -> p c w", p=128))
                                wait_dma(T, LW, mw)
                                for oo in range(4):
                                    oc = og * 4 + oo
                                    gsl = gbuf[:, oc * 512:(oc + 1) * 512]
                                    V.o.tensor_tensor(gsl, gsl, rhosb[:, :],
                                                      ALU.mult)
                                    V.o.drain()
                                    V.o.tensor_tensor(gsl, gsl, mursb[:, :],
                                                      ALU.add)
                                    V.o.drain()
                                    V.o.tensor_scalar(gsl, gsl,
                                                      lnwcs[:, oc:oc + 1],
                                                      lnwcs[:, 32 + oc:33 + oc],
                                                      ALU.mult, ALU.add)
                                    V.o.drain()
                                    V.o.tensor_copy(
                                        gh[:, oc * 512:(oc + 1) * 512], gsl)
                                    V.o.drain()
                                    nm_ = V.o.tensor_tensor(
                                        gl[:, oc * 512:(oc + 1) * 512], gsl,
                                        gh[:, oc * 512:(oc + 1) * 512],
                                        ALU.subtract)
                                    nmk = V.mark(nm_)
                                    T.wait(V, nmk)
                                    for ob in range(2):
                                        for u in range(4):
                                            T.o.matmul(
                                                pps[ob * 4 + u][:, :512],
                                                gh[:, oc * 512 + u * 128:
                                                   oc * 512 + (u + 1) * 128],
                                                wstD[:, oo * 1024 + ob * 512:
                                                     oo * 1024 + (ob + 1) * 512],
                                                start=(oc == 0), stop=False)
                                            mm_ = T.o.matmul(
                                                pps[ob * 4 + u][:, :512],
                                                gl[:, oc * 512 + u * 128:
                                                   oc * 512 + (u + 1) * 128],
                                                wstD[:, oo * 1024 + ob * 512:
                                                     oo * 1024 + (ob + 1) * 512],
                                                start=False, stop=(oc == 31))
                                T.mark(mm_)
                            V.wait(T)
                            for ob in range(2):
                                for u in range(4):
                                    ev = V.o.tensor_tensor(
                                        gbuf[:, u * 1024 + ob * 512:
                                             u * 1024 + (ob + 1) * 512],
                                        pps[ob * 4 + u][:, :512],
                                        xblk[:, u * D + ob * 512:u * D + (ob + 1) * 512],
                                        ALU.add)
                            V.mark(ev)
                            G.wait(V)
                            SG.put(yout[t0:t0 + TBLK, :]
                                   .rearrange("(a p) d -> p a d", p=128),
                                   gbuf[:, :NTILE * D]
                                   .rearrange("p (a d) -> p a d", d=D))
                barrier()

        if "D" in phases:
            phase_D()

        if "D" not in phases:
          with nc.sbuf_tensor("st_buf", [128, NTILE * D], F32) as sb:
            for s in range(NSEQ):
                for tb in range(NTB):
                    t0 = s * S + tb * TBLK
                    wait_dma(Y, SG)
                    m = LY.put(sb[:, :].rearrange("p (a d) -> p a d", d=D),
                               xpark[t0:t0 + TBLK, :]
                               .rearrange("(a p) d -> p a d", p=128))
                    wait_dma(G, LY, m)
                    SG.put(yout[t0:t0 + TBLK, :]
                           .rearrange("(a p) d -> p a d", p=128),
                           sb[:, :].rearrange("p (a d) -> p a d", d=D))
          barrier()

        wait_dma(G, SG)
        wait_dma(Y, LY)

    return nc


_NC_CACHE = {}


def _get_program():
    if "nc" not in _NC_CACHE:
        _NC_CACHE["nc"] = build_program()
    return _NC_CACHE["nc"]


def make_in_maps(inputs_by_name):
    x = np.ascontiguousarray(np.asarray(inputs_by_name["L0_attn_x"],
                                        dtype=np.float32))
    shared = {
        "w0": np.ascontiguousarray(inputs_by_name["L0_attn_W"]),
        "b0c": np.ascontiguousarray(
            np.asarray(inputs_by_name["L0_attn_b"]).reshape(24, 128).T),
        "b0r": np.asarray(inputs_by_name["L0_attn_b"]).reshape(1, 3 * D),
        "fw0": np.ascontiguousarray(inputs_by_name["L0_ffn_weight"]),
        "g0c": np.ascontiguousarray(
            np.asarray(inputs_by_name["L0_ffn_gamma"]).reshape(DC, 128).T),
        "be0c": np.ascontiguousarray(
            np.asarray(inputs_by_name["L0_ffn_beta"]).reshape(DC, 128).T),
        "w1": np.ascontiguousarray(inputs_by_name["L1_attn_qkv_w"]),
        "b1c": np.ascontiguousarray(
            np.asarray(inputs_by_name["L1_attn_qkv_b"]).reshape(24, 128).T),
        "b1r": np.asarray(inputs_by_name["L1_attn_qkv_b"]).reshape(1, 3 * D),
        "pw1": np.ascontiguousarray(inputs_by_name["L1_attn_proj_w"]),
        "pb1r": np.asarray(inputs_by_name["L1_attn_proj_b"]).reshape(1, D),
        "fcw1": np.ascontiguousarray(inputs_by_name["L1_ffn_W_fc"]),
        "lnwb2": np.ascontiguousarray(
            np.stack([np.asarray(inputs_by_name["L1_ffn_ln_w"]),
                      np.asarray(inputs_by_name["L1_ffn_ln_b"])])),
        "lnwc": np.ascontiguousarray(np.concatenate(
            [np.asarray(inputs_by_name["L1_ffn_ln_w"]).reshape(32, 128).T,
             np.asarray(inputs_by_name["L1_ffn_ln_b"]).reshape(32, 128).T],
            axis=1)),
        "pw2": np.ascontiguousarray(inputs_by_name["L1_ffn_W_proj"]),
        "cident": np.eye(128, dtype=np.float32),
        "conerow": np.ones((1, 128), dtype=np.float32),
        "cvals": np.broadcast_to(np.array(
            [127.0, EPS_LN, -63.5, 1.0, 0.0, 0.0, 0.0, 0.0],
            dtype=np.float32), (128, 8)).copy(),
    }
    in_maps = []
    for c in range(NCORE):
        im = dict(shared)
        im["xin"] = np.ascontiguousarray(x[2 * c:2 * c + 2].reshape(NSEQ * S, D))
        in_maps.append(im)
    return in_maps


def kernel(L0_attn_x, L0_attn_W, L0_attn_b, L0_ffn_weight, L0_ffn_gamma,
           L0_ffn_beta, L1_attn_qkv_w, L1_attn_qkv_b, L1_attn_proj_w,
           L1_attn_proj_b, L1_ffn_W_fc, L1_ffn_ln_w, L1_ffn_ln_b,
           L1_ffn_W_proj):
    kw = dict(L0_attn_x=L0_attn_x, L0_attn_W=L0_attn_W, L0_attn_b=L0_attn_b,
              L0_ffn_weight=L0_ffn_weight, L0_ffn_gamma=L0_ffn_gamma,
              L0_ffn_beta=L0_ffn_beta, L1_attn_qkv_w=L1_attn_qkv_w,
              L1_attn_qkv_b=L1_attn_qkv_b, L1_attn_proj_w=L1_attn_proj_w,
              L1_attn_proj_b=L1_attn_proj_b, L1_ffn_W_fc=L1_ffn_W_fc,
              L1_ffn_ln_w=L1_ffn_ln_w, L1_ffn_ln_b=L1_ffn_ln_b,
              L1_ffn_W_proj=L1_ffn_W_proj)
    nc = _get_program()
    in_maps = make_in_maps(kw)
    res = run_bass_kernel_spmd(nc, in_maps, core_ids=list(range(NCORE)))
    out = np.concatenate(
        [np.asarray(r["y"]).reshape(NSEQ, S, D) for r in res.results], axis=0)
    return out.astype(np.float32)


# version 06
# speedup vs baseline: 1.5312x; 1.0534x over previous
import contextlib
import numpy as np

import concourse.bass as bass
import concourse.mybir as mybir
from concourse.bass_utils import run_bass_kernel_spmd

F32 = mybir.dt.float32
BF16 = mybir.dt.bfloat16
F16 = mybir.dt.float16
ALU = mybir.AluOpType
ACT = mybir.ActivationFunctionType
AXX = mybir.AxisListType.X

D = 1024
S = 2048
NSEQ = 2
NCORE = 8
DC = D // 128
TBLK = 512
NTB = S // TBLK
NTILE = TBLK // 128
H = 16
HD = 64
DFF = 4096
EPS_LN = 1e-5
ALL_PHASES = ("P", "A", "B", "C", "D")
A_STOP = [4]
A1_NTB = [NTB]
A1_NFC = [16]
LN_PARTS = [3]
DBG_EPARK = [False]
A_SEQS = [2]
C_STOP = [99]


class Eng:
    def __init__(self, obj, sem, name):
        self.o = obj
        self.sem = sem
        self.name = name
        self.n = 0
        self._seen = {}

    def mark(self, instr, k=1):
        assert instr is not None, "bare sem_inc fires before datapath drain"
        instr.then_inc(self.sem, k)
        self.n += k
        return self.n

    def wait(self, other, val=None):
        if val is None:
            val = other.n
        if val <= 0:
            return
        key = other.name
        if self._seen.get(key, -1) >= val:
            return
        self.o.wait_ge(other.sem, val)
        self._seen[key] = val


class DmaStream:

    def __init__(self, eng, sem, name):
        self.eng = eng
        self.sem = sem
        self.name = name
        self.n = 0

    def put(self, out, in_):
        self.eng.o.dma_start(out=out, in_=in_).then_inc(self.sem, 16)
        self.n += 16
        return self.n


def wait_dma(eng, st, val=None):
    if val is None:
        val = st.n
    if val <= 0:
        return
    key = "dma_" + st.name
    if eng._seen.get(key, -1) >= val:
        return
    eng.o.wait_ge(st.sem, val)
    eng._seen[key] = val


def build_program(phases=ALL_PHASES):
    phases = set(phases)
    nc = bass.Bass("TRN2", target_bir_lowering=False,
                   detect_race_conditions=False)

    xin = nc.dram_tensor("xin", [NSEQ * S, D], F32, kind="ExternalInput").ap()
    w0 = nc.dram_tensor("w0", [3 * D, D], F32, kind="ExternalInput").ap()
    b0c = nc.dram_tensor("b0c", [128, 24], F32, kind="ExternalInput").ap()
    b0r = nc.dram_tensor("b0r", [1, 3 * D], F32, kind="ExternalInput").ap()
    fw0 = nc.dram_tensor("fw0", [D, D], F32, kind="ExternalInput").ap()
    g0c = nc.dram_tensor("g0c", [128, DC], F32, kind="ExternalInput").ap()
    be0c = nc.dram_tensor("be0c", [128, DC], F32, kind="ExternalInput").ap()
    w1 = nc.dram_tensor("w1", [3 * D, D], F32, kind="ExternalInput").ap()
    b1c = nc.dram_tensor("b1c", [128, 24], F32, kind="ExternalInput").ap()
    b1r = nc.dram_tensor("b1r", [1, 3 * D], F32, kind="ExternalInput").ap()
    pw1 = nc.dram_tensor("pw1", [D, D], F32, kind="ExternalInput").ap()
    pb1r = nc.dram_tensor("pb1r", [1, D], F32, kind="ExternalInput").ap()
    fcw1 = nc.dram_tensor("fcw1", [DFF, D], F32, kind="ExternalInput").ap()
    lnwb2 = nc.dram_tensor("lnwb2", [2, DFF], F32, kind="ExternalInput").ap()
    lnwc = nc.dram_tensor("lnwc", [128, 64], F32, kind="ExternalInput").ap()
    pw2 = nc.dram_tensor("pw2", [D, DFF], F32, kind="ExternalInput").ap()
    cident = nc.dram_tensor("cident", [128, 128], F32, kind="ExternalInput").ap()
    conerow = nc.dram_tensor("conerow", [1, 128], F32, kind="ExternalInput").ap()
    cvals = nc.dram_tensor("cvals", [128, 8], F32, kind="ExternalInput").ap()
    yout = nc.dram_tensor("y", [NSEQ * S, D], F32, kind="ExternalOutput").ap()

    xpark = nc.dram_tensor("xpark", [NSEQ * S, D], F32,
                           kind="ExternalOutput" if DBG_EPARK[0] else "Internal").ap()
    epark = nc.dram_tensor("epark", [S, S], F32,
                           kind="ExternalOutput" if DBG_EPARK[0] else "Internal").ap()
    w0T = nc.dram_tensor("w0T", [D, 3 * D], F32).ap()
    w1T = nc.dram_tensor("w1T", [D, 3 * D], F32).ap()
    wp1T = nc.dram_tensor("wp1T", [D, D], F32).ap()
    t0T = nc.dram_tensor("t0T", [D, D], BF16).ap()
    tfcT = nc.dram_tensor("tfcT", [D, DFF], F16).ap()
    tp2T = nc.dram_tensor("tp2T", [DFF, D], F16).ap()
    ccin = nc.dram_tensor("ccin", [1, 128], F32)
    ccout = nc.dram_tensor("ccout", [1, 128], F32)

    es = contextlib.ExitStack()
    with es:
        semT = es.enter_context(nc.semaphore("semT"))
        semV = es.enter_context(nc.semaphore("semV"))
        semS = es.enter_context(nc.semaphore("semS"))
        semG = es.enter_context(nc.semaphore("semG"))
        semY = es.enter_context(nc.semaphore("semY"))
        semLY = es.enter_context(nc.semaphore("semLY"))
        semLW = es.enter_context(nc.semaphore("semLW"))
        semSG = es.enter_context(nc.semaphore("semSG"))
        semLG = es.enter_context(nc.semaphore("semLG"))
        ccsem = es.enter_context(nc.semaphore("ccsem"))

        es.enter_context(nc.Block())

        T = Eng(nc.tensor, semT, "T")
        V = Eng(nc.vector, semV, "V")
        SC = Eng(nc.scalar, semS, "S")
        G = Eng(nc.gpsimd, semG, "G")
        Y = Eng(nc.sync, semY, "Y")
        ALL_E = [T, V, SC, G, Y]

        LY = DmaStream(Y, semLY, "LY")
        LW = DmaStream(Y, semLW, "LW")
        SG = DmaStream(G, semSG, "SG")
        LG = DmaStream(G, semLG, "LG")

        def barrier():
            for e in ALL_E:
                for o in ALL_E:
                    if o is not e:
                        e.wait(o)
                for st_ in (LY, LW, SG, LG):
                    wait_dma(e, st_)

        ident = es.enter_context(nc.sbuf_tensor("ident", [128, 128], F32))
        onerow = es.enter_context(nc.sbuf_tensor("onerow", [1, 128], F32))
        dump = es.enter_context(nc.sbuf_tensor("dump", [128, 1024], F32))
        stats = es.enter_context(nc.sbuf_tensor("stats", [128, 64], F32))
        srow = es.enter_context(nc.sbuf_tensor("srow", [1, 64], F32))
        b0sb = es.enter_context(nc.sbuf_tensor("b0sb", [128, 24], F32))
        b1sb = es.enter_context(nc.sbuf_tensor("b1sb", [128, 24], F32))
        g0sb = es.enter_context(nc.sbuf_tensor("g0sb", [128, DC], F32))
        be0sb = es.enter_context(nc.sbuf_tensor("be0sb", [128, DC], F32))
        scv = es.enter_context(nc.sbuf_tensor("scv", [128, 8], F32))
        qscal = es.enter_context(nc.sbuf_tensor("qscal", [128, 8], F32))

        LY.put(ident[:, :], cident[:, :])
        LY.put(onerow[:1, :], conerow[:1, :])
        LY.put(b0sb[:, :], b0c[:, :])
        LY.put(b1sb[:, :], b1c[:, :])
        LY.put(g0sb[:, :], g0c[:, :])
        LY.put(be0sb[:, :], be0c[:, :])
        LY.put(scv[:, :], cvals[:, :])
        nc.const_aps.register(F32, EPS_LN, scv[:, 1:2])
        nc.const_aps.register(F32, -63.5, scv[:, 2:3])

        def col_to_scalar(col_ap, out_ap, op, ps):
            T.wait(V)
            tr = T.o.transpose(ps[:1, :128], col_ap, ident[:, :])
            T.mark(tr)
            V.wait(T)
            rd = V.o.tensor_reduce(out_ap, ps[:1, :128], axis=AXX, op=op)
            V.mark(rd)

        def bcast_scalar(src_ap, dst_ap, ps):
            w = src_ap.shape[-1]
            T.wait(V)
            mm = T.o.matmul(ps[:, :w], onerow[:1, :], src_ap,
                            start=True, stop=True)
            T.mark(mm)
            V.wait(T)
            cp = V.o.tensor_copy(dst_ap, ps[:, :w])
            V.mark(cp)

        if "P" in phases:
          with contextlib.ExitStack() as ps_:
            win = ps_.enter_context(nc.sbuf_tensor("p_win", [128, 4096], F32))
            wot = ps_.enter_context(nc.sbuf_tensor("p_wot", [128, 8192], F32))
            pst = ps_.enter_context(nc.psum_tensor("p_pst", [128, 512], F32))
            pst2 = ps_.enter_context(nc.psum_tensor("p_pst2", [128, 512], F32))

            def transpose_rows(src, src_rows, src_cols, dstT):
                nrc = src_rows // 128
                ncc = src_cols // 128
                evmark = {}
                for rc in range(nrc):
                    Y.wait(T)
                    m = LY.put(win[:, :src_cols], src[rc * 128:(rc + 1) * 128, :])
                    wait_dma(T, LY, m)
                    for cc in range(ncc):
                        pb = pst if cc % 2 == 0 else pst2
                        ev = V if cc % 2 == 0 else SC
                        if cc >= 2:
                            T.wait(ev, evmark.get(cc % 2, 0))
                        tr = T.o.transpose(pb[:, :128],
                                           win[:, cc * 128:(cc + 1) * 128],
                                           ident[:, :])
                        tm = T.mark(tr)
                        ev.wait(T, tm)
                        wait_dma(ev, SG)
                        if ev is V:
                            cp = V.o.tensor_copy(wot[:, cc * 128:(cc + 1) * 128],
                                                 pb[:, :128])
                        else:
                            cp = SC.o.copy(wot[:, cc * 128:(cc + 1) * 128],
                                           pb[:, :128])
                        evmark[cc % 2] = ev.mark(cp)
                    G.wait(V)
                    G.wait(SC)
                    SG.put(dstT[:, rc * 128:(rc + 1) * 128]
                           .rearrange("(c p) w -> p c w", p=128),
                           wot[:, :src_cols].rearrange("p (c w) -> p c w", w=128))

            transpose_rows(w0, 3 * D, D, w0T)
            transpose_rows(w1, 3 * D, D, w1T)
            transpose_rows(pw1, D, D, wp1T)

            def gsum_abs(src, rows, cols, out_col):
                nrc = rows // 128
                for rc in range(nrc):
                    Y.wait(V)
                    Y.wait(SC)
                    m = LY.put(win[:, :cols], src[rc * 128:(rc + 1) * 128, :])
                    wait_dma(V, LY, m)
                    V.o.tensor_reduce(stats[:, 62:63], win[:, :cols], axis=AXX,
                                      op=ALU.add, apply_absolute_value=True)
                    V.o.drain()
                    if rc == 0:
                        cp = V.o.tensor_copy(stats[:, out_col:out_col + 1],
                                             stats[:, 62:63])
                    else:
                        cp = V.o.tensor_tensor(stats[:, out_col:out_col + 1],
                                               stats[:, out_col:out_col + 1],
                                               stats[:, 62:63], ALU.add)
                    V.mark(cp)
                    V.o.drain()
                col_to_scalar(stats[:, out_col:out_col + 1],
                              srow[:1, out_col:out_col + 1], ALU.add, pst)

            gsum_abs(fw0, D, D, 0)
            V.o.drain()
            dl_ = V.o.tensor_scalar(srow[:1, 1:2], srow[:1, 0:1],
                                    0.7 / (2.0 ** 20), None, ALU.mult)
            V.mark(dl_)
            bcast_scalar(srow[:1, 1:2], qscal[:, 2:3], pst)
            for rc in range(D // 128):
                Y.wait(V)
                Y.wait(SC)
                m = LY.put(win[:, :D], fw0[rc * 128:(rc + 1) * 128, :])
                wait_dma(SC, LY, m)
                SC.wait(V)
                ab = SC.o.activation(wot[:, :D], win[:, :D], ACT.Abs)
                SC.mark(ab)
                wait_dma(V, LY, m)
                V.wait(SC)
                V.o.tensor_scalar(wot[:, 4096:4096 + D], wot[:, :D],
                                  qscal[:, 2:3], None, ALU.is_gt)
                V.o.drain()
                V.o.tensor_reduce(stats[:, 62:63], wot[:, 4096:4096 + D],
                                  axis=AXX, op=ALU.add)
                V.o.drain()
                if rc == 0:
                    V.o.tensor_copy(stats[:, 3:4], stats[:, 62:63])
                else:
                    V.o.tensor_tensor(stats[:, 3:4], stats[:, 3:4],
                                      stats[:, 62:63], ALU.add)
                V.o.tensor_tensor(wot[:, :D], wot[:, :D], wot[:, 4096:4096 + D],
                                  ALU.mult)
                V.o.drain()
                V.o.tensor_reduce(stats[:, 62:63], wot[:, :D], axis=AXX,
                                  op=ALU.add)
                V.o.drain()
                if rc == 0:
                    cp = V.o.tensor_copy(stats[:, 4:5], stats[:, 62:63])
                else:
                    cp = V.o.tensor_tensor(stats[:, 4:5], stats[:, 4:5],
                                           stats[:, 62:63], ALU.add)
                V.mark(cp)
                V.o.drain()
            col_to_scalar(stats[:, 3:4], srow[:1, 3:4], ALU.add, pst)
            col_to_scalar(stats[:, 4:5], srow[:1, 4:5], ALU.add, pst2)
            V.o.tensor_scalar(srow[:1, 5:6], srow[:1, 3:4], 1.0, None, ALU.max)
            V.o.drain()
            V.o.reciprocal(srow[:1, 5:6], srow[:1, 5:6])
            V.o.drain()
            al = V.o.tensor_tensor(srow[:1, 6:7], srow[:1, 4:5], srow[:1, 5:6],
                                   ALU.mult)
            V.mark(al)
            bcast_scalar(srow[:1, 6:7], qscal[:, 3:4], pst)

            def quantize_transposed(src, rows, cols, dstT, thr_col, scale_col):
                vmark = {}
                for rc in range(rows // 128):
                    Y.wait(T)
                    m = LY.put(win[:, :cols], src[rc * 128:(rc + 1) * 128, :])
                    wait_dma(T, LY, m)
                    for cc in range(cols // 128):
                        pb = pst if cc % 2 == 0 else pst2
                        if cc >= 2:
                            T.wait(SC, vmark.get(("s", cc % 2), 0))
                        tr = T.o.transpose(pb[:, :128],
                                           win[:, cc * 128:(cc + 1) * 128],
                                           ident[:, :])
                        tm = T.mark(tr)
                        SC.wait(T, tm)
                        wait_dma(SC, SG)
                        SC.wait(V, vmark.get(("v", cc % 2), 0))
                        sg = SC.o.activation(
                            wot[:, 4096 + cc * 128:4096 + (cc + 1) * 128],
                            pb[:, :128], ACT.Sign)
                        ab = SC.o.activation(wot[:, cc * 128:(cc + 1) * 128],
                                             pb[:, :128], ACT.Abs)
                        vmark[("s", cc % 2)] = SC.mark(ab)
                        V.wait(SC, vmark[("s", cc % 2)])
                        if scale_col is not None:
                            V.o.tensor_scalar(wot[:, cc * 128:(cc + 1) * 128],
                                              wot[:, cc * 128:(cc + 1) * 128],
                                              qscal[:, thr_col:thr_col + 1],
                                              qscal[:, scale_col:scale_col + 1],
                                              ALU.is_gt, ALU.mult)
                        else:
                            V.o.tensor_scalar(wot[:, cc * 128:(cc + 1) * 128],
                                              wot[:, cc * 128:(cc + 1) * 128],
                                              qscal[:, thr_col:thr_col + 1],
                                              None, ALU.is_gt)
                        V.o.drain()
                        cp = V.o.tensor_tensor(
                            wot[:, cc * 128:(cc + 1) * 128],
                            wot[:, cc * 128:(cc + 1) * 128],
                            wot[:, 4096 + cc * 128:4096 + (cc + 1) * 128],
                            ALU.mult)
                        vmark[("v", cc % 2)] = V.mark(cp)
                    G.wait(V)
                    SG.put(dstT[:, rc * 128:(rc + 1) * 128]
                           .rearrange("(c p) w -> p c w", p=128),
                           wot[:, :cols].rearrange("p (c w) -> p c w", w=128))

            quantize_transposed(fw0, D, D, t0T, 2, None)

            def absmean_ternary(src, rows, cols, dstT, statcol):
                gsum_abs(src, rows, cols, statcol)
                cnt = float(rows * cols)
                V.o.drain()
                V.o.tensor_scalar(srow[:1, statcol + 1:statcol + 2],
                                  srow[:1, statcol:statcol + 1],
                                  1.0 / cnt, 1e-6, ALU.mult, ALU.add)
                V.o.drain()
                thr = V.o.tensor_scalar(srow[:1, statcol + 1:statcol + 2],
                                        srow[:1, statcol + 1:statcol + 2],
                                        0.5, None, ALU.mult)
                V.mark(thr)
                bcast_scalar(srow[:1, statcol + 1:statcol + 2],
                             qscal[:, 4:5], pst)
                quantize_transposed(src, rows, cols, dstT, 4, None)

            absmean_ternary(fcw1, DFF, D, tfcT, 8)
            absmean_ternary(pw2, D, DFF, tp2T, 10)
          barrier()

        def load_ln_transpose(src, s, tb, xblk, xnT, psA, psB, ldstream,
                              gamma_col=None, beta_col=None):
            t0 = s * S + tb * TBLK
            ldstream.eng.wait(V)
            ldstream.eng.wait(SC)
            ldstream.eng.wait(T)
            m = ldstream.put(xblk[:, :].rearrange("p (a d) -> p a d", d=D),
                             src[t0:t0 + TBLK, :]
                             .rearrange("(a p) d -> p a d", p=128))
            wait_dma(V, ldstream, m)
            wait_dma(SC, ldstream, m)
            SC.wait(T)
            SC.wait(V)
            for a in range(NTILE):
                c = xblk[:, a * D:(a + 1) * D]
                rd_ = V.o.tensor_reduce(stats[:, 16 + a:17 + a], c, axis=AXX,
                                        op=ALU.add)
                sq_ = SC.o.activation(dump[:, :1024], c, ACT.Square,
                                      accum_out=stats[:, 20 + a:21 + a])
            V.mark(rd_)
            SC.mark(sq_)
            V.wait(SC)
            V.o.drain()
            V.o.tensor_scalar(stats[:, 16:20], stats[:, 16:20], 1.0 / D,
                              None, ALU.mult)
            V.o.tensor_scalar(stats[:, 20:24], stats[:, 20:24], 1.0 / D,
                              None, ALU.mult)
            V.o.drain()
            V.o.tensor_tensor(stats[:, 24:28], stats[:, 16:20],
                              stats[:, 16:20], ALU.mult)
            V.o.drain()
            vv = V.o.tensor_tensor(stats[:, 20:24], stats[:, 20:24],
                                   stats[:, 24:28], ALU.subtract)
            V.mark(vv)
            SC.wait(V)
            sq = SC.o.activation(stats[:, 24:28], stats[:, 20:24], ACT.Sqrt,
                                 bias=EPS_LN)
            SC.mark(sq)
            V.wait(SC)
            rr = V.o.reciprocal(stats[:, 24:28], stats[:, 24:28])
            V.mark(rr)
            if LN_PARTS[0] < 2:
                return
            for a in range(NTILE):
                src_c = xblk[:, a * D:(a + 1) * D]
                V.wait(T)
                nrm = V.o.tensor_scalar(dump[:, :1024], src_c,
                                        stats[:, 16 + a:17 + a],
                                        stats[:, 24 + a:25 + a],
                                        ALU.subtract, ALU.mult)
                V.mark(nrm)
                if LN_PARTS[0] < 3:
                    continue
                for c in range(DC):
                    pb = psA if c % 2 == 0 else psB
                    T.wait(V)
                    tr = T.o.transpose(pb[:, :128], dump[:, c * 128:(c + 1) * 128],
                                       ident[:, :])
                    T.mark(tr)
                    V.wait(T)
                    cp = V.o.tensor_copy(
                        xnT[:, c * TBLK + a * 128:c * TBLK + (a + 1) * 128],
                        pb[:, :128])
                    V.mark(cp)
            if gamma_col is not None:
                for c in range(DC):
                    sc_ = V.o.tensor_scalar(xnT[:, c * TBLK:(c + 1) * TBLK],
                                            xnT[:, c * TBLK:(c + 1) * TBLK],
                                            gamma_col[:, c:c + 1],
                                            beta_col[:, c:c + 1],
                                            ALU.mult, ALU.add)
                    V.mark(sc_)

        def phase_A(s):
            pfx = f"a{s}_"
            with contextlib.ExitStack() as ph:
                KT = ph.enter_context(nc.sbuf_tensor(pfx + "KT", [128, DC * S], F32))
                QT = ph.enter_context(nc.sbuf_tensor(pfx + "QT", [128, DC * S], F32))
                xblk = ph.enter_context(nc.sbuf_tensor(pfx + "xblk", [128, NTILE * D], F32))
                xnT = ph.enter_context(nc.sbuf_tensor(pfx + "xnT", [128, DC * TBLK], F32))
                wrow = ph.enter_context(nc.sbuf_tensor(pfx + "wrow", [128, 2 * 1024], F32))
                brow = ph.enter_context(nc.sbuf_tensor(pfx + "brow", [1, 3 * D], F32))
                zall = ph.enter_context(nc.sbuf_tensor(pfx + "zall", [128, 16], F32))

                ps1 = contextlib.ExitStack()
                psA = ps1.enter_context(nc.psum_tensor(pfx + "psA", [128, 512], F32))
                psB = ps1.enter_context(nc.psum_tensor(pfx + "psB", [128, 512], F32))
                tmark_half = [0, 0]
                for tb in range(A1_NTB[0]):
                    load_ln_transpose(xin, s, tb, xblk, xnT, psA, psB, LY)
                    for fc in range(A1_NFC[0]):
                        dst = QT if fc < 8 else KT
                        freg = fc % 8
                        half = fc % 2
                        pb = psA if half == 0 else psB
                        Y.wait(T, tmark_half[half])
                        m = LW.put(wrow[:, half * 1024:half * 1024 + 128 * DC]
                                   .rearrange("p (c w) -> p c w", w=128),
                                   w0T[:, fc * 128:(fc + 1) * 128]
                                   .rearrange("(c p) w -> p c w", p=128))
                        wait_dma(T, LW, m)
                        T.wait(V)
                        for c in range(DC):
                            mm_ = T.o.matmul(pb[:, :512],
                                             wrow[:, half * 1024 + c * 128:
                                                  half * 1024 + (c + 1) * 128],
                                             xnT[:, c * TBLK:(c + 1) * TBLK],
                                             start=(c == 0), stop=(c == DC - 1))
                        tmark_half[half] = T.mark(mm_)
                        V.wait(T)
                        ev = V.o.tensor_scalar(
                            dst[:, freg * S + tb * TBLK:freg * S + (tb + 1) * TBLK],
                            pb[:, :512], b0sb[:, fc:fc + 1], None, ALU.add)
                        V.mark(ev)
                barrier()
                ps1.close()
                if A_STOP[0] < 2:
                    return

                with contextlib.ExitStack() as p2:
                    psS1 = p2.enter_context(nc.psum_tensor(pfx + "psS1", [128, S], F32))
                    psS2 = p2.enter_context(nc.psum_tensor(pfx + "psS2", [128, S], F32))
                    est = p2.enter_context(nc.sbuf_tensor(pfx + "est", [128, 2 * S], F32))
                    exm = {}
                    rdm = {}
                    for jt in range(16):
                        pb = psS1 if jt % 2 == 0 else psS2
                        half = jt % 2
                        T.wait(SC, exm.get(half, 0))
                        T.wait(V, rdm.get(half, 0))
                        for ig in range(4):
                            for c in range(DC):
                                mm_ = T.o.matmul(pb[:, ig * 512:(ig + 1) * 512],
                                                 KT[:, c * S + jt * 128:c * S + (jt + 1) * 128],
                                                 QT[:, c * S + ig * 512:c * S + (ig + 1) * 512],
                                                 start=(c == 0), stop=(c == DC - 1))
                        smk = T.mark(mm_)
                        V.wait(T, smk)
                        V.o.tensor_reduce(stats[:, 42 + half:43 + half], pb[:, :],
                                          axis=AXX, op=ALU.max)
                        V.o.drain()
                        nm = V.o.tensor_scalar(stats[:, 44 + half:45 + half],
                                               stats[:, 42 + half:43 + half],
                                               -1.0 / 1024.0, None, ALU.mult)
                        rdm[half] = V.mark(nm)
                        SC.wait(V, rdm[half])
                        wait_dma(SC, SG)
                        ex = SC.o.activation(est[:, half * S:(half + 1) * S],
                                             pb[:, :], ACT.Exp,
                                             bias=stats[:, 44 + half:45 + half],
                                             scale=1.0 / 1024.0,
                                             accum_out=zall[:, jt:jt + 1])
                        exm[half] = SC.mark(ex)
                        G.wait(SC)
                        SG.put(epark[jt * 128:(jt + 1) * 128, :],
                               est[:, half * S:(half + 1) * S])
                    V.wait(SC)
                    rz = V.o.reciprocal(zall[:, :], zall[:, :])
                    V.mark(rz)
                barrier()
                if A_STOP[0] < 3:
                    return

                Vb = QT
                wvres = KT
                with contextlib.ExitStack() as p3:
                    psA = p3.enter_context(nc.psum_tensor(pfx + "psA2", [128, 512], F32))
                    psB = p3.enter_context(nc.psum_tensor(pfx + "psB2", [128, 512], F32))
                    m = LW.put(wvres[:, :DC * 1024]
                               .rearrange("p (c w) -> p c w", w=1024),
                               w0T[:, 2048:3072]
                               .rearrange("(c p) w -> p c w", p=128))
                    mrow = LY.put(brow[:1, :], b0r[:1, :])
                    for tb in range(NTB):
                        load_ln_transpose(xin, s, tb, xblk, xnT, psA, psB, LY)
                        wait_dma(T, LW, m)
                        wait_dma(T, LY, mrow)
                        for a in range(NTILE):
                            jt = tb * NTILE + a
                            for kb in range(2):
                                pb = psA if kb == 0 else psB
                                T.wait(V)
                                for c in range(DC):
                                    T.o.matmul(pb[:, :512],
                                               xnT[:, c * TBLK + a * 128:
                                                   c * TBLK + (a + 1) * 128],
                                               wvres[:, c * 1024 + kb * 512:
                                                     c * 1024 + (kb + 1) * 512],
                                               start=(c == 0), stop=False)
                                mm_ = T.o.matmul(pb[:, :512], onerow[:1, :],
                                                 brow[:1, 2048 + kb * 512:
                                                      2048 + (kb + 1) * 512],
                                                 start=False, stop=True)
                                T.mark(mm_)
                                V.wait(T)
                                ev = V.o.tensor_scalar(
                                    Vb[:, jt * 1024 + kb * 512:
                                       jt * 1024 + (kb + 1) * 512],
                                    pb[:, :512], zall[:, jt:jt + 1],
                                    None, ALU.mult)
                                V.mark(ev)
                    barrier()
                    if A_STOP[0] < 4:
                        return

                    EC = KT
                    OB = 16 * 512
                    for ib in range(4):
                        t0 = s * S + ib * TBLK
                        Y.wait(T)
                        for jt in range(16):
                            LY.put(EC[:, jt * 512:(jt + 1) * 512],
                                   epark[jt * 128:(jt + 1) * 128,
                                         ib * TBLK:(ib + 1) * TBLK])
                        G.wait(V)
                        mx_ = LG.put(xblk[:, :].rearrange("p (a d) -> p a d", d=D),
                                     xin[t0:t0 + TBLK, :]
                                     .rearrange("(a p) d -> p a d", p=128))
                        wait_dma(T, LY)
                        wait_dma(V, SG)
                        for u in range(4):
                            for kb in range(2):
                                pb = psA if (u * 2 + kb) % 2 == 0 else psB
                                T.wait(V)
                                for jt in range(16):
                                    mm_ = T.o.matmul(pb[:, :512],
                                                     EC[:, jt * 512 + u * 128:
                                                        jt * 512 + u * 128 + 128],
                                                     Vb[:, jt * 1024 + kb * 512:
                                                        jt * 1024 + (kb + 1) * 512],
                                                     start=(jt == 0), stop=(jt == 15))
                                T.mark(mm_)
                                V.wait(T)
                                wait_dma(V, LG, mx_)
                                ev = V.o.tensor_tensor(
                                    KT[:, OB + u * D + kb * 512:
                                       OB + u * D + (kb + 1) * 512],
                                    pb[:, :512],
                                    xblk[:, u * D + kb * 512:u * D + (kb + 1) * 512],
                                    ALU.add)
                                V.mark(ev)
                        G.wait(V)
                        SG.put(xpark[t0:t0 + TBLK, :]
                               .rearrange("(a p) d -> p a d", p=128),
                               KT[:, OB:OB + NTILE * D]
                               .rearrange("p (a d) -> p a d", d=D))
                    barrier()

        if "A" in phases:
            phase_A(0)
            if A_SEQS[0] > 1:
                phase_A(1)
        else:
            with nc.sbuf_tensor("cp_buf", [128, NTILE * D], F32) as sb:
                for s in range(NSEQ):
                    for tb in range(NTB):
                        t0 = s * S + tb * TBLK
                        wait_dma(Y, SG)
                        m = LY.put(sb[:, :].rearrange("p (a d) -> p a d", d=D),
                                   xin[t0:t0 + TBLK, :]
                                   .rearrange("(a p) d -> p a d", p=128))
                        wait_dma(G, LY, m)
                        SG.put(xpark[t0:t0 + TBLK, :]
                               .rearrange("(a p) d -> p a d", p=128),
                               sb[:, :].rearrange("p (a d) -> p a d", d=D))
            barrier()


        def phase_B():
            with contextlib.ExitStack() as ph:
                xblk = ph.enter_context(nc.sbuf_tensor("b_xblk", [128, NTILE * D], F32))
                xnT = ph.enter_context(nc.sbuf_tensor("b_xnT", [128, DC * TBLK], F32))
                zb = ph.enter_context(nc.sbuf_tensor("b_zb", [128, DC * TBLK], F32))
                abuf = ph.enter_context(nc.sbuf_tensor("b_abuf", [128, DC * TBLK], F32))
                sgn = ph.enter_context(nc.sbuf_tensor("b_sgn", [128, DC * TBLK], F32))
                ternsb = ph.enter_context(nc.sbuf_tensor("b_tern", [128, DC * 1024], BF16))
                nb = ph.enter_context(nc.sbuf_tensor("b_nb", [128, DC * TBLK], BF16))
                satl = ph.enter_context(nc.sbuf_tensor("b_satl", [128, 1], F32))
                outb = ph.enter_context(nc.sbuf_tensor("b_outb", [128, NTILE * D], F32))
                ccsb = ph.enter_context(nc.sbuf_tensor("b_ccsb", [1, 128], F32))
                sgst = ph.enter_context(nc.sbuf_tensor("b_sgst", [128, DC * TBLK], F32))
                gpbc = ph.enter_context(nc.sbuf_tensor("b_gpbc", [128, 16], F32))
                psA = ph.enter_context(nc.psum_tensor("b_psA", [128, 512], F32))
                psB = ph.enter_context(nc.psum_tensor("b_psB", [128, 512], F32))

                first = [True]
                for s in range(NSEQ):
                    for tb in range(NTB):
                        load_ln_transpose(xpark, s, tb, xblk, xnT, psA, psB, LY,
                                          gamma_col=g0sb, beta_col=be0sb)
                        V.o.drain()
                        V.o.tensor_reduce(stats[:, 48:49], xnT[:, :], axis=AXX,
                                          op=ALU.max, apply_absolute_value=True)
                        V.o.drain()
                        if first[0]:
                            mxcp = V.o.tensor_copy(stats[:, 49:50], stats[:, 48:49])
                            first[0] = False
                        else:
                            mxcp = V.o.tensor_tensor(stats[:, 49:50], stats[:, 49:50],
                                                     stats[:, 48:49], ALU.max)
                        V.mark(mxcp)
                        V.o.drain()
                col_to_scalar(stats[:, 49:50], srow[:1, 12:13], ALU.max, psA)
                V.o.drain()

                V.o.memset(ccsb[:1, :], 0.0)
                V.o.drain()
                c0 = V.o.tensor_copy(ccsb[:1, 0:1], srow[:1, 12:13])
                V.mark(c0)
                G.wait(V)
                SG.put(ccin.ap()[:1, :], ccsb[:1, :])
                wait_dma(G, SG)
                cc = G.o.collective_compute(
                    "AllReduce", ALU.max, replica_groups=[list(range(NCORE))],
                    ins=[ccin.ap().opt()], outs=[ccout.ap().opt()])
                G.mark(cc)
                G.wait(G)
                LG.put(ccsb[:1, :], ccout.ap()[:1, :])
                wait_dma(V, LG)
                V.o.tensor_reduce(srow[:1, 13:14], ccsb[:1, :], axis=AXX, op=ALU.max)
                V.o.drain()
                gm = V.o.tensor_scalar(srow[:1, 14:15], srow[:1, 13:14], 1e-8,
                                       None, ALU.max)
                V.mark(gm)
                bcast_scalar(srow[:1, 14:15], qscal[:, 5:6], psA)
                V.o.reciprocal(qscal[:, 6:7], scv[:, 0:1])
                V.o.drain()
                V.o.tensor_tensor(qscal[:, 0:1], qscal[:, 5:6], qscal[:, 6:7],
                                  ALU.mult)
                V.o.drain()
                V.o.reciprocal(qscal[:, 1:2], qscal[:, 0:1])
                V.o.drain()
                V.o.tensor_tensor(satl[:, 0:1], qscal[:, 0:1], qscal[:, 3:4],
                                  ALU.mult)
                V.o.drain()
                V.o.tensor_scalar(gpbc[:, 0:8], g0sb[:, :], qscal[:, 1:2],
                                  None, ALU.mult)
                gb = V.o.tensor_scalar(gpbc[:, 8:16], be0sb[:, :], qscal[:, 1:2],
                                       None, ALU.mult)
                V.mark(gb)
                V.o.drain()

                mt = LW.put(ternsb[:, :].rearrange("p (c w) -> p c w", w=1024),
                            t0T[:, :].rearrange("(c p) w -> p c w", p=128))

                for s in range(NSEQ):
                    for tb in range(NTB):
                        t0 = s * S + tb * TBLK
                        wait_dma(V, SG)
                        load_ln_transpose(xpark, s, tb, xblk, xnT, psA, psB, LY,
                                          gamma_col=gpbc[:, 0:8],
                                          beta_col=gpbc[:, 8:16])
                        SC.wait(V)
                        ab_ = SC.o.activation(abuf[:, :], xnT[:, :], ACT.Abs)
                        sg_ = SC.o.activation(sgn[:, :], xnT[:, :], ACT.Sign)
                        SC.mark(sg_)
                        V.wait(SC)
                        cz = V.o.tensor_copy(zb[:, :], abuf[:, :])
                        V.mark(cz)
                        for k in (64, 32, 16, 8, 4, 2, 1):
                            SC.wait(V)
                            sk = SC.o.activation(sgst[:, :], zb[:, :],
                                                 ACT.Sign, bias=-63.5)
                            SC.mark(sk)
                            V.wait(SC)
                            st1 = V.o.scalar_tensor_tensor(
                                zb[:, :], sgst[:, :], -k / 2.0,
                                zb[:, :], ALU.mult, ALU.add)
                            V.mark(st1)
                        V.o.drain()
                        V.o.scalar_tensor_tensor(abuf[:, :], abuf[:, :], 63.5,
                                                 zb[:, :], ALU.add, ALU.subtract)
                        V.o.drain()
                        nf = V.o.tensor_tensor(nb[:, :], abuf[:, :], sgn[:, :],
                                               ALU.mult)
                        V.mark(nf)
                        wait_dma(T, LW, mt)
                        for a in range(NTILE):
                            for kb in range(2):
                                pb = psA if kb == 0 else psB
                                T.wait(V)
                                for c in range(DC):
                                    mm_ = T.o.matmul(
                                        pb[:, :512],
                                        nb[:, c * TBLK + a * 128:
                                           c * TBLK + (a + 1) * 128],
                                        ternsb[:, c * 1024 + kb * 512:
                                               c * 1024 + (kb + 1) * 512],
                                        start=(c == 0), stop=(c == DC - 1))
                                T.mark(mm_)
                                V.wait(T)
                                ev = V.o.scalar_tensor_tensor(
                                    outb[:, a * D + kb * 512:a * D + (kb + 1) * 512],
                                    pb[:, :512], satl[:, 0:1],
                                    xblk[:, a * D + kb * 512:a * D + (kb + 1) * 512],
                                    ALU.mult, ALU.add)
                                V.mark(ev)
                        G.wait(V)
                        SG.put(xpark[t0:t0 + TBLK, :]
                               .rearrange("(a p) d -> p a d", p=128),
                               outb[:, :].rearrange("p (a d) -> p a d", d=D))
                barrier()

        if "B" in phases:
            phase_B()

        def phase_C(s):
            pfx = f"c{s}_"
            with contextlib.ExitStack() as ph:
                KT = ph.enter_context(nc.sbuf_tensor(pfx + "KT", [128, DC * S], F32))
                VA = ph.enter_context(nc.sbuf_tensor(pfx + "VA", [128, 16 * 1040], F32))
                QTb = ph.enter_context(nc.sbuf_tensor(pfx + "QTb", [128, DC * 512], F32))
                xblk = ph.enter_context(nc.sbuf_tensor(pfx + "xblk", [128, NTILE * D], F32))
                xnT = ph.enter_context(nc.sbuf_tensor(pfx + "xnT", [128, DC * TBLK], F32))
                attK = xnT
                wst = ph.enter_context(nc.sbuf_tensor(pfx + "wst", [128, 4096], F32))
                EB = ph.enter_context(nc.sbuf_tensor(pfx + "EB", [128, 1024], F32))
                knb = ph.enter_context(nc.sbuf_tensor(pfx + "knb", [128, 256], F32))
                brow1 = ph.enter_context(nc.sbuf_tensor(pfx + "brow1", [1, 1024], F32))
                pbr = brow1
                cb = stats[:, 0:16]
                rzb = stats[:, 52:60]
                attT = QTb

                mb1 = LY.put(brow1[:1, :], b1r[:1, 2048:3072])
                mpb_holder = [0]

                with contextlib.ExitStack() as p1:
                    psA = p1.enter_context(nc.psum_tensor(pfx + "psA", [128, 512], F32))
                    psB = p1.enter_context(nc.psum_tensor(pfx + "psB", [128, 512], F32))
                    for tb in range(NTB):
                        load_ln_transpose(xpark, s, tb, xblk, xnT, psA, psB, LY)
                        kld = {}
                        ktm = [0, 0]
                        kev = [0, 0]
                        Y.wait(T, ktm[0])
                        kld[8] = LW.put(wst[:, 0:DC * 128]
                                        .rearrange("p (c w) -> p c w", w=128),
                                        w1T[:, 8 * 128:9 * 128]
                                        .rearrange("(c p) w -> p c w", p=128))
                        for fc in range(8, 16):
                            freg = fc - 8
                            half = fc % 2
                            pb = psA if half == 0 else psB
                            if fc + 1 < 16:
                                nh = (fc + 1) % 2
                                Y.wait(T, ktm[nh])
                                kld[fc + 1] = LW.put(
                                    wst[:, nh * 1024:nh * 1024 + DC * 128]
                                    .rearrange("p (c w) -> p c w", w=128),
                                    w1T[:, (fc + 1) * 128:(fc + 2) * 128]
                                    .rearrange("(c p) w -> p c w", p=128))
                            wait_dma(T, LW, kld[fc])
                            T.wait(V, kev[half])
                            for c in range(DC):
                                mm_ = T.o.matmul(pb[:, :512],
                                                 wst[:, half * 1024 + c * 128:
                                                     half * 1024 + (c + 1) * 128],
                                                 xnT[:, c * TBLK:(c + 1) * TBLK],
                                                 start=(c == 0), stop=(c == DC - 1))
                            ktm[half] = T.mark(mm_)
                            V.wait(T, ktm[half])
                            ev = V.o.tensor_scalar(
                                KT[:, freg * S + tb * TBLK:freg * S + (tb + 1) * TBLK],
                                pb[:, :512], b1sb[:, fc:fc + 1], None, ALU.add)
                            kev[half] = V.mark(ev)
                        wait_dma(T, LY, mb1)
                        for kb in range(2):
                            Y.wait(T)
                            mw = LW.put(wst[:, :].rearrange("p (c w) -> p c w", w=512),
                                        w1T[:, 2048 + kb * 512:2048 + (kb + 1) * 512]
                                        .rearrange("(c p) w -> p c w", p=128))
                            wait_dma(T, LW, mw)
                            for a in range(NTILE):
                                jt = tb * NTILE + a
                                pb = psA if a % 2 == 0 else psB
                                T.wait(V)
                                for c in range(DC):
                                    T.o.matmul(pb[:, :512],
                                               xnT[:, c * TBLK + a * 128:
                                                   c * TBLK + (a + 1) * 128],
                                               wst[:, c * 512:(c + 1) * 512],
                                               start=(c == 0), stop=False)
                                mm_ = T.o.matmul(pb[:, :512], onerow[:1, :],
                                                 brow1[:1, kb * 512:(kb + 1) * 512],
                                                 start=False, stop=True)
                                T.mark(mm_)
                                V.wait(T)
                                dst = bass.AP(VA, jt * 1040 + kb * 520,
                                              [[16 * 1040, 128], [65, 8], [1, 64]])
                                ev = V.o.tensor_copy(
                                    dst, pb[:, :512].rearrange("p (h w) -> p h w", w=64))
                                V.mark(ev)
                    for jt in range(16):
                        on = V.o.memset(
                            bass.AP(VA, jt * 1040 + 64, [[16 * 1040, 128], [65, 16]]),
                            1.0)
                        V.mark(on)
                    for c in range(DC):
                        for half in range(2):
                            SC.wait(T)
                            SC.wait(V)
                            sq_ = SC.o.activation(
                                dump[:, :1024],
                                KT[:, c * S + half * 1024:c * S + (half + 1) * 1024],
                                ACT.Square)
                            SC.mark(sq_)
                            T.wait(SC)
                            T.wait(V)
                            for hh in range(2):
                                for jj in range(8):
                                    jt = half * 8 + jj
                                    base = hh * 64
                                    mm_ = T.o.matmul(
                                        psA[:, jt:jt + 1] if hh == 0 else psB[:, jt:jt + 1],
                                        dump[base:base + 64, jj * 128:(jj + 1) * 128],
                                        scv[base:base + 64, 3:4],
                                        start=(half == 0 and jj == 0),
                                        stop=(half == 1 and jj == 7))
                                T.mark(mm_)
                        V.wait(T)
                        cv1 = V.o.tensor_copy(knb[:, (2 * c) * 16:(2 * c) * 16 + 16],
                                              psA[:, 0:16])
                        cv2 = V.o.tensor_copy(knb[:, (2 * c + 1) * 16:(2 * c + 1) * 16 + 16],
                                              psB[:, 0:16])
                        V.mark(cv2)
                    for h in range(16):
                        V.o.drain()
                        rk = V.o.tensor_reduce(stats[:, 50:51],
                                               knb[:, h * 16:(h + 1) * 16],
                                               axis=AXX, op=ALU.max)
                        V.mark(rk)
                        col_to_scalar(stats[:, 50:51], srow[:1, 16 + h:17 + h],
                                      ALU.max, psA)
                barrier()

                with contextlib.ExitStack() as p2:
                    psE1 = p2.enter_context(nc.psum_tensor(pfx + "psE1", [128, 512], F32))
                    psE2 = p2.enter_context(nc.psum_tensor(pfx + "psE2", [128, 512], F32))
                    pacA = p2.enter_context(nc.psum_tensor(pfx + "pacA", [128, 260], F32))
                    pacB = p2.enter_context(nc.psum_tensor(pfx + "pacB", [128, 260], F32))
                    psW = p2.enter_context(nc.psum_tensor(pfx + "psW", [128, 512], F32))
                    psX = p2.enter_context(nc.psum_tensor(pfx + "psX", [128, 512], F32))
                    for ib in range(4):
                        t0 = s * S + ib * TBLK
                        wait_dma(V, SG)
                        load_ln_transpose(xpark, s, ib, xblk, xnT, psW, psX, LY)
                        qld = {}
                        qtm = [0, 0]
                        qev = [0, 0]
                        Y.wait(T, qtm[0])
                        qld[0] = LW.put(wst[:, 0:DC * 128]
                                        .rearrange("p (c w) -> p c w", w=128),
                                        w1T[:, 0:128]
                                        .rearrange("(c p) w -> p c w", p=128))
                        for fc in range(8):
                            half = fc % 2
                            pb = psW if half == 0 else psX
                            if fc + 1 < 8:
                                nh = (fc + 1) % 2
                                Y.wait(T, qtm[nh])
                                qld[fc + 1] = LW.put(
                                    wst[:, nh * 1024:nh * 1024 + DC * 128]
                                    .rearrange("p (c w) -> p c w", w=128),
                                    w1T[:, (fc + 1) * 128:(fc + 2) * 128]
                                    .rearrange("(c p) w -> p c w", p=128))
                            wait_dma(T, LW, qld[fc])
                            T.wait(V, qev[half])
                            for c in range(DC):
                                mm_ = T.o.matmul(pb[:, :512],
                                                 wst[:, half * 1024 + c * 128:
                                                     half * 1024 + (c + 1) * 128],
                                                 xnT[:, c * TBLK:(c + 1) * TBLK],
                                                 start=(c == 0), stop=(c == DC - 1))
                            qtm[half] = T.mark(mm_)
                            V.wait(T, qtm[half])
                            ev = V.o.tensor_scalar(
                                QTb[:, fc * 512:(fc + 1) * 512],
                                pb[:, :512], b1sb[:, fc:fc + 1], None, ALU.add)
                            qev[half] = V.mark(ev)
                        if C_STOP[0] == 1:
                            G.wait(V)
                            SG.put(epark[0:128, :], QTb[:, 0:2048])
                            SG.put(epark[128:256, :], xnT[:, 0:2048])
                            wait_dma(G, SG)
                            return
                        for c in range(DC):
                            SC.wait(T)
                            SC.wait(V)
                            sq_ = SC.o.activation(dump[:, :512],
                                                  QTb[:, c * 512:(c + 1) * 512],
                                                  ACT.Square)
                            SC.mark(sq_)
                            T.wait(SC)
                            for hh in range(2):
                                base = hh * 64
                                for u in range(4):
                                    mm_ = T.o.matmul(
                                        psW[:, (2 * c + hh) * 4 + u:
                                            (2 * c + hh) * 4 + u + 1],
                                        dump[base:base + 64, u * 128:(u + 1) * 128],
                                        scv[base:base + 64, 3:4],
                                        start=(c == 0 and hh == 0 and u == 0),
                                        stop=(c == DC - 1 and hh == 1 and u == 3))
                            T.mark(mm_)
                        for h in range(16):
                            V.wait(T)
                            rq = V.o.tensor_reduce(stats[:, 51:52],
                                                   psW[:, 4 * h:4 * h + 4],
                                                   axis=AXX, op=ALU.max)
                            V.mark(rq)
                            col_to_scalar(stats[:, 51:52], srow[:1, h:h + 1],
                                          ALU.max, psX)
                        V.o.drain()
                        tt_ = V.o.tensor_tensor(srow[:1, 40:56], srow[:1, 0:16],
                                                srow[:1, 16:32], ALU.mult)
                        V.mark(tt_)
                        SC.wait(V)
                        sq_ = SC.o.activation(srow[:1, 40:56], srow[:1, 40:56],
                                              ACT.Sqrt)
                        SC.mark(sq_)
                        V.wait(SC)
                        ng = V.o.tensor_scalar(srow[:1, 40:56], srow[:1, 40:56],
                                               -0.125, None, ALU.mult)
                        V.mark(ng)
                        V.o.drain()
                        bcast_scalar(srow[:1, 40:56], cb, psW)

                        if C_STOP[0] == 2:
                            G.wait(V)
                            SG.put(epark[0:1, 0:64], srow[:1, 0:64])
                            SG.put(epark[2:130, 0:16], cb)
                            SG.put(epark[256:384, :], KT[:, 0:2048])
                            wait_dma(G, SG)
                            return
                        sc_mark = [0, 0]
                        t_attn_mark = [0, 0]
                        exp_marks = {}
                        for hp in range(8):
                            hA, hB = 2 * hp, 2 * hp + 1
                            for step in range(17):
                                if step < 16:
                                    jt = step
                                    par = jt % 2
                                    sbA, sbB = (psE1, psE2) if par == 0 else (psW, psX)
                                    T.wait(SC, sc_mark[par])
                                    mmA = T.o.matmul(
                                        sbA[:, :512],
                                        KT[0:64, hp * S + jt * 128:hp * S + (jt + 1) * 128],
                                        QTb[0:64, hp * 512:(hp + 1) * 512],
                                        start=True, stop=True)
                                    mmB = T.o.matmul(
                                        sbB[:, :512],
                                        KT[64:128, hp * S + jt * 128:hp * S + (jt + 1) * 128],
                                        QTb[64:128, hp * 512:(hp + 1) * 512],
                                        start=True, stop=True)
                                    smark = T.mark(mmB)
                                    SC.wait(T, smark)
                                    SC.wait(T, t_attn_mark[par])
                                    EBp = EB if par == 0 else dump
                                    exA = SC.o.activation(
                                        EBp[:, 0:512],
                                        sbA[:, :], ACT.Exp, scale=0.125,
                                        bias=cb[:, hA:hA + 1])
                                    exB = SC.o.activation(
                                        EBp[:, 512:1024],
                                        sbB[:, :], ACT.Exp, scale=0.125,
                                        bias=cb[:, hB:hB + 1])
                                    sc_mark[par] = SC.mark(exB)
                                    exp_marks[jt] = sc_mark[par]
                                if step >= 1:
                                    jt = step - 1
                                    par = jt % 2
                                    T.wait(SC, exp_marks[jt])
                                    if jt == 0:
                                        T.wait(V)
                                    EBp = EB if par == 0 else dump
                                    for u in range(4):
                                        T.o.matmul(pacA[:, 65 * u:65 * u + 65],
                                                   EBp[:, u * 128:(u + 1) * 128],
                                                   bass.AP(VA, jt * 1040 + hA * 65,
                                                           [[16 * 1040, 128], [1, 65]]),
                                                   start=(jt == 0 and u == 0),
                                                   stop=(jt == 15 and u == 3))
                                        mm_ = T.o.matmul(
                                            pacB[:, 65 * u:65 * u + 65],
                                            EBp[:, 512 + u * 128:512 + (u + 1) * 128],
                                            bass.AP(VA, jt * 1040 + hB * 65,
                                                    [[16 * 1040, 128], [1, 65]]),
                                            start=(jt == 0 and u == 0),
                                            stop=(jt == 15 and u == 3))
                                    t_attn_mark[par] = T.mark(mm_)
                            V.wait(T)
                            r1 = V.o.reciprocal(rzb[:, 0:4],
                                                bass.AP(pacA, 64, [[260, 128], [65, 4]]))
                            r2 = V.o.reciprocal(rzb[:, 4:8],
                                                bass.AP(pacB, 64, [[260, 128], [65, 4]]))
                            V.mark(r2)
                            V.o.drain()
                            for u in range(4):
                                V.o.tensor_scalar(
                                    attK[:, u * 1024 + hA * 64:u * 1024 + hA * 64 + 64],
                                    pacA[:, 65 * u:65 * u + 64],
                                    rzb[:, u:u + 1], None, ALU.mult)
                                nv = V.o.tensor_scalar(
                                    attK[:, u * 1024 + hB * 64:u * 1024 + hB * 64 + 64],
                                    pacB[:, 65 * u:65 * u + 64],
                                    rzb[:, 4 + u:5 + u], None, ALU.mult)
                            V.mark(nv)
                            if C_STOP[0] == 3 and hp == 0:
                                pz = V.o.tensor_copy(dump[:, 0:260], pacA[:, :])
                                V.mark(pz)
                                G.wait(V)
                                SG.put(epark[0:128, 0:1024], EB[:, :])
                                SG.put(epark[128:256, 0:260], dump[:, 0:260])
                                SG.put(epark[256:384, 0:8], rzb)
                                SG.put(epark[384:512, 0:2048], attK[:, 0:2048])
                                SG.put(epark[512:513, 0:64], srow[:1, 0:64])
                                wait_dma(G, SG)
                                return

                        if ib == 0:
                            Y.wait(T)
                            mpb = LY.put(pbr[:1, :], pb1r[:1, :])
                            mpb_holder[0] = mpb
                        for u in range(NTILE):
                            for c in range(DC):
                                pb = psW if c % 2 == 0 else psX
                                T.wait(V)
                                tr = T.o.transpose(pb[:, :128],
                                                   attK[:, u * 1024 + c * 128:
                                                        u * 1024 + (c + 1) * 128],
                                                   ident[:, :])
                                T.mark(tr)
                                V.wait(T)
                                cp = V.o.tensor_copy(
                                    attT[:, c * 512 + u * 128:c * 512 + (u + 1) * 128],
                                    pb[:, :128])
                                V.mark(cp)
                        wait_dma(T, LY, mpb_holder[0])
                        for ob in range(2):
                            Y.wait(T)
                            mw = LW.put(wst[:, :].rearrange("p (c w) -> p c w", w=512),
                                        wp1T[:, ob * 512:(ob + 1) * 512]
                                        .rearrange("(c p) w -> p c w", p=128))
                            wait_dma(T, LW, mw)
                            for u in range(NTILE):
                                pb = psW if u % 2 == 0 else psX
                                T.wait(V)
                                for c in range(DC):
                                    T.o.matmul(pb[:, :512],
                                               attT[:, c * 512 + u * 128:
                                                    c * 512 + (u + 1) * 128],
                                               wst[:, c * 512:c * 512 + 512],
                                               start=(c == 0), stop=False)
                                mm_ = T.o.matmul(pb[:, :512], onerow[:1, :],
                                                 pbr[:1, ob * 512:(ob + 1) * 512],
                                                 start=False, stop=True)
                                T.mark(mm_)
                                V.wait(T)
                                ev = V.o.tensor_tensor(
                                    attK[:, u * 1024 + ob * 512:u * 1024 + (ob + 1) * 512],
                                    pb[:, :512],
                                    xblk[:, u * D + ob * 512:u * D + (ob + 1) * 512],
                                    ALU.add)
                                V.mark(ev)
                        G.wait(V)
                        SG.put(xpark[t0:t0 + TBLK, :]
                               .rearrange("(a p) d -> p a d", p=128),
                               attK[:, :].rearrange("p (a d) -> p a d", d=D))
                    barrier()

        if "C" in phases:
            phase_C(0)
            phase_C(1)

        def phase_D():
            with contextlib.ExitStack() as ph:
                xblk = ph.enter_context(nc.sbuf_tensor("d_xblk", [128, NTILE * D], F32))
                xnT = ph.enter_context(nc.sbuf_tensor("d_xnT", [128, DC * TBLK], F32))
                gbuf = ph.enter_context(nc.sbuf_tensor("d_gbuf", [128, 32 * 512], F32))
                wstD = ph.enter_context(nc.sbuf_tensor("d_wst", [128, 4096], F16))
                xh = ph.enter_context(nc.sbuf_tensor("d_xh", [128, DC * TBLK], F16))
                xl = ph.enter_context(nc.sbuf_tensor("d_xl", [128, DC * TBLK], F16))
                gh = ph.enter_context(nc.sbuf_tensor("d_gh", [128, 32 * 512], F16))
                gl = ph.enter_context(nc.sbuf_tensor("d_gl", [128, 32 * 512], F16))
                lnwcs = ph.enter_context(nc.sbuf_tensor("d_lnwc", [128, 64], F32))
                rhosb = ph.enter_context(nc.sbuf_tensor("d_rho", [128, 512], F32))
                mursb = ph.enter_context(nc.sbuf_tensor("d_mur", [128, 512], F32))
                rowsA = ph.enter_context(nc.sbuf_tensor("d_rowsA", [1, 512], F32))
                rowsB = ph.enter_context(nc.sbuf_tensor("d_rowsB", [1, 512], F32))
                trsb = ph.enter_context(nc.sbuf_tensor("d_trsb", [8, 128], F32))
                mlc = LY.put(lnwcs[:, :], lnwc[:, :])
                for s in range(NSEQ):
                    for tb in range(NTB):
                        t0 = s * S + tb * TBLK
                        with contextlib.ExitStack() as pp:
                            psA = pp.enter_context(
                                nc.psum_tensor(f"d_psA{s}{tb}", [128, 512], F32))
                            psB = pp.enter_context(
                                nc.psum_tensor(f"d_psB{s}{tb}", [128, 512], F32))
                            wait_dma(SC, SG)
                            wait_dma(V, SG)
                            load_ln_transpose(xpark, s, tb, xblk, xnT, psA, psB, LY)
                            V.o.tensor_copy(xh[:, :], xnT[:, :])
                            V.o.drain()
                            sp = V.o.tensor_tensor(xl[:, :], xnT[:, :], xh[:, :],
                                                   ALU.subtract)
                            spl_mark = [V.mark(sp)]
                            gel_mark = {}
                            for og in range(8):
                                Y.wait(T)
                                mw = LW.put(wstD[:, :]
                                            .rearrange("p (c w) -> p c w", w=512),
                                            tfcT[:, og * 512:(og + 1) * 512]
                                            .rearrange("(c p) w -> p c w", p=128))
                                wait_dma(T, LW, mw)
                                for oo in range(4):
                                    oc = og * 4 + oo
                                    pb = psA if oc % 2 == 0 else psB
                                    T.wait(SC, gel_mark.get(oc % 2, 0))
                                    T.wait(V, spl_mark[0])
                                    for c in range(DC):
                                        T.o.matmul(
                                            pb[:, :512],
                                            wstD[:, c * 512 + oo * 128:
                                                 c * 512 + oo * 128 + 128],
                                            xh[:, c * TBLK:(c + 1) * TBLK],
                                            start=(c == 0), stop=False)
                                        mm_ = T.o.matmul(
                                            pb[:, :512],
                                            wstD[:, c * 512 + oo * 128:
                                                 c * 512 + oo * 128 + 128],
                                            xl[:, c * TBLK:(c + 1) * TBLK],
                                            start=False, stop=(c == DC - 1))
                                    fm = T.mark(mm_)
                                    SC.wait(T, fm)
                                    ge = SC.o.activation(
                                        gbuf[:, oc * 512:(oc + 1) * 512],
                                        pb[:, :512], ACT.Gelu)
                                    gel_mark[oc % 2] = SC.mark(ge)
                            sqm = {}
                            stm = {}
                            for oc in range(32):
                                SC.wait(T, stm.get(oc % 2, 0))
                                half = (oc % 2) * 512
                                sq_ = SC.o.activation(dump[:, half:half + 512],
                                                      gbuf[:, oc * 512:(oc + 1) * 512],
                                                      ACT.Square)
                                sqm[oc % 2] = SC.mark(sq_)
                                T.wait(SC, sqm[oc % 2])
                                for u in range(4):
                                    T.o.matmul(psA[:, 32 + u:33 + u],
                                               gbuf[:, oc * 512 + u * 128:
                                                    oc * 512 + (u + 1) * 128],
                                               scv[:, 3:4],
                                               start=(oc == 0 and u == 0),
                                               stop=(oc == 31 and u == 3))
                                    mm_ = T.o.matmul(psB[:, 32 + u:33 + u],
                                                     dump[:, half + u * 128:
                                                          half + (u + 1) * 128],
                                                     scv[:, 3:4],
                                                     start=(oc == 0 and u == 0),
                                                     stop=(oc == 31 and u == 3))
                                stm[oc % 2] = T.mark(mm_)
                            V.wait(T)
                            V.o.tensor_scalar(stats[:, 32:36], psA[:, 32:36],
                                              1.0 / DFF, None, ALU.mult)
                            V.o.tensor_scalar(stats[:, 36:40], psB[:, 32:36],
                                              1.0 / DFF, None, ALU.mult)
                            V.o.drain()
                            V.o.tensor_tensor(stats[:, 40:44], stats[:, 32:36],
                                              stats[:, 32:36], ALU.mult)
                            V.o.drain()
                            vv = V.o.tensor_tensor(stats[:, 36:40], stats[:, 36:40],
                                                   stats[:, 40:44], ALU.subtract)
                            V.mark(vv)
                            SC.wait(V)
                            sq_ = SC.o.activation(stats[:, 40:44], stats[:, 36:40],
                                                  ACT.Sqrt, bias=EPS_LN)
                            SC.mark(sq_)
                            V.wait(SC)
                            V.o.reciprocal(stats[:, 40:44], stats[:, 40:44])
                            V.o.drain()
                            V.o.tensor_tensor(stats[:, 44:48], stats[:, 32:36],
                                              stats[:, 40:44], ALU.mult)
                            V.o.drain()
                            mr = V.o.tensor_scalar(stats[:, 44:48], stats[:, 44:48],
                                                   -1.0, None, ALU.mult)
                            V.mark(mr)
                            T.wait(V)
                            tr = T.o.transpose(psA[0:8, 0:128], stats[:, 40:48],
                                               ident[:, :])
                            T.mark(tr)
                            V.wait(T)
                            cpr = V.o.tensor_copy(trsb[0:8, :], psA[0:8, 0:128])
                            V.mark(cpr)
                            G.wait(V)
                            mra = SG.put(rowsA[:1, :], trsb[0:4, :])
                            mrb = SG.put(rowsB[0:1, :], trsb[4:8, :])
                            wait_dma(T, SG, mrb)
                            wait_dma(V, LY, mlc)
                            T.wait(V)
                            bA = T.o.matmul(psA[:, :512], onerow[:1, :],
                                            rowsA[:1, :], start=True, stop=True)
                            bB = T.o.matmul(psB[:, :512], onerow[:1, :],
                                            rowsB[:1, :], start=True, stop=True)
                            bm = T.mark(bB)
                            V.wait(T, bm)
                            V.o.tensor_copy(rhosb[:, :], psA[:, :512])
                            bc = V.o.tensor_copy(mursb[:, :], psB[:, :512])
                            V.mark(bc)
                        with contextlib.ExitStack() as pp2:
                            pps = [pp2.enter_context(
                                nc.psum_tensor(f"d_pp{s}{tb}_{i}", [128, 512], F32))
                                for i in range(8)]
                            for og in range(8):
                                Y.wait(T)
                                mw = LW.put(wstD[:, :]
                                            .rearrange("p (c w) -> p c w", w=1024),
                                            tp2T[og * 512:(og + 1) * 512, :]
                                            .rearrange("(c p) w -> p c w", p=128))
                                wait_dma(T, LW, mw)
                                for oo in range(4):
                                    oc = og * 4 + oo
                                    gsl = gbuf[:, oc * 512:(oc + 1) * 512]
                                    V.o.tensor_tensor(gsl, gsl, rhosb[:, :],
                                                      ALU.mult)
                                    V.o.drain()
                                    V.o.tensor_tensor(gsl, gsl, mursb[:, :],
                                                      ALU.add)
                                    V.o.drain()
                                    V.o.tensor_scalar(gsl, gsl,
                                                      lnwcs[:, oc:oc + 1],
                                                      lnwcs[:, 32 + oc:33 + oc],
                                                      ALU.mult, ALU.add)
                                    V.o.drain()
                                    V.o.tensor_copy(
                                        gh[:, oc * 512:(oc + 1) * 512], gsl)
                                    V.o.drain()
                                    nm_ = V.o.tensor_tensor(
                                        gl[:, oc * 512:(oc + 1) * 512], gsl,
                                        gh[:, oc * 512:(oc + 1) * 512],
                                        ALU.subtract)
                                    nmk = V.mark(nm_)
                                    T.wait(V, nmk)
                                    for ob in range(2):
                                        for u in range(4):
                                            T.o.matmul(
                                                pps[ob * 4 + u][:, :512],
                                                gh[:, oc * 512 + u * 128:
                                                   oc * 512 + (u + 1) * 128],
                                                wstD[:, oo * 1024 + ob * 512:
                                                     oo * 1024 + (ob + 1) * 512],
                                                start=(oc == 0), stop=False)
                                            mm_ = T.o.matmul(
                                                pps[ob * 4 + u][:, :512],
                                                gl[:, oc * 512 + u * 128:
                                                   oc * 512 + (u + 1) * 128],
                                                wstD[:, oo * 1024 + ob * 512:
                                                     oo * 1024 + (ob + 1) * 512],
                                                start=False, stop=(oc == 31))
                                T.mark(mm_)
                            V.wait(T)
                            for ob in range(2):
                                for u in range(4):
                                    ev = V.o.tensor_tensor(
                                        gbuf[:, u * 1024 + ob * 512:
                                             u * 1024 + (ob + 1) * 512],
                                        pps[ob * 4 + u][:, :512],
                                        xblk[:, u * D + ob * 512:u * D + (ob + 1) * 512],
                                        ALU.add)
                            V.mark(ev)
                            G.wait(V)
                            SG.put(yout[t0:t0 + TBLK, :]
                                   .rearrange("(a p) d -> p a d", p=128),
                                   gbuf[:, :NTILE * D]
                                   .rearrange("p (a d) -> p a d", d=D))
                barrier()

        if "D" in phases:
            phase_D()

        if "D" not in phases:
          with nc.sbuf_tensor("st_buf", [128, NTILE * D], F32) as sb:
            for s in range(NSEQ):
                for tb in range(NTB):
                    t0 = s * S + tb * TBLK
                    wait_dma(Y, SG)
                    m = LY.put(sb[:, :].rearrange("p (a d) -> p a d", d=D),
                               xpark[t0:t0 + TBLK, :]
                               .rearrange("(a p) d -> p a d", p=128))
                    wait_dma(G, LY, m)
                    SG.put(yout[t0:t0 + TBLK, :]
                           .rearrange("(a p) d -> p a d", p=128),
                           sb[:, :].rearrange("p (a d) -> p a d", d=D))
          barrier()

        wait_dma(G, SG)
        wait_dma(Y, LY)

    return nc


_NC_CACHE = {}


def _get_program():
    if "nc" not in _NC_CACHE:
        _NC_CACHE["nc"] = build_program()
    return _NC_CACHE["nc"]


def make_in_maps(inputs_by_name):
    x = np.ascontiguousarray(np.asarray(inputs_by_name["L0_attn_x"],
                                        dtype=np.float32))
    shared = {
        "w0": np.ascontiguousarray(inputs_by_name["L0_attn_W"]),
        "b0c": np.ascontiguousarray(
            np.asarray(inputs_by_name["L0_attn_b"]).reshape(24, 128).T),
        "b0r": np.asarray(inputs_by_name["L0_attn_b"]).reshape(1, 3 * D),
        "fw0": np.ascontiguousarray(inputs_by_name["L0_ffn_weight"]),
        "g0c": np.ascontiguousarray(
            np.asarray(inputs_by_name["L0_ffn_gamma"]).reshape(DC, 128).T),
        "be0c": np.ascontiguousarray(
            np.asarray(inputs_by_name["L0_ffn_beta"]).reshape(DC, 128).T),
        "w1": np.ascontiguousarray(inputs_by_name["L1_attn_qkv_w"]),
        "b1c": np.ascontiguousarray(
            np.asarray(inputs_by_name["L1_attn_qkv_b"]).reshape(24, 128).T),
        "b1r": np.asarray(inputs_by_name["L1_attn_qkv_b"]).reshape(1, 3 * D),
        "pw1": np.ascontiguousarray(inputs_by_name["L1_attn_proj_w"]),
        "pb1r": np.asarray(inputs_by_name["L1_attn_proj_b"]).reshape(1, D),
        "fcw1": np.ascontiguousarray(inputs_by_name["L1_ffn_W_fc"]),
        "lnwb2": np.ascontiguousarray(
            np.stack([np.asarray(inputs_by_name["L1_ffn_ln_w"]),
                      np.asarray(inputs_by_name["L1_ffn_ln_b"])])),
        "lnwc": np.ascontiguousarray(np.concatenate(
            [np.asarray(inputs_by_name["L1_ffn_ln_w"]).reshape(32, 128).T,
             np.asarray(inputs_by_name["L1_ffn_ln_b"]).reshape(32, 128).T],
            axis=1)),
        "pw2": np.ascontiguousarray(inputs_by_name["L1_ffn_W_proj"]),
        "cident": np.eye(128, dtype=np.float32),
        "conerow": np.ones((1, 128), dtype=np.float32),
        "cvals": np.broadcast_to(np.array(
            [127.0, EPS_LN, -63.5, 1.0, 0.0, 0.0, 0.0, 0.0],
            dtype=np.float32), (128, 8)).copy(),
    }
    in_maps = []
    for c in range(NCORE):
        im = dict(shared)
        im["xin"] = np.ascontiguousarray(x[2 * c:2 * c + 2].reshape(NSEQ * S, D))
        in_maps.append(im)
    return in_maps


def kernel(L0_attn_x, L0_attn_W, L0_attn_b, L0_ffn_weight, L0_ffn_gamma,
           L0_ffn_beta, L1_attn_qkv_w, L1_attn_qkv_b, L1_attn_proj_w,
           L1_attn_proj_b, L1_ffn_W_fc, L1_ffn_ln_w, L1_ffn_ln_b,
           L1_ffn_W_proj):
    kw = dict(L0_attn_x=L0_attn_x, L0_attn_W=L0_attn_W, L0_attn_b=L0_attn_b,
              L0_ffn_weight=L0_ffn_weight, L0_ffn_gamma=L0_ffn_gamma,
              L0_ffn_beta=L0_ffn_beta, L1_attn_qkv_w=L1_attn_qkv_w,
              L1_attn_qkv_b=L1_attn_qkv_b, L1_attn_proj_w=L1_attn_proj_w,
              L1_attn_proj_b=L1_attn_proj_b, L1_ffn_W_fc=L1_ffn_W_fc,
              L1_ffn_ln_w=L1_ffn_ln_w, L1_ffn_ln_b=L1_ffn_ln_b,
              L1_ffn_W_proj=L1_ffn_W_proj)
    nc = _get_program()
    in_maps = make_in_maps(kw)
    res = run_bass_kernel_spmd(nc, in_maps, core_ids=list(range(NCORE)))
    out = np.concatenate(
        [np.asarray(r["y"]).reshape(NSEQ, S, D) for r in res.results], axis=0)
    return out.astype(np.float32)
